# version 13
# baseline (speedup 1.0000x reference)
import numpy as np
import ml_dtypes
import concourse.bass as bass
import concourse.mybir as mybir
from concourse.bass_utils import run_bass_kernel_spmd
from contextlib import ExitStack

F32 = mybir.dt.float32
BF16 = mybir.dt.bfloat16
I32 = mybir.dt.int32
AF = mybir.ActivationFunctionType
ALU = mybir.AluOpType
AX = mybir.AxisListType

ENGS = ("sy", "te", "ve", "sc", "gp")
SHIFT = 8.0


class Cfg:
    def __init__(self, B=2, S=8192, H=2048, NH=16, HD=128, E=8, I=1408, SI=2816):
        assert SI == 2 * I
        self.B, self.S, self.H, self.NH, self.HD = B, S, H, NH, HD
        self.E, self.I, self.SI = E, I, SI
        self.HT = H // 128
        self.IT = I // 128
        self.HPC = NH * B // 8
        self.TOWN = S // 4
        self.TH = self.TOWN // 2
        self.SEG = S // 4
        self.W = min(1024, self.SEG)
        self.NW = self.SEG // self.W
        self.QCH = min(1024, S)
        self.NCH = S // self.QCH
        self.KSH = self.QCH // 128
        self.QKO = 2 * self.HPC
        self.VD = self.HPC * 128
        self.NHH = max(1, H // 512)
        self.DHW = H // self.NHH
        self.NTT = self.TH // 128
        assert self.QCH % 128 == 0 and self.W % 128 == 0 and self.TH % 128 == 0
        assert self.DHW % 512 == 0


class Plan:
    def __init__(self, nc):
        self.nc = nc
        self.plan = {e: [] for e in ENGS}
        self.cnt = {}
        self.sems = {}

    def sem(self, name):
        self.sems[name] = self.nc.alloc_semaphore(name)
        self.cnt[name] = 0

    def v(self, name):
        return self.cnt[name]

    def w(self, name, val=None):
        return (self.sems[name], self.cnt[name] if val is None else val)

    def emit(self, eng, fn, waits=(), inc=None, inck=1):
        if inc is not None:
            self.cnt[inc] += inck
        ih = None if inc is None else (self.sems[inc], inck)
        ws = tuple((a, b) for (a, b) in waits)

        def run(h, fn=fn, ws=ws, ih=ih):
            for sm, vv in ws:
                h.wait_ge(sm, vv)
            ins = fn(h)
            if ih is not None:
                ins.then_inc(ih[0], ih[1])
        self.plan[eng].append(run)
        return None if inc is None else self.cnt[inc]

    def mm(self, o, l, r, start, stop, waits=(), inc=None):
        return self.emit("te", lambda h, o=o, l=l, r=r, a=start, b=stop:
                         h.matmul(o, l, r, start=a, stop=b), waits, inc)

    def tr(self, o, i, ident, waits=(), inc=None):
        return self.emit("te", lambda h, o=o, i=i, d=ident: h.transpose(o, i, d),
                         waits, inc)

    def act(self, o, i, f, bias=0.0, scale=1.0, waits=(), inc=None):
        sw = [self.w("cS")]
        return self.emit("sc", lambda h, o=o, i=i, f=f, b=bias, s=scale:
                         h.activation(o, i, f, bias=b, scale=s),
                         sw + list(waits), inc)

    def ve_(self, fname, *args, waits=(), inc=None):
        sw = [self.w("cV")]
        return self.emit("ve", lambda h, fn=fname, aa=args: getattr(h, fn)(*aa),
                         sw + list(waits), inc)

    def gp_(self, fname, *args, waits=(), inc=None):
        sw = [self.w("cG")]
        return self.emit("gp", lambda h, fn=fname, aa=args: getattr(h, fn)(*aa),
                         sw + list(waits), inc)

    def dma(self, eng, o, i, sem, waits=()):
        return self.emit(eng, lambda h, o=o, i=i: h.dma_start(out=o, in_=i),
                         waits, sem, 16)


def build_program(cfg, upto=5):
    c = cfg
    nc = bass.Bass("TRN2", target_bir_lowering=False)
    S, H, HT, HPC, W, TH = c.S, c.H, c.HT, c.HPC, c.W, c.TH
    NE = c.E + 2

    din = {}

    def inp(name, shape, dt=F32):
        din[name] = nc.dram_tensor(name, list(shape), dt, kind="ExternalInput")
        return din[name]

    xb = inp("xb", [S, H])
    xo = inp("xo", [c.TOWN, H])
    wqkvT = inp("wqkvT", [H, 3 * HPC * 128])
    woT = inp("woT", [H, H])
    rwT = inp("rwT", [H, c.E])
    gwt = inp("gwt", [NE * c.IT, 128, HT, 128])
    uwt = inp("uwt", [NE * c.IT, 128, HT, 128])
    dwt = inp("dwt", [NE * c.IT * 128, H])
    sgb = inp("sgb", [128, 2 * c.IT])
    sub = inp("sub", [128, 2 * c.IT])
    sdbp = inp("sdbp", [128, H])
    cpar = inp("cpar", [1, 48], I32)

    out_t = nc.dram_tensor("out", [c.TOWN, H], F32, kind="ExternalOutput")

    qT_scr = nc.dram_tensor("qT_scr", [HPC * 128, S], BF16, kind="Internal")
    kT_scr = nc.dram_tensor("kT_scr", [HPC * 128, S], BF16, kind="Internal")
    v_scr = nc.dram_tensor("v_scr", [S, c.VD], BF16, kind="Internal")
    contrib = nc.dram_tensor("contrib", [HPC * 128, S], BF16, kind="Internal")
    gath = nc.dram_tensor("gath", [8 * HPC * 128, S], BF16, kind="Internal")
    tok_in = nc.dram_tensor("tok_in", [1, 4], F32, kind="Internal")
    tok_out = nc.dram_tensor("tok_out", [1, 4], F32, kind="Internal")
    y_scr = nc.dram_tensor("y_scr", [c.TOWN, H], F32, kind="Internal")
    xnT_scr = nc.dram_tensor("xnT_scr", [HT * 128, S], BF16, kind="Internal")

    ident_bf_d = nc.inline_tensor(np.eye(128, dtype=ml_dtypes.bfloat16), "ident_bf")
    ident_f_d = nc.inline_tensor(np.eye(128, dtype=np.float32), "ident_f")
    tri_np = np.triu(np.ones((128, 128), dtype=np.float32)).astype(ml_dtypes.bfloat16)
    tri_bf_d = nc.inline_tensor(tri_np, "tri_bf")
    selm = np.zeros((128, c.KSH * 128), dtype=ml_dtypes.bfloat16)
    for b in range(c.KSH):
        selm[b, b * 128:(b + 1) * 128] = 1
    sel_bf_d = nc.inline_tensor(selm, "sel_bf")
    row0m = np.zeros((128, 128), dtype=ml_dtypes.bfloat16)
    row0m[0, :] = 1
    row0_bf_d = nc.inline_tensor(row0m, "row0_bf")

    P = Plan(nc)
    for name in ["dx0", "dx1", "dqk0", "dqk1", "dv0", "dv1", "dwq0", "dwq1", "dmisc",
                 "dq0", "dq1", "dk0", "dk1", "dk2", "dk3", "dvv0", "dvv1", "dvv2", "dvv3", "dct0", "dct1",
                 "dtok", "dg0", "dg1", "du0", "du1", "dd0", "dd1", "dyi",
                 "dys", "dsb", "dgr", "dwo0", "dwo1", "dyl0", "dyl1", "dout0", "dout1",
                 "dxn", "dxl",
                 "cT", "cV", "cS", "cG", "cc"]:
        P.sem(name)

    stk = ExitStack()

    def SBp(name, shape, dt):
        return stk.enter_context(nc.sbuf_tensor(name, shape, dt))

    rw_bf = SBp("rw_bf", [128, HT * c.E], BF16)
    RDP = SBp("RDP", [128, 128], F32)
    CPAR = SBp("CPAR", [1, 48], I32)
    LNW = SBp("LNW", [128, 80], F32)
    LNW2 = SBp("LNW2", [128, 64], F32)
    TOKS = SBp("TOKS", [1, 4], F32)
    IDB = SBp("IDB", [128, 128], BF16)
    IDF = SBp("IDF", [128, 128], F32)
    TRI = SBp("TRI", [128, 128], BF16)
    SEL = SBp("SEL", [128, c.KSH * 128], BF16)
    ROW0 = SBp("ROW0", [128, 128], BF16)
    ident_bf, ident_f = IDB[:, :], IDF[:, :]
    tri_bf, sel_bf, row0_bf = TRI[:, :], SEL[:, :], ROW0[:, :]

    CONSTS = SBp("CONSTS", [128, 4], F32)

    P.dma("sy", CPAR[:, :], cpar.ap(), "dmisc")
    for dst, srcd in ((IDB, ident_bf_d), (IDF, ident_f_d), (TRI, tri_bf_d),
                      (SEL, sel_bf_d), (ROW0, row0_bf_d)):
        P.dma("sy", dst[:, :], srcd.ap(), "dmisc")
    const_ready = P.v("dmisc")
    P.ve_("memset", RDP[:, :], 0.0, inc="cV")
    P.ve_("memset", CONSTS[:, 0:1], 1e-5, inc="cV")
    P.ve_("memset", CONSTS[:, 1:2], -SHIFT, inc="cV")
    cmem = P.v("cV")
    EPS_AP = CONSTS[:, 0:1]
    NSHIFT_AP = CONSTS[:, 1:2]
    for eng in ENGS:
        P.emit(eng, lambda h: h.nop(),
               waits=[P.w("dmisc", const_ready), P.w("cV", cmem)])

    def emit_ln_tile(xsrc_ap, xstg_t, xnstg_t, xw_sem, par, dma_waits, act_waits=()):
        L = LNW[:, 40 * par:40 * par + 40]
        P.dma("sy", xstg_t[:, :], xsrc_ap, xw_sem, waits=dma_waits)
        dv = P.v(xw_sem)
        nbk = (H + 511) // 512
        for b in range(nbk):
            cw = min(512, H - b * 512)
            P.ve_("bn_stats", L[:, 6 * b:6 * b + 6], xstg_t[:, b * 512:b * 512 + cw],
                  waits=([P.w(xw_sem, dv)] if b == 0 else ()), inc="cV")
        P.ve_("bn_aggr", L[:, 6 * nbk:6 * nbk + 2], L[:, 0:6 * nbk], inc="cV")
        va = P.v("cV")
        mcol = 6 * nbk
        P.act(L[:, mcol + 2:mcol + 3], L[:, mcol + 1:mcol + 2], AF.Sqrt,
              bias=EPS_AP, scale=1.0, waits=[P.w("cV", va)], inc="cS")
        sq = P.v("cS")
        P.ve_("reciprocal", L[:, mcol + 3:mcol + 4], L[:, mcol + 2:mcol + 3],
              waits=[P.w("cS", sq)], inc="cV")
        P.ve_("tensor_scalar", xnstg_t[:, :], xstg_t[:, :], L[:, mcol:mcol + 1],
              L[:, mcol + 3:mcol + 4], ALU.subtract, ALU.mult,
              waits=list(act_waits), inc="cV")
        return ("cV", P.v("cV"))

    midstk = ExitStack()
    xnTo = midstk.enter_context(nc.sbuf_tensor("xnTo", [128, HT * c.TH], BF16))

    with ExitStack() as ph1:
        def S1(name, shape, dt):
            return ph1.enter_context(nc.sbuf_tensor(name, shape, dt))
        xstg = [S1(f"xstg{i}", [128, H], F32) for i in range(2)]
        xnstg = [S1(f"xnstg{i}", [128, H], BF16) for i in range(2)]
        xnT = S1("xnT", [128, HT * W], BF16)
        qkstg = [S1(f"qkstg{i}", [128, W], BF16) for i in range(2)]
        vstg = [S1(f"vstg{i}", [128, c.VD], BF16) for i in range(2)]
        OW = 3 * HPC * 128
        wqstg = [S1(f"wqstg{i}", [128, OW], F32) for i in range(2)]
        wq_bf = S1("wq_bf", [128, HT * OW], BF16)

        pqk = [ph1.enter_context(nc.psum_tensor(f"pqk{i}", [128, W], F32)) for i in range(2)]
        pv = [ph1.enter_context(nc.psum_tensor(f"pv{i}", [128, c.VD], F32)) for i in range(2)]
        ptr = [ph1.enter_context(nc.psum_tensor(f"ptr{i}", [128, 8 * 128], BF16)) for i in range(2)]

        wq_cast = []
        for ht in range(HT):
            par = ht % 2
            wv = [P.w("cV", wq_cast[ht - 2])] if ht >= 2 else []
            P.dma("sc", wqstg[par][:, :], wqkvT.ap()[ht * 128:(ht + 1) * 128, :],
                  f"dwq{par}", waits=wv)
            P.ve_("tensor_copy", wq_bf[:, ht * OW:(ht + 1) * OW], wqstg[par][:, :],
                  waits=[P.w(f"dwq{par}")], inc="cV")
            wq_cast.append(P.v("cV"))
        P.dma("sc", wqstg[0][:, 0:HT * c.E].rearrange("p (t e) -> p t e", t=HT),
              rwT.ap().rearrange("(t p) e -> p t e", p=128), "dwq0",
              waits=[P.w("cV", wq_cast[-2])])
        P.ve_("tensor_copy", rw_bf[:, :], wqstg[0][:, 0:HT * c.E],
              waits=[P.w("dwq0")], inc="cV")
        wq_ready = P.v("cV")

        ln_hist = {}
        tpose_hist = {}
        evac_hist = {}
        trn = [0]
        qk_evac, qk_st = {}, {}
        v_evac, v_st = {}, {}
        nttw = W // 128
        vn = [0]
        xnst = {}

        for g in range(4):
            for wi in range(c.NW):
                wfirst = trn[0]
                for i in range(nttw):
                    ti = (g * c.NW + wi) * nttw + i
                    par = ti % 2
                    row0 = g * c.SEG + wi * W + i * 128
                    wv = [P.w(*ln_hist[ti - 2])] if ti >= 2 else []
                    aw = [P.w("cT", tpose_hist[ti - 2])] if ti >= 2 else []
                    ln_ms = emit_ln_tile(xb.ap()[row0:row0 + 128, :],
                                         xstg[par], xnstg[par], f"dx{par}", par,
                                         wv, aw)
                    ln_hist[ti] = ln_ms
                    ngrp = (HT + 7) // 8
                    for grp in range(ngrp):
                        nj = min(8, HT - grp * 8)
                        u = trn[0]
                        tp = u % 2
                        wv2 = [P.w(*ln_ms)]
                        if u >= 2:
                            wv2.append(P.w("cV", evac_hist[u - 2]))
                        for j in range(nj):
                            jj = grp * 8 + j
                            P.tr(ptr[tp][:, j * 128:(j + 1) * 128],
                                 xnstg[par][:, jj * 128:(jj + 1) * 128], ident_bf,
                                 waits=(wv2 if j == 0 else ()),
                                 inc=("cT" if j == nj - 1 else None))
                        tpose_hist[ti] = P.v("cT")
                        oap = xnT[:, :].rearrange("p (t x) -> p t x", t=HT)[
                            :, grp * 8:grp * 8 + nj, i * 128:i * 128 + 128]
                        iap = ptr[tp][:, 0:nj * 128].rearrange("p (t x) -> p t x", t=nj)
                        ew = [P.w("cT", P.v("cT"))]
                        if u == wfirst:
                            wiq = g * c.NW + wi
                            if wiq >= 1:
                                ew.append(P.w("dxn", xnst[wiq - 1]))
                        P.ve_("tensor_copy", oap, iap, waits=ew, inc="cV")
                        evac_hist[u] = P.v("cV")
                        trn[0] += 1
                xn_ready = P.v("cV")
                col = g * c.SEG + wi * W
                for ht in range(HT):
                    P.dma("sy", xnT_scr.ap()[ht * 128:(ht + 1) * 128, col:col + W],
                          xnT[:, ht * W:(ht + 1) * W], "dxn",
                          waits=([P.w("cV", xn_ready)] if ht == 0 else ()))
                xnst[g * c.NW + wi] = P.v("dxn")

                nmm = (W + 511) // 512
                for ot in range(c.QKO):
                    pb = ot % 2
                    gk = (g * c.NW + wi) * c.QKO + ot
                    wv = [P.w("cV", xn_ready), P.w("cV", wq_ready)]
                    if gk >= 2:
                        wv.append(P.w("cV", qk_evac[gk - 2]))
                    first = True
                    for ht in range(HT):
                        for m in range(nmm):
                            cw = min(512, W - m * 512)
                            P.mm(pqk[pb][:, m * 512:m * 512 + cw],
                                 wq_bf[:, ht * OW + ot * 128:ht * OW + (ot + 1) * 128],
                                 xnT[:, ht * W + m * 512:ht * W + m * 512 + cw],
                                 start=(ht == 0), stop=(ht == HT - 1),
                                 waits=(wv if first else ()),
                                 inc=("cT" if (ht == HT - 1 and m == nmm - 1) else None))
                            first = False
                    tm = P.v("cT")
                    stw = [P.w("cT", tm)]
                    if gk >= 2:
                        stw.append(P.w(f"dqk{pb}", qk_st[gk - 2]))
                    P.ve_("tensor_copy", qkstg[pb][:, :], pqk[pb][:, :],
                          waits=stw, inc="cV")
                    qk_evac[gk] = P.v("cV")
                    scr = qT_scr if ot < HPC else kT_scr
                    rb = (ot % HPC) * 128
                    col = g * c.SEG + wi * W
                    P.dma("sy", scr.ap()[rb:rb + 128, col:col + W], qkstg[pb][:, :],
                          f"dqk{pb}", waits=[P.w("cV", qk_evac[gk])])
                    qk_st[gk] = P.v(f"dqk{pb}")

                for i in range(nttw):
                    vi = vn[0]
                    par = vi % 2
                    row0 = g * c.SEG + wi * W + i * 128
                    wv = [P.w("cV", xn_ready)]
                    if vi >= 2:
                        wv.append(P.w("cV", v_evac[vi - 2]))
                    first = True
                    for ht in range(HT):
                        P.mm(pv[par][:, :],
                             xnT[:, ht * W + i * 128:ht * W + (i + 1) * 128],
                             wq_bf[:, ht * OW + 2 * HPC * 128:(ht + 1) * OW],
                             start=(ht == 0), stop=(ht == HT - 1),
                             waits=(wv if first else ()),
                             inc=("cT" if ht == HT - 1 else None))
                        first = False
                    tm = P.v("cT")
                    stw = [P.w("cT", tm)]
                    if vi >= 2:
                        stw.append(P.w(f"dv{par}", v_st[vi - 2]))
                    P.ve_("tensor_copy", vstg[par][:, :], pv[par][:, :], waits=stw,
                          inc="cV")
                    v_evac[vi] = P.v("cV")
                    P.dma("sy", v_scr.ap()[row0:row0 + 128, :], vstg[par][:, :],
                          f"dv{par}", waits=[P.w("cV", v_evac[vi])])
                    v_st[vi] = P.v(f"dv{par}")
                    vn[0] += 1

        ph1_end = dict(cT=P.v("cT"), cV=P.v("cV"), cS=P.v("cS"),
                       q0=P.v("dqk0"), q1=P.v("dqk1"),
                       v0=P.v("dv0"), v1=P.v("dv1"))

    if upto < 2:
        _finish(P, nc, stk)
        return nc

    scl = 1.0 / float(np.sqrt(c.HD))
    KSH = c.KSH
    with ExitStack() as ph2:
        def S2(name, shape, dt):
            return ph2.enter_context(nc.sbuf_tensor(name, shape, dt))
        QT = [S2(f"QT{i}", [128, c.QCH], BF16) for i in range(2)]
        KT = [S2(f"KT{i}", [128, c.QCH], BF16) for i in range(2)]
        VT = [S2(f"VT{i}", [128, c.QCH], BF16) for i in range(2)]
        PRBN = 7
        PRB = [S2(f"PRB{i}", [128, c.QCH], BF16) for i in range(PRBN)]
        ACL = S2("ACL", [128, c.QCH], BF16)
        ACC = S2("ACC", [128, c.QCH], F32)
        RDB = S2("RDB", [128, c.QCH], BF16)
        AST = [S2(f"AST{i}", [128, c.QCH], BF16) for i in range(2)]
        RDT = S2("RDT", [128, 128], BF16)
        DN = S2("DN", [128, KSH], F32)

        pav = ph2.enter_context(nc.psum_tensor("pav", [128, c.QCH], F32))
        psc = [ph2.enter_context(nc.psum_tensor(f"psc{i}", [128, c.QCH], F32)) for i in range(3)]
        pdn = psc[2]

        entry2 = [P.w("cV", ph1_end["cV"]), P.w("cT", ph1_end["cT"]),
                  P.w("dqk0", ph1_end["q0"]), P.w("dqk1", ph1_end["q1"]),
                  P.w("dv0", ph1_end["v0"]), P.w("dv1", ph1_end["v1"])]

        kld, vld, kuse, vuse = {}, {}, {}, {}
        qld, quse = {}, {}
        ast_st = {}
        prev_norm = [None]
        prev_trm = [None]
        kst = [0]
        qn = [0]

        def load_ksup(hl, st_idx):
            n = kst[0]
            par = n % 2
            wv = list(entry2) if n == 0 else []
            if n >= 2:
                wv.append(P.w("cT", kuse[n - 2]))
            P.dma("gp", KT[par][:, :],
                  kT_scr.ap()[hl * 128:(hl + 1) * 128,
                              st_idx * c.QCH:(st_idx + 1) * c.QCH],
                  f"dk{par}", waits=wv)
            kld[n] = P.v(f"dk{par}")
            wv2 = list(entry2) if n == 0 else []
            if n >= 2:
                wv2.append(P.w("cT", vuse[n - 2]))
            vsl = v_scr.ap()[st_idx * c.QCH:(st_idx + 1) * c.QCH,
                             hl * 128:(hl + 1) * 128]
            P.dma("sy", VT[par][:, :].rearrange("p (s d) -> p s d", s=KSH),
                  vsl.rearrange("(s p) d -> p s d", p=128), f"dvv{par}", waits=wv2)
            vld[n] = P.v(f"dvv{par}")
            kst[0] += 1
            return n

        for hl in range(HPC):
            for ch in range(c.NCH):
                cn = qn[0]
                qpar = cn % 2
                wv = list(entry2) if cn == 0 else []
                if cn >= 2:
                    wv.append(P.w("cT", quse[cn - 2]))
                P.dma("gp", QT[qpar][:, :],
                      qT_scr.ap()[hl * 128:(hl + 1) * 128,
                                  ch * c.QCH:(ch + 1) * c.QCH],
                      f"dq{qpar}", waits=wv)
                qld[cn] = P.v(f"dq{qpar}")
                nsup = ch + 1
                nkt = nsup * KSH
                ksup_ids = [load_ksup(hl, 0)]
                exp_ms, av_ms, acc_ms = {}, {}, {}
                prb_rdy = {}

                def meta(kt):
                    st_i, j = divmod(kt, KSH)
                    diag = (st_i == nsup - 1)
                    return st_i, j, diag, (j * 128 if diag else 0)

                def emit_scores(kt):
                    st_i, j, diag, qlo = meta(kt)
                    ku = ksup_ids[st_i]
                    kpar = ku % 2
                    spar = kt % 3
                    ppar = kt % PRBN
                    wv = [P.w(f"dk{kpar}", kld[ku]), P.w(f"dq{qpar}", qld[cn])]
                    if kt >= 3:
                        wv.append(P.w("cS", exp_ms[kt - 3]))
                    else:
                        wv.append(P.w("cS"))
                        wv.append(P.w("cV"))
                    first = True
                    off = qlo
                    while off < c.QCH:
                        cw = min(512 - off % 512, c.QCH - off)
                        P.mm(psc[spar][:, off:off + cw],
                             KT[kpar][:, j * 128:(j + 1) * 128],
                             QT[qpar][:, off:off + cw], True, True,
                             waits=(wv if first else ()),
                             inc=("cT" if off + cw >= c.QCH else None))
                        first = False
                        off += cw
                    sc_ms = P.v("cT")
                    wv = [P.w("cT", sc_ms)]
                    if kt >= PRBN:
                        wv += [P.w("cT", av_ms[kt - PRBN]),
                               P.w("cV", acc_ms[kt - PRBN])]
                    P.act(PRB[ppar][:, qlo:c.QCH], psc[spar][:, qlo:c.QCH], AF.Exp,
                          bias=NSHIFT_AP, scale=scl, waits=wv, inc="cS")
                    exp_ms[kt] = P.v("cS")
                    rdy = ("cS", exp_ms[kt])
                    if diag:
                        P.ve_("tensor_mul", PRB[ppar][:, qlo:qlo + 128],
                              PRB[ppar][:, qlo:qlo + 128], tri_bf,
                              waits=[P.w("cS", exp_ms[kt])], inc="cV")
                        rdy = ("cV", P.v("cV"))
                    prb_rdy[kt] = rdy
                    wacc = [P.w(rdy[0], rdy[1])]
                    if kt == 0 and prev_trm[0] is not None:
                        wacc.append(P.w("cT", prev_trm[0]))
                    if j == 0:
                        P.ve_("tensor_copy", ACL[:, :], PRB[ppar][:, :],
                              waits=wacc, inc="cV")
                    else:
                        P.ve_("tensor_add", ACL[:, qlo:c.QCH], ACL[:, qlo:c.QCH],
                              PRB[ppar][:, qlo:c.QCH], waits=wacc, inc="cV")
                    if j == KSH - 1:
                        if st_i == 0:
                            P.ve_("tensor_copy", ACC[:, :], ACL[:, :], inc="cV")
                        else:
                            P.ve_("tensor_add", ACC[:, :], ACC[:, :], ACL[:, :],
                                  inc="cV")
                    acc_ms[kt] = P.v("cV")

                def emit_av(kt):
                    st_i, j, diag, qlo = meta(kt)
                    if j == 0 and st_i + 1 < nsup:
                        ksup_ids.append(load_ksup(hl, st_i + 1))
                    ku = ksup_ids[st_i]
                    kpar = ku % 2
                    ppar = kt % PRBN
                    rdy = prb_rdy[kt]
                    wv = [P.w(rdy[0], rdy[1]), P.w(f"dvv{kpar}", vld[ku])]
                    if kt == 0 and prev_norm[0] is not None:
                        wv.append(P.w("cV", prev_norm[0]))
                    first = True
                    off = qlo
                    while off < c.QCH:
                        cw = min(512 - off % 512, c.QCH - off)
                        bank = off // 512
                        jmax = min(KSH - 1, 4 * bank + 3)
                        stp = diag and (j == jmax)
                        lastmm = (off + cw >= c.QCH)
                        P.mm(pav[:, off:off + cw],
                             VT[kpar][:, j * 128:(j + 1) * 128],
                             PRB[ppar][:, off:off + cw],
                             (kt == 0), stp,
                             waits=(wv if first else ()),
                             inc=("cT" if lastmm else None))
                        first = False
                        off += cw
                    av_ms[kt] = P.v("cT")
                    if j == KSH - 1:
                        kuse[ku] = P.v("cT")
                        vuse[ku] = P.v("cT")

                BAT = 3
                done_av = 0
                for g0 in range(0, nkt, BAT):
                    for kt in range(g0, min(g0 + BAT, nkt)):
                        emit_scores(kt)
                    if g0 > 0:
                        for kt in range(g0 - BAT, g0):
                            emit_av(kt)
                            done_av = kt + 1
                for kt in range(done_av, nkt):
                    emit_av(kt)
                quse[cn] = P.v("cT")
                accv = P.v("cV")
                for qb in range(KSH):
                    P.tr(pdn[:, qb * 128:(qb + 1) * 128],
                         ACC[:, qb * 128:(qb + 1) * 128], ident_f,
                         waits=([P.w("cV", accv)] if qb == 0 else ()),
                         inc=("cT" if qb == KSH - 1 else None))
                trm = P.v("cT")
                prev_trm[0] = trm
                P.ve_("tensor_reduce",
                      DN[:, 0:KSH].rearrange("p (a b) -> p a b", a=KSH),
                      pdn[:, :].rearrange("p (a b) -> p a b", a=KSH),
                      AX.X, ALU.add, waits=[P.w("cT", trm)], inc="cV")
                P.ve_("reciprocal", RDP[:, 0:KSH], DN[:, 0:KSH], inc="cV")
                rdv = P.v("cV")
                P.tr(pdn[:, 0:128], RDP[:, :], ident_f,
                     waits=[P.w("cV", rdv)], inc="cT")
                t2 = P.v("cT")
                P.ve_("tensor_copy", RDT[:, :], pdn[:, 0:128],
                      waits=[P.w("cT", t2)], inc="cV")
                rdtv = P.v("cV")
                for qb in range(KSH):
                    P.mm(pdn[:, qb * 128:(qb + 1) * 128],
                         sel_bf[:, qb * 128:(qb + 1) * 128], RDT[:, :],
                         True, True,
                         waits=([P.w("cV", rdtv)] if qb == 0 else ()),
                         inc=("cT" if qb == KSH - 1 else None))
                bctm = P.v("cT")
                apar = cn % 2
                wv = [P.w("cT", bctm)]
                if cn >= 2:
                    wv.append(P.w(f"dct{apar}", ast_st[cn - 2]))
                P.ve_("tensor_copy", RDB[:, :], pdn[:, :], waits=wv, inc="cV")
                P.ve_("tensor_mul", AST[apar][:, :], pav[:, :], RDB[:, :], inc="cV")
                prev_norm[0] = P.v("cV")
                P.dma("gp", contrib.ap()[hl * 128:(hl + 1) * 128,
                                         ch * c.QCH:(ch + 1) * c.QCH],
                      AST[apar][:, :], f"dct{apar}",
                      waits=[P.w("cV", prev_norm[0])])
                ast_st[cn] = P.v(f"dct{apar}")
                qn[0] += 1
        ph2_end = dict(cT=P.v("cT"), cV=P.v("cV"), cS=P.v("cS"),
                       ct0=P.v("dct0"), ct1=P.v("dct1"), xn=P.v("dxn"))

    rg_xn = nc.gpsimd.alloc_register("rg_xn")
    for jg in range(4):
        P.emit("gp", lambda h, jg=jg: h.reg_load(rg_xn, CPAR[0:1, 16 + 4 * jg:17 + 4 * jg]),
               waits=([P.w("dxn", ph2_end["xn"]), P.w("dmisc", const_ready)]
                      if jg == 0 else ()))
        P.dma("gp", xnTo[:, :].rearrange("p (t x) -> p t x", t=HT)[:, 4 * jg:4 * jg + 4, :],
              bass.AP(xnT_scr, rg_xn, [[S, 128], [128 * S, 4], [1, c.TH]]),
              "dxl")
    xnl_h0 = P.v("dxl")

    if upto >= 3:
        P.gp_("memset", TOKS[:, :], 1.0, inc="cG")
        P.dma("gp", tok_in.ap(), TOKS[:, :], "dtok",
              waits=[P.w("cG"), P.w("dct0", ph2_end["ct0"]),
                     P.w("dct1", ph2_end["ct1"])])
        P.emit("gp", lambda h: h.collective_compute(
            "AllGather", ALU.bypass, replica_groups=[list(range(8))],
            ins=[contrib.ap()], outs=[gath.ap()]),
            waits=[P.w("dtok")], inc="cc")
        P.emit("gp", lambda h: h.collective_compute(
            "AllReduce", ALU.add, replica_groups=[list(range(8))],
            ins=[tok_in.ap()], outs=[tok_out.ap()]),
            waits=[P.w("cc")], inc="cc")

    if upto < 4:
        _finish(P, nc, stk)
        return nc

    NTT, NHH, DHW = c.NTT, c.NHH, c.DHW
    with ExitStack() as ph4:
        def S4(name, shape, dt):
            return ph4.enter_context(nc.sbuf_tensor(name, shape, dt))
        yacc = S4("yacc", [128, NTT * H], F32)
        mT = S4("mT", [128, c.IT * TH], BF16)
        dbfh = S4("dbfh", [128, 2 * c.IT * DHW], BF16)
        LNU = S4("LNU", [128, 4 * H + 2 * DHW], F32)
        gstg = [LNU[:, 0:H], LNU[:, H:2 * H]]
        ustg = [LNU[:, 2 * H:3 * H], LNU[:, 3 * H:4 * H]]
        dstg = [LNU[:, 4 * H:4 * H + DHW], LNU[:, 4 * H + DHW:4 * H + 2 * DHW]]
        gbf = [S4(f"gbf{i}", [128, H], BF16) for i in range(2)]
        ubf = [S4(f"ubf{i}", [128, H], BF16) for i in range(2)]
        MGS = S4("MGS", [128, TH], BF16)
        COMB = S4("COMB", [128, NTT * c.E], F32)
        SGB = S4("SGB", [128, 2 * c.IT], F32)
        SUB = S4("SUB", [128, 2 * c.IT], F32)
        SDB = S4("SDB", [128, H], BF16)

        pg = ph4.enter_context(nc.psum_tensor("pg", [128, TH], F32))
        pu = ph4.enter_context(nc.psum_tensor("pu", [128, TH], F32))
        pd = [ph4.enter_context(nc.psum_tensor(f"pd{i}", [128, DHW], F32)) for i in range(2)]
        pr = ph4.enter_context(nc.psum_tensor("pr", [128, c.E], F32))

        entry4 = [P.w("cV", ph2_end["cV"]), P.w("cT", ph2_end["cT"]),
                  P.w("cS", ph2_end["cS"]),
                  P.w("dct0", ph2_end["ct0"]), P.w("dct1", ph2_end["ct1"]),
                  P.w("dxn", ph2_end["xn"])]

        P.dma("sy", SGB[:, :], sgb.ap(), "dsb", waits=entry4)
        P.dma("sy", SUB[:, :], sub.ap(), "dsb")
        sgb_v = P.v("dsb")

        gcast, ucast, dcast_all = {}, {}, {}
        guse, uuse = {}, {}
        gun = [0]
        g4n = [0]
        g4evac = {}
        ddn = [0]
        y_ms = {}
        y_prev = {}
        silu_last = [None]
        mupd_last = [None]
        dbfh_freeR = [None, None]
        dgc = [0]
        stt_ms = {}
        down_done = [None]
        ys_half0 = None
        prev_half_done = [None]
        sdb_ready = [None]

        P.dma("sy", LNU[:, 0:H], sdbp.ap(), "dsb", waits=entry4)
        sdl = P.v("dsb")
        P.ve_("tensor_copy", SDB[:, :], LNU[:, 0:H],
              waits=[P.w("dsb", sdl)], inc="cV")
        sdb_ready = [P.v("cV")]

        for hf in range(2):
            if hf == 0:
                xnl = xnl_h0
            else:
                for jg in range(4):
                    P.emit("gp", lambda h, jg=jg: h.reg_load(
                        rg_xn, CPAR[0:1, 32 + 4 * jg:33 + 4 * jg]),
                        waits=([P.w("cT"), P.w("cV")] if jg == 0 else ()))
                    P.dma("gp", xnTo[:, :].rearrange("p (t x) -> p t x", t=HT)[:, 4 * jg:4 * jg + 4, :],
                          bass.AP(xnT_scr, rg_xn, [[S, 128], [128 * S, 4], [1, c.TH]]),
                          "dxl")
                xnl = P.v("dxl")

            yi_w = [P.w("cV", P.v("cV"))]
            if hf == 1:
                yi_w.append(P.w("dys", ys_half0))
            for tt in range(NTT):
                P.dma("sy", yacc[:, tt * H:(tt + 1) * H],
                      xo.ap()[(hf * NTT + tt) * 128:(hf * NTT + tt + 1) * 128, :],
                      "dyi", waits=(yi_w if tt == 0 else ()))
            yinit = P.v("dyi")

            for tt in range(NTT):
                first = True
                wv = [P.w("dxl", xnl), P.w("cV", P.v("cV"))]
                for ht in range(HT):
                    P.mm(pr[:, 0:c.E],
                         xnTo[:, ht * TH + tt * 128:ht * TH + (tt + 1) * 128],
                         rw_bf[:, ht * c.E:(ht + 1) * c.E],
                         start=(ht == 0), stop=(ht == HT - 1),
                         waits=(wv if first else ()),
                         inc=("cT" if ht == HT - 1 else None))
                    first = False
                tm = P.v("cT")
                E = c.E
                Z = LNW2[:, 0:E]
                P.ve_("reduce_max", LNW2[:, 32:33], pr[:, 0:E], AX.X,
                      None, None, True, waits=[P.w("cT", tm)], inc="cV")
                P.ve_("tensor_scalar_add", Z, pr[:, 0:E], LNW2[:, 32:33], inc="cV")
                P.ve_("tensor_single_scalar", LNW2[:, 8:8 + E], Z, 0.0,
                      ALU.is_equal, inc="cV")
                P.ve_("scalar_tensor_tensor", LNW2[:, 16:16 + E],
                      LNW2[:, 8:8 + E], -1e30, Z, ALU.mult, ALU.add, inc="cV")
                P.ve_("reduce_max", LNW2[:, 33:34], LNW2[:, 16:16 + E], AX.X,
                      None, None, True, inc="cV")
                P.ve_("tensor_scalar_mul", LNW2[:, 34:35], LNW2[:, 33:34],
                      -1.0, inc="cV")
                vv = P.v("cV")
                P.act(LNW2[:, 35:36], LNW2[:, 34:35], AF.Exp,
                      waits=[P.w("cV", vv)], inc="cS")
                P.act(LNW2[:, 24:24 + E], Z, AF.Exp, inc="cS")
                sv = P.v("cS")
                P.ve_("tensor_scalar_add", LNW2[:, 36:37], LNW2[:, 35:36], 1.0,
                      waits=[P.w("cS", sv)], inc="cV")
                P.ve_("reciprocal", LNW2[:, 37:38], LNW2[:, 36:37], inc="cV")
                P.ve_("tensor_single_scalar", LNW2[:, 40:40 + E], Z,
                      LNW2[:, 34:35], ALU.is_ge, inc="cV")
                P.ve_("tensor_mul", LNW2[:, 48:48 + E], LNW2[:, 24:24 + E],
                      LNW2[:, 40:40 + E], inc="cV")
                P.ve_("tensor_scalar_mul", COMB[:, tt * E:(tt + 1) * E],
                      LNW2[:, 48:48 + E], LNW2[:, 37:38], inc="cV")
            comb_ready = P.v("cV")

            for e2 in range(NE):
                shared = (e2 >= c.E)
                silu_ms = {}
                mupd_ms = {}
                def emit_down_stream(hh, it, extra=()):
                    dn_ = ddn[0]
                    dpar = dn_ % 2
                    wv = list(extra)
                    if dn_ >= 2:
                        wv.append(P.w("cG", dcast_all[dn_ - 2]))
                    else:
                        wv.append(P.w("cG"))
                    reg = hh % 2
                    r0 = (e2 * c.IT + it) * 128
                    P.dma("sc", dstg[dpar][:, :],
                          dwt.ap()[r0:r0 + 128, hh * DHW:(hh + 1) * DHW],
                          f"dd{dpar}", waits=wv)
                    dv = P.v(f"dd{dpar}")
                    wv2 = [P.w(f"dd{dpar}", dv)]
                    if dbfh_freeR[reg] is not None and it == 0:
                        wv2.append(P.w("cT", dbfh_freeR[reg]))
                    P.gp_("tensor_copy",
                          dbfh[:, (reg * c.IT + it) * DHW:(reg * c.IT + it + 1) * DHW],
                          dstg[dpar][:, :], waits=wv2, inc="cG")
                    dcast_all[dn_] = P.v("cG")
                    ddn[0] += 1
                    return P.v("cG")

                dcast_hh = {}
                for it in range(c.IT):
                    n = gun[0]
                    par = n % 2
                    tid = e2 * c.IT + it
                    for which, stg, bfb, castd, used, sq, dmaq, castq in (
                            ("g", gstg, gbf, gcast, guse, "dg", "sy", "sc"),
                            ("u", ustg, ubf, ucast, uuse, "du", "sy", "gp")):
                        eap = (gwt if which == "g" else uwt).ap()[tid]
                        csem = "cS" if castq == "sc" else "cG"
                        wv = []
                        if n == 0 and hf == 0:
                            wv = list(entry4)
                        if e2 == 0 and it == 0:
                            wv.append(P.w("cV"))
                        if n >= 2:
                            wv.append(P.w(csem, castd[n - 2]))
                        P.dma(dmaq,
                              stg[par][:, :].rearrange("p (s i) -> p s i", i=128),
                              eap, f"{sq}{par}", waits=wv)
                        dv = P.v(f"{sq}{par}")
                        wv2 = [P.w(f"{sq}{par}", dv)]
                        if n >= 2:
                            wv2.append(P.w("cT", used[n - 2]))
                        if castq == "sc":
                            P.act(bfb[par][:, :], stg[par][:, :], AF.Copy,
                                  waits=wv2, inc="cS")
                        else:
                            P.gp_("tensor_copy", bfb[par][:, :], stg[par][:, :],
                                  waits=wv2, inc="cG")
                        castd[n] = P.v(csem)
                    nmm = (TH + 511) // 512
                    first = True
                    wv = [P.w("cS", gcast[n]), P.w("dxl", xnl)]
                    if it >= 1:
                        wv.append(P.w("cS", silu_ms[it - 1]))
                    elif silu_last[0] is not None:
                        wv.append(P.w("cS", silu_last[0]))
                    for ht in range(HT):
                        for m in range(nmm):
                            cw = min(512, TH - m * 512)
                            P.mm(pg[:, m * 512:m * 512 + cw],
                                 gbf[par][:, ht * 128:(ht + 1) * 128],
                                 xnTo[:, ht * TH + m * 512:ht * TH + m * 512 + cw],
                                 start=(ht == 0), stop=(ht == HT - 1),
                                 waits=(wv if first else ()),
                                 inc=("cT" if (ht == HT - 1 and m == nmm - 1) else None))
                            first = False
                    gmm = P.v("cT")
                    guse[n] = gmm
                    first = True
                    wv = [P.w("cG", ucast[n])]
                    if it >= 1:
                        wv.append(P.w("cV", mupd_ms[it - 1]))
                    elif mupd_last[0] is not None:
                        wv.append(P.w("cV", mupd_last[0]))
                    for ht in range(HT):
                        for m in range(nmm):
                            cw = min(512, TH - m * 512)
                            P.mm(pu[:, m * 512:m * 512 + cw],
                                 ubf[par][:, ht * 128:(ht + 1) * 128],
                                 xnTo[:, ht * TH + m * 512:ht * TH + m * 512 + cw],
                                 start=(ht == 0), stop=(ht == HT - 1),
                                 waits=(wv if first else ()),
                                 inc=("cT" if (ht == HT - 1 and m == nmm - 1) else None))
                            first = False
                    umm = P.v("cT")
                    uuse[n] = umm
                    gb = SGB[:, (e2 - c.E) * c.IT + it:(e2 - c.E) * c.IT + it + 1] if shared else 0.0
                    wv = [P.w("cT", gmm)]
                    if it >= 1:
                        wv.append(P.w("cV", mupd_ms[it - 1]))
                    elif mupd_last[0] is not None:
                        wv.append(P.w("cV", mupd_last[0]))
                    if shared:
                        wv.append(P.w("dsb"))
                    P.act(MGS[:, :], pg[:, :], AF.Silu, bias=gb, waits=wv, inc="cS")
                    silu_ms[it] = P.v("cS")
                    ub = SUB[:, (e2 - c.E) * c.IT + it:(e2 - c.E) * c.IT + it + 1] if shared else 0.0
                    P.ve_("scalar_tensor_tensor", mT[:, it * TH:(it + 1) * TH],
                          pu[:, :], ub, MGS[:, :], ALU.add, ALU.mult,
                          waits=[P.w("cT", umm), P.w("cS", silu_ms[it]),
                                 P.w("cT", P.v("cT"))], inc="cV")
                    mupd_ms[it] = P.v("cV")
                    ex = []
                    if e2 == 0 and it == 0:
                        ex.append(P.w("cV"))
                        if hf == 0:
                            ex += list(entry4)
                    dcast_hh[(0, it)] = emit_down_stream(0, it, extra=ex)
                    gun[0] += 1
                silu_last[0] = silu_ms[c.IT - 1]
                mupd_last[0] = mupd_ms[c.IT - 1]
                m_ready = P.v("cV")

                for hh in range(NHH):
                    if hh + 1 < NHH:
                        for it in range(c.IT):
                            dcast_hh[(hh + 1, it)] = emit_down_stream(hh + 1, it)
                    last_e = (e2 == NE - 1)
                    for tt in range(NTT):
                        pk = tt * NHH + hh
                        gidx = dgc[0]
                        ppar = gidx % 2
                        wv = [P.w("cV", m_ready),
                              P.w("cG", dcast_hh[(hh, c.IT - 1)]),
                              P.w("dyi", yinit)]
                        if stt_ms.get(gidx - 2) is not None:
                            wv.append(P.w("cV", stt_ms[gidx - 2]))
                        first = True
                        nm = DHW // 512
                        reg = hh % 2
                        for it in range(c.IT):
                            for m in range(nm):
                                lastmm = (not last_e) and it == c.IT - 1 and m == nm - 1
                                P.mm(pd[ppar][:, m * 512:(m + 1) * 512],
                                     mT[:, it * TH + tt * 128:it * TH + (tt + 1) * 128],
                                     dbfh[:, (reg * c.IT + it) * DHW + m * 512:
                                          (reg * c.IT + it) * DHW + (m + 1) * 512],
                                     start=(it == 0),
                                     stop=(it == c.IT - 1 and not last_e),
                                     waits=(wv if first else ()),
                                     inc=("cT" if lastmm else None))
                                first = False
                        if last_e:
                            for m in range(nm):
                                P.mm(pd[ppar][:, m * 512:(m + 1) * 512], row0_bf,
                                     SDB[:, hh * DHW + m * 512:hh * DHW + (m + 1) * 512],
                                     False, True,
                                     waits=([P.w("cV", sdb_ready[0])] if (tt == 0 and hh == 0 and hf == 0) else ()),
                                     inc=("cT" if m == nm - 1 else None))
                        dmm = P.v("cT")
                        combs = 1.0 if shared else COMB[:, tt * c.E + e2:tt * c.E + e2 + 1]
                        ysl = yacc[:, tt * H + hh * DHW:tt * H + (hh + 1) * DHW]
                        P.ve_("scalar_tensor_tensor", ysl, pd[ppar][:, :], combs,
                              ysl, ALU.mult, ALU.add,
                              waits=[P.w("cT", dmm)], inc="cV")
                        y_ms[(hf, e2, pk)] = P.v("cV")
                        stt_ms[gidx] = P.v("cV")
                        dgc[0] += 1
                    dbfh_freeR[hh % 2] = P.v("cT")
                down_done[0] = P.v("cT")
            y_prev = {pk: y_ms.get((hf, NE - 1, pk)) for pk in range(NTT * NHH)}
            for tt in range(NTT):
                wv = [P.w("cV", y_ms[(hf, NE - 1, tt * NHH + hhx)])
                      for hhx in range(NHH)]
                P.dma("sy", y_scr.ap()[(hf * NTT + tt) * 128:(hf * NTT + tt + 1) * 128, :],
                      yacc[:, tt * H:(tt + 1) * H], "dys", waits=wv)
            if hf == 0:
                ys_half0 = P.v("dys")
            prev_half_done[0] = P.v("cV")
        ph4_end = dict(cT=P.v("cT"), cV=P.v("cV"), cS=P.v("cS"), cG=P.v("cG"),
                       ys=P.v("dys"))

    if upto < 5:
        _finish(P, nc, stk)
        return nc

    midstk.close()

    with ExitStack() as ph5:
        def S5(name, shape, dt):
            return ph5.enter_context(nc.sbuf_tensor(name, shape, dt))
        G = S5("G", [128, HT * c.TOWN], BF16)
        wo_bf = S5("wo_bf", [128, HT * H], BF16)
        wostg = [S5(f"wostg{i}", [128, H], F32) for i in range(2)]
        yld = [S5(f"yld{i}", [128, H], F32) for i in range(2)]
        ostg = [S5(f"ostg{i}", [128, H], F32) for i in range(2)]

        po = [ph5.enter_context(nc.psum_tensor(f"po{i}", [128, min(H, 2048)], F32)) for i in range(2)]

        entry5 = [P.w("cV", ph4_end["cV"]), P.w("cT", ph4_end["cT"]),
                  P.w("cG", ph4_end["cG"]), P.w("dys", ph4_end["ys"])]

        rg = nc.gpsimd.alloc_register("rg_gath")
        for jg in range(8):
            P.emit("gp", lambda h, jg=jg: h.reg_load(rg, CPAR[0:1, 2 * jg:2 * jg + 1]),
                   waits=(([P.w("dmisc", const_ready), P.w("cc", 2)] + entry5)
                          if jg == 0 else ()))
            P.dma("gp", G[:, :].rearrange("p (t x) -> p t x", t=HT)[:, 2 * jg:2 * jg + 2, :],
                  bass.AP(gath, rg, [[S, 128], [128 * S, 2], [1, c.TOWN]]), "dgr")
        g_done = P.v("dgr")

        wo_cms = []
        for j in range(HT):
            par = j % 2
            wv = list(entry5) if j < 2 else [P.w("cV", wo_cms[j - 2])]
            P.dma("sy", wostg[par][:, :], woT.ap()[j * 128:(j + 1) * 128, :],
                  f"dwo{par}", waits=wv)
            P.ve_("tensor_copy", wo_bf[:, j * H:(j + 1) * H], wostg[par][:, :],
                  waits=[P.w(f"dwo{par}")], inc="cV")
            wo_cms.append(P.v("cV"))
        wo_done = P.v("cV")

        fin_ms, st_ms = {}, {}
        for tt in range(c.TOWN // 128):
            ppar = tt % 2
            wv = [P.w("dys", ph4_end["ys"])]
            if tt >= 2:
                wv.append(P.w("cV", fin_ms[tt - 2]))
            P.dma("sc", yld[ppar][:, :], y_scr.ap()[tt * 128:(tt + 1) * 128, :],
                  f"dyl{ppar}", waits=wv)
            ylv = P.v(f"dyl{ppar}")
            first = True
            wv = [P.w("dgr", g_done), P.w("cV", wo_done)]
            if tt >= 2:
                wv.append(P.w("cV", fin_ms[tt - 2]))
            for j in range(HT):
                for m in range(H // 512):
                    P.mm(po[ppar][:, m * 512:(m + 1) * 512],
                         G[:, j * c.TOWN + tt * 128:j * c.TOWN + (tt + 1) * 128],
                         wo_bf[:, j * H + m * 512:j * H + (m + 1) * 512],
                         start=(j == 0), stop=(j == HT - 1),
                         waits=(wv if first else ()),
                         inc=("cT" if (j == HT - 1 and m == H // 512 - 1) else None))
                    first = False
            omm = P.v("cT")
            wv = [P.w("cT", omm), P.w(f"dyl{ppar}", ylv)]
            if tt >= 2:
                wv.append(P.w(f"dout{ppar}", st_ms[tt - 2]))
            P.ve_("tensor_add", ostg[ppar][:, :], po[ppar][:, :], yld[ppar][:, :],
                  waits=wv, inc="cV")
            fin_ms[tt] = P.v("cV")
            P.dma("sy", out_t.ap()[tt * 128:(tt + 1) * 128, :], ostg[ppar][:, :],
                  f"dout{ppar}", waits=[P.w("cV", fin_ms[tt])])
            st_ms[tt] = P.v(f"dout{ppar}")
        P.emit("sy", lambda h: h.nop(),
               waits=[P.w("dout0", P.v("dout0")), P.w("dout1", P.v("dout1"))])

    _finish(P, nc, stk)
    return nc


def _finish(P, nc, stk):
    with nc.Block() as block:
        @block.sync
        def _(h):
            for fn in P.plan["sy"]:
                fn(h)

        @block.tensor
        def _(h):
            for fn in P.plan["te"]:
                fn(h)

        @block.vector
        def _(h):
            for fn in P.plan["ve"]:
                fn(h)

        @block.scalar
        def _(h):
            for fn in P.plan["sc"]:
                fn(h)

        @block.gpsimd
        def _(h):
            for fn in P.plan["gp"]:
                fn(h)
    stk.close()



def shard_inputs(cfg, A):
    c = cfg
    x = np.ascontiguousarray(np.asarray(A["L0_attn_hidden_states"], np.float32))
    w_qkv = np.asarray(A["L0_attn_w_qkv"], dtype=np.float32)
    w_o = np.asarray(A["L0_attn_w_o"], dtype=np.float32)
    woT = np.ascontiguousarray(w_o.T)
    rwT = np.ascontiguousarray(np.asarray(A["L0_ffn_router_w"], np.float32).T)
    def tile_gu(w_eHI, w_sh):
        E_, H_, I_ = w_eHI.shape
        HT_, IT_ = H_ // 128, I_ // 128
        a = w_eHI.reshape(E_, HT_, 128, IT_, 128).transpose(0, 3, 2, 1, 4)
        a = a.reshape(E_ * IT_, 128, HT_, 128)
        b = w_sh.reshape(HT_, 128, 2 * IT_, 128).transpose(2, 1, 0, 3)
        return np.ascontiguousarray(np.concatenate([a, b], axis=0), dtype=np.float32)

    gwt = tile_gu(np.asarray(A["L0_ffn_gate_w"], np.float32),
                  np.asarray(A["L0_ffn_s_gate_w"], np.float32))
    uwt = tile_gu(np.asarray(A["L0_ffn_up_w"], np.float32),
                  np.asarray(A["L0_ffn_s_up_w"], np.float32))
    dwa = np.asarray(A["L0_ffn_down_w"], np.float32)
    dwt = np.ascontiguousarray(np.concatenate(
        [dwa.reshape(-1, c.H), np.asarray(A["L0_ffn_s_down_w"], np.float32)], axis=0))
    sgb = np.ascontiguousarray(
        np.asarray(A["L0_ffn_s_gate_b"], np.float32).reshape(2 * c.IT, 128).T)
    sub = np.ascontiguousarray(
        np.asarray(A["L0_ffn_s_up_b"], np.float32).reshape(2 * c.IT, 128).T)
    sdbp = np.zeros((128, c.H), np.float32)
    sdbp[0, :] = np.asarray(A["L0_ffn_s_down_b"], np.float32)

    HPC, HD = c.HPC, c.HD
    maps = []
    for core in range(8):
        b = core // 4
        qpos = core % 4
        heads = list(range(HPC * qpos, HPC * (qpos + 1)))
        rows = []
        for kind in range(3):
            for h in heads:
                r0 = kind * c.NH * HD + h * HD
                rows.append(w_qkv[r0:r0 + HD, :])
        wqkvT = np.ascontiguousarray(np.concatenate(rows, axis=0).T)
        xbat = np.ascontiguousarray(x[b])
        xown = np.ascontiguousarray(x[b, qpos * c.TOWN:(qpos + 1) * c.TOWN, :])
        cpar = np.zeros((1, 48), np.int32)
        q0 = qpos * c.TOWN
        for j in range(c.HT):
            rank = 4 * b + j // HPC
            row = rank * HPC * 128 + (j % HPC) * 128
            cpar[0, j] = row * c.S + q0
            cpar[0, 16 + j] = (128 * j) * c.S + q0
            cpar[0, 32 + j] = (128 * j) * c.S + q0 + c.TH
        maps.append(dict(xb=xbat, xo=xown, wqkvT=wqkvT, woT=woT, rwT=rwT,
                         gwt=gwt, uwt=uwt, dwt=dwt,
                         sgb=sgb, sub=sub, sdbp=sdbp, cpar=cpar))
    return maps


def kernel(**inputs):
    cfg = Cfg()
    nc = build_program(cfg)
    maps = shard_inputs(cfg, inputs)
    res = run_bass_kernel_spmd(nc, maps, core_ids=list(range(8)))
    parts = [res.results[i]["out"] for i in range(8)]
    out = np.empty((cfg.B, cfg.S, cfg.H), np.float32)
    for core in range(8):
        b, qpos = core // 4, core % 4
        out[b, qpos * cfg.TOWN:(qpos + 1) * cfg.TOWN, :] = parts[core]
    return out


# version 14
# speedup vs baseline: 1.0113x; 1.0113x over previous
import numpy as np
import ml_dtypes
import concourse.bass as bass
import concourse.mybir as mybir
from concourse.bass_utils import run_bass_kernel_spmd
from contextlib import ExitStack

F32 = mybir.dt.float32
BF16 = mybir.dt.bfloat16
I32 = mybir.dt.int32
AF = mybir.ActivationFunctionType
ALU = mybir.AluOpType
AX = mybir.AxisListType

ENGS = ("sy", "te", "ve", "sc", "gp")
SHIFT = 8.0


class Cfg:
    def __init__(self, B=2, S=8192, H=2048, NH=16, HD=128, E=8, I=1408, SI=2816):
        assert SI == 2 * I
        self.B, self.S, self.H, self.NH, self.HD = B, S, H, NH, HD
        self.E, self.I, self.SI = E, I, SI
        self.HT = H // 128
        self.IT = I // 128
        self.HPC = NH * B // 8
        self.TOWN = S // 4
        self.TH = self.TOWN // 2
        self.SEG = S // 4
        self.W = min(1024, self.SEG)
        self.NW = self.SEG // self.W
        self.QCH = min(1024, S)
        self.NCH = S // self.QCH
        self.KSH = self.QCH // 128
        self.QKO = 2 * self.HPC
        self.VD = self.HPC * 128
        self.NHH = max(1, H // 512)
        self.DHW = H // self.NHH
        self.NTT = self.TH // 128
        assert self.QCH % 128 == 0 and self.W % 128 == 0 and self.TH % 128 == 0
        assert self.DHW % 512 == 0


class Plan:
    def __init__(self, nc):
        self.nc = nc
        self.plan = {e: [] for e in ENGS}
        self.cnt = {}
        self.sems = {}

    def sem(self, name):
        self.sems[name] = self.nc.alloc_semaphore(name)
        self.cnt[name] = 0

    def v(self, name):
        return self.cnt[name]

    def w(self, name, val=None):
        return (self.sems[name], self.cnt[name] if val is None else val)

    def emit(self, eng, fn, waits=(), inc=None, inck=1):
        if inc is not None:
            self.cnt[inc] += inck
        ih = None if inc is None else (self.sems[inc], inck)
        ws = tuple((a, b) for (a, b) in waits)

        def run(h, fn=fn, ws=ws, ih=ih):
            for sm, vv in ws:
                h.wait_ge(sm, vv)
            ins = fn(h)
            if ih is not None:
                ins.then_inc(ih[0], ih[1])
        self.plan[eng].append(run)
        return None if inc is None else self.cnt[inc]

    def mm(self, o, l, r, start, stop, waits=(), inc=None):
        return self.emit("te", lambda h, o=o, l=l, r=r, a=start, b=stop:
                         h.matmul(o, l, r, start=a, stop=b), waits, inc)

    def tr(self, o, i, ident, waits=(), inc=None):
        return self.emit("te", lambda h, o=o, i=i, d=ident: h.transpose(o, i, d),
                         waits, inc)

    def act(self, o, i, f, bias=0.0, scale=1.0, waits=(), inc=None):
        sw = [self.w("cS")]
        return self.emit("sc", lambda h, o=o, i=i, f=f, b=bias, s=scale:
                         h.activation(o, i, f, bias=b, scale=s),
                         sw + list(waits), inc)

    def ve_(self, fname, *args, waits=(), inc=None):
        sw = [self.w("cV")]
        return self.emit("ve", lambda h, fn=fname, aa=args: getattr(h, fn)(*aa),
                         sw + list(waits), inc)

    def gp_(self, fname, *args, waits=(), inc=None):
        sw = [self.w("cG")]
        return self.emit("gp", lambda h, fn=fname, aa=args: getattr(h, fn)(*aa),
                         sw + list(waits), inc)

    def dma(self, eng, o, i, sem, waits=()):
        return self.emit(eng, lambda h, o=o, i=i: h.dma_start(out=o, in_=i),
                         waits, sem, 16)


def build_program(cfg, upto=5):
    c = cfg
    nc = bass.Bass("TRN2", target_bir_lowering=False)
    S, H, HT, HPC, W, TH = c.S, c.H, c.HT, c.HPC, c.W, c.TH
    NE = c.E + 2

    din = {}

    def inp(name, shape, dt=F32):
        din[name] = nc.dram_tensor(name, list(shape), dt, kind="ExternalInput")
        return din[name]

    xb = inp("xb", [S, H])
    xo = inp("xo", [c.TOWN, H])
    wqkvT = inp("wqkvT", [H, 3 * HPC * 128])
    woT = inp("woT", [H, H])
    rwT = inp("rwT", [H, c.E])
    gwt = inp("gwt", [NE * c.IT, 128, HT, 128])
    uwt = inp("uwt", [NE * c.IT, 128, HT, 128])
    dwt = inp("dwt", [NE * c.IT * 128, H])
    sgb = inp("sgb", [128, 2 * c.IT])
    sub = inp("sub", [128, 2 * c.IT])
    sdbp = inp("sdbp", [128, H])
    cpar = inp("cpar", [1, 48], I32)

    out_t = nc.dram_tensor("out", [c.TOWN, H], F32, kind="ExternalOutput")

    qT_scr = nc.dram_tensor("qT_scr", [HPC * 128, S], BF16, kind="Internal")
    kT_scr = nc.dram_tensor("kT_scr", [HPC * 128, S], BF16, kind="Internal")
    v_scr = nc.dram_tensor("v_scr", [S, c.VD], BF16, kind="Internal")
    contrib = nc.dram_tensor("contrib", [HPC * 128, S], BF16, kind="Internal")
    gath = nc.dram_tensor("gath", [8 * HPC * 128, S], BF16, kind="Internal")
    tok_in = nc.dram_tensor("tok_in", [1, 4], F32, kind="Internal")
    tok_out = nc.dram_tensor("tok_out", [1, 4], F32, kind="Internal")
    y_scr = nc.dram_tensor("y_scr", [c.TOWN, H], F32, kind="Internal")
    xnT_scr = nc.dram_tensor("xnT_scr", [HT * 128, S], BF16, kind="Internal")

    ident_bf_d = nc.inline_tensor(np.eye(128, dtype=ml_dtypes.bfloat16), "ident_bf")
    ident_f_d = nc.inline_tensor(np.eye(128, dtype=np.float32), "ident_f")
    tri_np = np.triu(np.ones((128, 128), dtype=np.float32)).astype(ml_dtypes.bfloat16)
    tri_bf_d = nc.inline_tensor(tri_np, "tri_bf")
    selm = np.zeros((128, c.KSH * 128), dtype=ml_dtypes.bfloat16)
    for b in range(c.KSH):
        selm[b, b * 128:(b + 1) * 128] = 1
    sel_bf_d = nc.inline_tensor(selm, "sel_bf")
    row0m = np.zeros((128, 128), dtype=ml_dtypes.bfloat16)
    row0m[0, :] = 1
    row0_bf_d = nc.inline_tensor(row0m, "row0_bf")

    P = Plan(nc)
    for name in ["dx0", "dx1", "dqk0", "dqk1", "dv0", "dv1", "dwq0", "dwq1", "dmisc",
                 "dq0", "dq1", "dk0", "dk1", "dk2", "dk3", "dvv0", "dvv1", "dvv2", "dvv3", "dct0", "dct1",
                 "dtok", "dg0", "dg1", "du0", "du1", "dd0", "dd1", "dyi",
                 "dys", "dsb", "dgr", "dwo0", "dwo1", "dyl0", "dyl1", "dout0", "dout1",
                 "dxn", "dxl",
                 "cT", "cV", "cS", "cG", "cc"]:
        P.sem(name)

    stk = ExitStack()

    def SBp(name, shape, dt):
        return stk.enter_context(nc.sbuf_tensor(name, shape, dt))

    rw_bf = SBp("rw_bf", [128, HT * c.E], BF16)
    RDP = SBp("RDP", [128, 128], F32)
    CPAR = SBp("CPAR", [1, 48], I32)
    LNW = SBp("LNW", [128, 80], F32)
    LNW2 = SBp("LNW2", [128, 64], F32)
    TOKS = SBp("TOKS", [1, 4], F32)
    IDB = SBp("IDB", [128, 128], BF16)
    IDF = SBp("IDF", [128, 128], F32)
    TRI = SBp("TRI", [128, 128], BF16)
    SEL = SBp("SEL", [128, c.KSH * 128], BF16)
    ROW0 = SBp("ROW0", [128, 128], BF16)
    ident_bf, ident_f = IDB[:, :], IDF[:, :]
    tri_bf, sel_bf, row0_bf = TRI[:, :], SEL[:, :], ROW0[:, :]

    CONSTS = SBp("CONSTS", [128, 4], F32)

    P.dma("sy", CPAR[:, :], cpar.ap(), "dmisc")
    for dst, srcd in ((IDB, ident_bf_d), (IDF, ident_f_d), (TRI, tri_bf_d),
                      (SEL, sel_bf_d), (ROW0, row0_bf_d)):
        P.dma("sy", dst[:, :], srcd.ap(), "dmisc")
    const_ready = P.v("dmisc")
    P.ve_("memset", RDP[:, :], 0.0, inc="cV")
    P.ve_("memset", CONSTS[:, 0:1], 1e-5, inc="cV")
    P.ve_("memset", CONSTS[:, 1:2], -SHIFT, inc="cV")
    cmem = P.v("cV")
    EPS_AP = CONSTS[:, 0:1]
    NSHIFT_AP = CONSTS[:, 1:2]
    for eng in ENGS:
        P.emit(eng, lambda h: h.nop(),
               waits=[P.w("dmisc", const_ready), P.w("cV", cmem)])

    def emit_ln_tile(xsrc_ap, xstg_t, xnstg_t, xw_sem, par, dma_waits, act_waits=()):
        L = LNW[:, 40 * par:40 * par + 40]
        P.dma("sy", xstg_t[:, :], xsrc_ap, xw_sem, waits=dma_waits)
        dv = P.v(xw_sem)
        nbk = (H + 511) // 512
        for b in range(nbk):
            cw = min(512, H - b * 512)
            P.ve_("bn_stats", L[:, 6 * b:6 * b + 6], xstg_t[:, b * 512:b * 512 + cw],
                  waits=([P.w(xw_sem, dv)] if b == 0 else ()), inc="cV")
        P.ve_("bn_aggr", L[:, 6 * nbk:6 * nbk + 2], L[:, 0:6 * nbk], inc="cV")
        va = P.v("cV")
        mcol = 6 * nbk
        P.act(L[:, mcol + 2:mcol + 3], L[:, mcol + 1:mcol + 2], AF.Sqrt,
              bias=EPS_AP, scale=1.0, waits=[P.w("cV", va)], inc="cS")
        sq = P.v("cS")
        P.ve_("reciprocal", L[:, mcol + 3:mcol + 4], L[:, mcol + 2:mcol + 3],
              waits=[P.w("cS", sq)], inc="cV")
        P.ve_("tensor_scalar", xnstg_t[:, :], xstg_t[:, :], L[:, mcol:mcol + 1],
              L[:, mcol + 3:mcol + 4], ALU.subtract, ALU.mult,
              waits=list(act_waits), inc="cV")
        return ("cV", P.v("cV"))

    midstk = ExitStack()
    xnTo = midstk.enter_context(nc.sbuf_tensor("xnTo", [128, HT * c.TH], BF16))

    with ExitStack() as ph1:
        def S1(name, shape, dt):
            return ph1.enter_context(nc.sbuf_tensor(name, shape, dt))
        xstg = [S1(f"xstg{i}", [128, H], F32) for i in range(2)]
        xnstg = [S1(f"xnstg{i}", [128, H], BF16) for i in range(2)]
        xnT = S1("xnT", [128, HT * W], BF16)
        qkstg = [S1(f"qkstg{i}", [128, W], BF16) for i in range(2)]
        vstg = [S1(f"vstg{i}", [128, c.VD], BF16) for i in range(2)]
        OW = 3 * HPC * 128
        wqstg = [S1(f"wqstg{i}", [128, OW], F32) for i in range(2)]
        wq_bf = S1("wq_bf", [128, HT * OW], BF16)

        pqk = [ph1.enter_context(nc.psum_tensor(f"pqk{i}", [128, W], F32)) for i in range(2)]
        pv = [ph1.enter_context(nc.psum_tensor(f"pv{i}", [128, c.VD], F32)) for i in range(2)]
        ptr = [ph1.enter_context(nc.psum_tensor(f"ptr{i}", [128, 8 * 128], BF16)) for i in range(2)]

        wq_cast = []
        for ht in range(HT):
            par = ht % 2
            wv = [P.w("cV", wq_cast[ht - 2])] if ht >= 2 else []
            P.dma("sc", wqstg[par][:, :], wqkvT.ap()[ht * 128:(ht + 1) * 128, :],
                  f"dwq{par}", waits=wv)
            P.ve_("tensor_copy", wq_bf[:, ht * OW:(ht + 1) * OW], wqstg[par][:, :],
                  waits=[P.w(f"dwq{par}")], inc="cV")
            wq_cast.append(P.v("cV"))
        P.dma("sc", wqstg[0][:, 0:HT * c.E].rearrange("p (t e) -> p t e", t=HT),
              rwT.ap().rearrange("(t p) e -> p t e", p=128), "dwq0",
              waits=[P.w("cV", wq_cast[-2])])
        P.ve_("tensor_copy", rw_bf[:, :], wqstg[0][:, 0:HT * c.E],
              waits=[P.w("dwq0")], inc="cV")
        wq_ready = P.v("cV")

        ln_hist = {}
        tpose_hist = {}
        evac_hist = {}
        trn = [0]
        qk_evac, qk_st = {}, {}
        v_evac, v_st = {}, {}
        nttw = W // 128
        vn = [0]
        xnst = {}

        for g in range(4):
            for wi in range(c.NW):
                wfirst = trn[0]
                for i in range(nttw):
                    ti = (g * c.NW + wi) * nttw + i
                    par = ti % 2
                    row0 = g * c.SEG + wi * W + i * 128
                    wv = [P.w(*ln_hist[ti - 2])] if ti >= 2 else []
                    aw = [P.w("cT", tpose_hist[ti - 2])] if ti >= 2 else []
                    ln_ms = emit_ln_tile(xb.ap()[row0:row0 + 128, :],
                                         xstg[par], xnstg[par], f"dx{par}", par,
                                         wv, aw)
                    ln_hist[ti] = ln_ms
                    ngrp = (HT + 7) // 8
                    for grp in range(ngrp):
                        nj = min(8, HT - grp * 8)
                        u = trn[0]
                        tp = u % 2
                        wv2 = [P.w(*ln_ms)]
                        if u >= 2:
                            wv2.append(P.w("cV", evac_hist[u - 2]))
                        for j in range(nj):
                            jj = grp * 8 + j
                            P.tr(ptr[tp][:, j * 128:(j + 1) * 128],
                                 xnstg[par][:, jj * 128:(jj + 1) * 128], ident_bf,
                                 waits=(wv2 if j == 0 else ()),
                                 inc=("cT" if j == nj - 1 else None))
                        tpose_hist[ti] = P.v("cT")
                        oap = xnT[:, :].rearrange("p (t x) -> p t x", t=HT)[
                            :, grp * 8:grp * 8 + nj, i * 128:i * 128 + 128]
                        iap = ptr[tp][:, 0:nj * 128].rearrange("p (t x) -> p t x", t=nj)
                        ew = [P.w("cT", P.v("cT"))]
                        if u == wfirst:
                            wiq = g * c.NW + wi
                            if wiq >= 1:
                                ew.append(P.w("dxn", xnst[wiq - 1]))
                        P.ve_("tensor_copy", oap, iap, waits=ew, inc="cV")
                        evac_hist[u] = P.v("cV")
                        trn[0] += 1
                xn_ready = P.v("cV")
                col = g * c.SEG + wi * W
                for ht in range(HT):
                    P.dma("sy", xnT_scr.ap()[ht * 128:(ht + 1) * 128, col:col + W],
                          xnT[:, ht * W:(ht + 1) * W], "dxn",
                          waits=([P.w("cV", xn_ready)] if ht == 0 else ()))
                xnst[g * c.NW + wi] = P.v("dxn")

                nmm = (W + 511) // 512
                for ot in range(c.QKO):
                    pb = ot % 2
                    gk = (g * c.NW + wi) * c.QKO + ot
                    wv = [P.w("cV", xn_ready), P.w("cV", wq_ready)]
                    if gk >= 2:
                        wv.append(P.w("cV", qk_evac[gk - 2]))
                    first = True
                    for ht in range(HT):
                        for m in range(nmm):
                            cw = min(512, W - m * 512)
                            P.mm(pqk[pb][:, m * 512:m * 512 + cw],
                                 wq_bf[:, ht * OW + ot * 128:ht * OW + (ot + 1) * 128],
                                 xnT[:, ht * W + m * 512:ht * W + m * 512 + cw],
                                 start=(ht == 0), stop=(ht == HT - 1),
                                 waits=(wv if first else ()),
                                 inc=("cT" if (ht == HT - 1 and m == nmm - 1) else None))
                            first = False
                    tm = P.v("cT")
                    stw = [P.w("cT", tm)]
                    if gk >= 2:
                        stw.append(P.w(f"dqk{pb}", qk_st[gk - 2]))
                    P.ve_("tensor_copy", qkstg[pb][:, :], pqk[pb][:, :],
                          waits=stw, inc="cV")
                    qk_evac[gk] = P.v("cV")
                    scr = qT_scr if ot < HPC else kT_scr
                    rb = (ot % HPC) * 128
                    col = g * c.SEG + wi * W
                    P.dma("sy", scr.ap()[rb:rb + 128, col:col + W], qkstg[pb][:, :],
                          f"dqk{pb}", waits=[P.w("cV", qk_evac[gk])])
                    qk_st[gk] = P.v(f"dqk{pb}")

                for i in range(nttw):
                    vi = vn[0]
                    par = vi % 2
                    row0 = g * c.SEG + wi * W + i * 128
                    wv = [P.w("cV", xn_ready)]
                    if vi >= 2:
                        wv.append(P.w("cV", v_evac[vi - 2]))
                    first = True
                    for ht in range(HT):
                        P.mm(pv[par][:, :],
                             xnT[:, ht * W + i * 128:ht * W + (i + 1) * 128],
                             wq_bf[:, ht * OW + 2 * HPC * 128:(ht + 1) * OW],
                             start=(ht == 0), stop=(ht == HT - 1),
                             waits=(wv if first else ()),
                             inc=("cT" if ht == HT - 1 else None))
                        first = False
                    tm = P.v("cT")
                    stw = [P.w("cT", tm)]
                    if vi >= 2:
                        stw.append(P.w(f"dv{par}", v_st[vi - 2]))
                    P.ve_("tensor_copy", vstg[par][:, :], pv[par][:, :], waits=stw,
                          inc="cV")
                    v_evac[vi] = P.v("cV")
                    P.dma("sy", v_scr.ap()[row0:row0 + 128, :], vstg[par][:, :],
                          f"dv{par}", waits=[P.w("cV", v_evac[vi])])
                    v_st[vi] = P.v(f"dv{par}")
                    vn[0] += 1

        ph1_end = dict(cT=P.v("cT"), cV=P.v("cV"), cS=P.v("cS"),
                       q0=P.v("dqk0"), q1=P.v("dqk1"),
                       v0=P.v("dv0"), v1=P.v("dv1"))

    if upto < 2:
        _finish(P, nc, stk)
        return nc

    scl = 1.0 / float(np.sqrt(c.HD))
    KSH = c.KSH
    with ExitStack() as ph2:
        def S2(name, shape, dt):
            return ph2.enter_context(nc.sbuf_tensor(name, shape, dt))
        QT = [S2(f"QT{i}", [128, c.QCH], BF16) for i in range(2)]
        KT = [S2(f"KT{i}", [128, c.QCH], BF16) for i in range(2)]
        VT = [S2(f"VT{i}", [128, c.QCH], BF16) for i in range(2)]
        PRBN = 7
        PRB = [S2(f"PRB{i}", [128, c.QCH], BF16) for i in range(PRBN)]
        ACL = S2("ACL", [128, c.QCH], BF16)
        ACC = S2("ACC", [128, c.QCH], F32)
        RDB = S2("RDB", [128, c.QCH], BF16)
        AST = [S2(f"AST{i}", [128, c.QCH], BF16) for i in range(2)]
        RDT = S2("RDT", [128, 128], BF16)
        DN = S2("DN", [128, KSH], F32)

        pav = ph2.enter_context(nc.psum_tensor("pav", [128, c.QCH], F32))
        psc = [ph2.enter_context(nc.psum_tensor(f"psc{i}", [128, c.QCH], F32)) for i in range(3)]
        pdn = psc[2]

        entry2 = [P.w("cV", ph1_end["cV"]), P.w("cT", ph1_end["cT"]),
                  P.w("dqk0", ph1_end["q0"]), P.w("dqk1", ph1_end["q1"]),
                  P.w("dv0", ph1_end["v0"]), P.w("dv1", ph1_end["v1"])]

        kld, vld, kuse, vuse = {}, {}, {}, {}
        qld, quse = {}, {}
        ast_st = {}
        prev_norm = [None]
        prev_trm = [None]
        kst = [0]
        qn = [0]

        def load_ksup(hl, st_idx):
            n = kst[0]
            par = n % 2
            wv = list(entry2) if n == 0 else []
            if n >= 2:
                wv.append(P.w("cT", kuse[n - 2]))
            P.dma("gp", KT[par][:, :],
                  kT_scr.ap()[hl * 128:(hl + 1) * 128,
                              st_idx * c.QCH:(st_idx + 1) * c.QCH],
                  f"dk{par}", waits=wv)
            kld[n] = P.v(f"dk{par}")
            wv2 = list(entry2) if n == 0 else []
            if n >= 2:
                wv2.append(P.w("cT", vuse[n - 2]))
            vsl = v_scr.ap()[st_idx * c.QCH:(st_idx + 1) * c.QCH,
                             hl * 128:(hl + 1) * 128]
            P.dma("sy", VT[par][:, :].rearrange("p (s d) -> p s d", s=KSH),
                  vsl.rearrange("(s p) d -> p s d", p=128), f"dvv{par}", waits=wv2)
            vld[n] = P.v(f"dvv{par}")
            kst[0] += 1
            return n

        for hl in range(HPC):
            for ch in range(c.NCH):
                cn = qn[0]
                qpar = cn % 2
                wv = list(entry2) if cn == 0 else []
                if cn >= 2:
                    wv.append(P.w("cT", quse[cn - 2]))
                P.dma("gp", QT[qpar][:, :],
                      qT_scr.ap()[hl * 128:(hl + 1) * 128,
                                  ch * c.QCH:(ch + 1) * c.QCH],
                      f"dq{qpar}", waits=wv)
                qld[cn] = P.v(f"dq{qpar}")
                nsup = ch + 1
                nkt = nsup * KSH
                ksup_ids = [load_ksup(hl, 0)]
                exp_ms, av_ms, acc_ms = {}, {}, {}
                prb_rdy = {}

                def meta(kt):
                    st_i, j = divmod(kt, KSH)
                    diag = (st_i == nsup - 1)
                    return st_i, j, diag, (j * 128 if diag else 0)

                def emit_scores(kt):
                    st_i, j, diag, qlo = meta(kt)
                    ku = ksup_ids[st_i]
                    kpar = ku % 2
                    spar = kt % 3
                    ppar = kt % PRBN
                    wv = [P.w(f"dk{kpar}", kld[ku]), P.w(f"dq{qpar}", qld[cn])]
                    if kt >= 3:
                        wv.append(P.w("cS", exp_ms[kt - 3]))
                    else:
                        wv.append(P.w("cS"))
                        wv.append(P.w("cV"))
                    first = True
                    off = qlo
                    while off < c.QCH:
                        cw = min(512 - off % 512, c.QCH - off)
                        P.mm(psc[spar][:, off:off + cw],
                             KT[kpar][:, j * 128:(j + 1) * 128],
                             QT[qpar][:, off:off + cw], True, True,
                             waits=(wv if first else ()),
                             inc=("cT" if off + cw >= c.QCH else None))
                        first = False
                        off += cw
                    sc_ms = P.v("cT")
                    wv = [P.w("cT", sc_ms)]
                    if kt >= PRBN:
                        wv += [P.w("cT", av_ms[kt - PRBN]),
                               P.w("cV", acc_ms[kt - PRBN])]
                    P.act(PRB[ppar][:, qlo:c.QCH], psc[spar][:, qlo:c.QCH], AF.Exp,
                          bias=NSHIFT_AP, scale=scl, waits=wv, inc="cS")
                    exp_ms[kt] = P.v("cS")
                    rdy = ("cS", exp_ms[kt])
                    if diag:
                        P.ve_("tensor_mul", PRB[ppar][:, qlo:qlo + 128],
                              PRB[ppar][:, qlo:qlo + 128], tri_bf,
                              waits=[P.w("cS", exp_ms[kt])], inc="cV")
                        rdy = ("cV", P.v("cV"))
                    prb_rdy[kt] = rdy
                    wacc = [P.w(rdy[0], rdy[1])]
                    if kt == 0 and prev_trm[0] is not None:
                        wacc.append(P.w("cT", prev_trm[0]))
                    if j == 0:
                        P.ve_("tensor_copy", ACL[:, :], PRB[ppar][:, :],
                              waits=wacc, inc="cV")
                    else:
                        P.ve_("tensor_add", ACL[:, qlo:c.QCH], ACL[:, qlo:c.QCH],
                              PRB[ppar][:, qlo:c.QCH], waits=wacc, inc="cV")
                    if j == KSH - 1:
                        if st_i == 0:
                            P.ve_("tensor_copy", ACC[:, :], ACL[:, :], inc="cV")
                        else:
                            P.ve_("tensor_add", ACC[:, :], ACC[:, :], ACL[:, :],
                                  inc="cV")
                    acc_ms[kt] = P.v("cV")

                def emit_av(kt):
                    st_i, j, diag, qlo = meta(kt)
                    if j == 0 and st_i + 1 < nsup:
                        ksup_ids.append(load_ksup(hl, st_i + 1))
                    ku = ksup_ids[st_i]
                    kpar = ku % 2
                    ppar = kt % PRBN
                    rdy = prb_rdy[kt]
                    wv = [P.w(rdy[0], rdy[1]), P.w(f"dvv{kpar}", vld[ku])]
                    if kt == 0 and prev_norm[0] is not None:
                        wv.append(P.w("cV", prev_norm[0]))
                    first = True
                    off = qlo
                    while off < c.QCH:
                        cw = min(512 - off % 512, c.QCH - off)
                        bank = off // 512
                        jmax = min(KSH - 1, 4 * bank + 3)
                        stp = diag and (j == jmax)
                        lastmm = (off + cw >= c.QCH)
                        P.mm(pav[:, off:off + cw],
                             VT[kpar][:, j * 128:(j + 1) * 128],
                             PRB[ppar][:, off:off + cw],
                             (kt == 0), stp,
                             waits=(wv if first else ()),
                             inc=("cT" if lastmm else None))
                        first = False
                        off += cw
                    av_ms[kt] = P.v("cT")
                    if j == KSH - 1:
                        kuse[ku] = P.v("cT")
                        vuse[ku] = P.v("cT")

                BAT = 3
                done_av = 0
                for g0 in range(0, nkt, BAT):
                    for kt in range(g0, min(g0 + BAT, nkt)):
                        emit_scores(kt)
                    if g0 > 0:
                        for kt in range(g0 - BAT, g0):
                            emit_av(kt)
                            done_av = kt + 1
                for kt in range(done_av, nkt):
                    emit_av(kt)
                quse[cn] = P.v("cT")
                accv = P.v("cV")
                for qb in range(KSH):
                    P.tr(pdn[:, qb * 128:(qb + 1) * 128],
                         ACC[:, qb * 128:(qb + 1) * 128], ident_f,
                         waits=([P.w("cV", accv)] if qb == 0 else ()),
                         inc=("cT" if qb == KSH - 1 else None))
                trm = P.v("cT")
                prev_trm[0] = trm
                P.ve_("tensor_reduce",
                      DN[:, 0:KSH].rearrange("p (a b) -> p a b", a=KSH),
                      pdn[:, :].rearrange("p (a b) -> p a b", a=KSH),
                      AX.X, ALU.add, waits=[P.w("cT", trm)], inc="cV")
                P.ve_("reciprocal", RDP[:, 0:KSH], DN[:, 0:KSH], inc="cV")
                rdv = P.v("cV")
                P.tr(pdn[:, 0:128], RDP[:, :], ident_f,
                     waits=[P.w("cV", rdv)], inc="cT")
                t2 = P.v("cT")
                P.ve_("tensor_copy", RDT[:, :], pdn[:, 0:128],
                      waits=[P.w("cT", t2)], inc="cV")
                rdtv = P.v("cV")
                for qb in range(KSH):
                    P.mm(pdn[:, qb * 128:(qb + 1) * 128],
                         sel_bf[:, qb * 128:(qb + 1) * 128], RDT[:, :],
                         True, True,
                         waits=([P.w("cV", rdtv)] if qb == 0 else ()),
                         inc=("cT" if qb == KSH - 1 else None))
                bctm = P.v("cT")
                apar = cn % 2
                wv = [P.w("cT", bctm)]
                if cn >= 2:
                    wv.append(P.w(f"dct{apar}", ast_st[cn - 2]))
                P.ve_("tensor_copy", RDB[:, :], pdn[:, :], waits=wv, inc="cV")
                P.ve_("tensor_mul", AST[apar][:, :], pav[:, :], RDB[:, :], inc="cV")
                prev_norm[0] = P.v("cV")
                P.dma("gp", contrib.ap()[hl * 128:(hl + 1) * 128,
                                         ch * c.QCH:(ch + 1) * c.QCH],
                      AST[apar][:, :], f"dct{apar}",
                      waits=[P.w("cV", prev_norm[0])])
                ast_st[cn] = P.v(f"dct{apar}")
                qn[0] += 1
        ph2_end = dict(cT=P.v("cT"), cV=P.v("cV"), cS=P.v("cS"),
                       ct0=P.v("dct0"), ct1=P.v("dct1"), xn=P.v("dxn"))

    rg_xn = nc.gpsimd.alloc_register("rg_xn")
    for jg in range(4):
        P.emit("gp", lambda h, jg=jg: h.reg_load(rg_xn, CPAR[0:1, 16 + 4 * jg:17 + 4 * jg]),
               waits=([P.w("dxn", ph2_end["xn"]), P.w("dmisc", const_ready)]
                      if jg == 0 else ()))
        P.dma("gp", xnTo[:, :].rearrange("p (t x) -> p t x", t=HT)[:, 4 * jg:4 * jg + 4, :],
              bass.AP(xnT_scr, rg_xn, [[S, 128], [128 * S, 4], [1, c.TH]]),
              "dxl")
    xnl_h0 = P.v("dxl")

    if upto >= 3:
        P.gp_("memset", TOKS[:, :], 1.0, inc="cG")
        P.dma("gp", tok_in.ap(), TOKS[:, :], "dtok",
              waits=[P.w("cG"), P.w("dct0", ph2_end["ct0"]),
                     P.w("dct1", ph2_end["ct1"])])
        P.emit("gp", lambda h: h.collective_compute(
            "AllGather", ALU.bypass, replica_groups=[list(range(8))],
            ins=[contrib.ap()], outs=[gath.ap()]),
            waits=[P.w("dtok")], inc="cc")

    if upto < 4:
        _finish(P, nc, stk)
        return nc

    NTT, NHH, DHW = c.NTT, c.NHH, c.DHW
    with ExitStack() as ph4:
        def S4(name, shape, dt):
            return ph4.enter_context(nc.sbuf_tensor(name, shape, dt))
        yacc = S4("yacc", [128, NTT * H], F32)
        mT = S4("mT", [128, c.IT * TH], BF16)
        dbfh = S4("dbfh", [128, 2 * c.IT * DHW], BF16)
        LNU = S4("LNU", [128, 4 * H + 2 * DHW], F32)
        gstg = [LNU[:, 0:H], LNU[:, H:2 * H]]
        ustg = [LNU[:, 2 * H:3 * H], LNU[:, 3 * H:4 * H]]
        dstg = [LNU[:, 4 * H:4 * H + DHW], LNU[:, 4 * H + DHW:4 * H + 2 * DHW]]
        gbf = [S4(f"gbf{i}", [128, H], BF16) for i in range(2)]
        ubf = [S4(f"ubf{i}", [128, H], BF16) for i in range(2)]
        MGS = S4("MGS", [128, TH], BF16)
        COMB = S4("COMB", [128, NTT * c.E], F32)
        SGB = S4("SGB", [128, 2 * c.IT], F32)
        SUB = S4("SUB", [128, 2 * c.IT], F32)
        SDB = S4("SDB", [128, H], BF16)

        pg = ph4.enter_context(nc.psum_tensor("pg", [128, TH], F32))
        pu = ph4.enter_context(nc.psum_tensor("pu", [128, TH], F32))
        pd = [ph4.enter_context(nc.psum_tensor(f"pd{i}", [128, DHW], F32)) for i in range(2)]
        pr = ph4.enter_context(nc.psum_tensor("pr", [128, c.E], F32))

        entry4 = [P.w("cV", ph2_end["cV"]), P.w("cT", ph2_end["cT"]),
                  P.w("cS", ph2_end["cS"]),
                  P.w("dct0", ph2_end["ct0"]), P.w("dct1", ph2_end["ct1"]),
                  P.w("dxn", ph2_end["xn"])]

        P.dma("sy", SGB[:, :], sgb.ap(), "dsb", waits=entry4)
        P.dma("sy", SUB[:, :], sub.ap(), "dsb")
        sgb_v = P.v("dsb")

        gcast, ucast, dcast_all = {}, {}, {}
        guse, uuse = {}, {}
        gun = [0]
        g4n = [0]
        g4evac = {}
        ddn = [0]
        y_ms = {}
        y_prev = {}
        silu_last = [None]
        mupd_last = [None]
        dbfh_freeR = [None, None]
        dgc = [0]
        stt_ms = {}
        down_done = [None]
        ys_half0 = None
        prev_half_done = [None]
        sdb_ready = [None]

        P.dma("sy", LNU[:, 0:H], sdbp.ap(), "dsb", waits=entry4)
        sdl = P.v("dsb")
        P.ve_("tensor_copy", SDB[:, :], LNU[:, 0:H],
              waits=[P.w("dsb", sdl)], inc="cV")
        sdb_ready = [P.v("cV")]

        for hf in range(2):
            if hf == 0:
                xnl = xnl_h0
            else:
                for jg in range(4):
                    P.emit("gp", lambda h, jg=jg: h.reg_load(
                        rg_xn, CPAR[0:1, 32 + 4 * jg:33 + 4 * jg]),
                        waits=([P.w("cT"), P.w("cV")] if jg == 0 else ()))
                    P.dma("gp", xnTo[:, :].rearrange("p (t x) -> p t x", t=HT)[:, 4 * jg:4 * jg + 4, :],
                          bass.AP(xnT_scr, rg_xn, [[S, 128], [128 * S, 4], [1, c.TH]]),
                          "dxl")
                xnl = P.v("dxl")

            yi_w = [P.w("cV", P.v("cV"))]
            if hf == 1:
                yi_w.append(P.w("dys", ys_half0))
            for tt in range(NTT):
                P.dma("sy", yacc[:, tt * H:(tt + 1) * H],
                      xo.ap()[(hf * NTT + tt) * 128:(hf * NTT + tt + 1) * 128, :],
                      "dyi", waits=(yi_w if tt == 0 else ()))
            yinit = P.v("dyi")

            for tt in range(NTT):
                first = True
                wv = [P.w("dxl", xnl), P.w("cV", P.v("cV"))]
                for ht in range(HT):
                    P.mm(pr[:, 0:c.E],
                         xnTo[:, ht * TH + tt * 128:ht * TH + (tt + 1) * 128],
                         rw_bf[:, ht * c.E:(ht + 1) * c.E],
                         start=(ht == 0), stop=(ht == HT - 1),
                         waits=(wv if first else ()),
                         inc=("cT" if ht == HT - 1 else None))
                    first = False
                tm = P.v("cT")
                E = c.E
                Z = LNW2[:, 0:E]
                P.ve_("reduce_max", LNW2[:, 32:33], pr[:, 0:E], AX.X,
                      None, None, True, waits=[P.w("cT", tm)], inc="cV")
                P.ve_("tensor_scalar_add", Z, pr[:, 0:E], LNW2[:, 32:33], inc="cV")
                P.ve_("tensor_single_scalar", LNW2[:, 8:8 + E], Z, 0.0,
                      ALU.is_equal, inc="cV")
                P.ve_("scalar_tensor_tensor", LNW2[:, 16:16 + E],
                      LNW2[:, 8:8 + E], -1e30, Z, ALU.mult, ALU.add, inc="cV")
                P.ve_("reduce_max", LNW2[:, 33:34], LNW2[:, 16:16 + E], AX.X,
                      None, None, True, inc="cV")
                P.ve_("tensor_scalar_mul", LNW2[:, 34:35], LNW2[:, 33:34],
                      -1.0, inc="cV")
                vv = P.v("cV")
                P.act(LNW2[:, 35:36], LNW2[:, 34:35], AF.Exp,
                      waits=[P.w("cV", vv)], inc="cS")
                P.act(LNW2[:, 24:24 + E], Z, AF.Exp, inc="cS")
                sv = P.v("cS")
                P.ve_("tensor_scalar_add", LNW2[:, 36:37], LNW2[:, 35:36], 1.0,
                      waits=[P.w("cS", sv)], inc="cV")
                P.ve_("reciprocal", LNW2[:, 37:38], LNW2[:, 36:37], inc="cV")
                P.ve_("tensor_single_scalar", LNW2[:, 40:40 + E], Z,
                      LNW2[:, 34:35], ALU.is_ge, inc="cV")
                P.ve_("tensor_mul", LNW2[:, 48:48 + E], LNW2[:, 24:24 + E],
                      LNW2[:, 40:40 + E], inc="cV")
                P.ve_("tensor_scalar_mul", COMB[:, tt * E:(tt + 1) * E],
                      LNW2[:, 48:48 + E], LNW2[:, 37:38], inc="cV")
            comb_ready = P.v("cV")

            for e2 in range(NE):
                shared = (e2 >= c.E)
                silu_ms = {}
                mupd_ms = {}
                def emit_down_stream(hh, it, extra=()):
                    dn_ = ddn[0]
                    dpar = dn_ % 2
                    wv = list(extra)
                    if dn_ >= 2:
                        wv.append(P.w("cG", dcast_all[dn_ - 2]))
                    else:
                        wv.append(P.w("cG"))
                    reg = hh % 2
                    r0 = (e2 * c.IT + it) * 128
                    P.dma("sc", dstg[dpar][:, :],
                          dwt.ap()[r0:r0 + 128, hh * DHW:(hh + 1) * DHW],
                          f"dd{dpar}", waits=wv)
                    dv = P.v(f"dd{dpar}")
                    wv2 = [P.w(f"dd{dpar}", dv)]
                    if dbfh_freeR[reg] is not None and it == 0:
                        wv2.append(P.w("cT", dbfh_freeR[reg]))
                    P.gp_("tensor_copy",
                          dbfh[:, (reg * c.IT + it) * DHW:(reg * c.IT + it + 1) * DHW],
                          dstg[dpar][:, :], waits=wv2, inc="cG")
                    dcast_all[dn_] = P.v("cG")
                    ddn[0] += 1
                    return P.v("cG")

                dcast_hh = {}
                for it in range(c.IT):
                    n = gun[0]
                    par = n % 2
                    tid = e2 * c.IT + it
                    for which, stg, bfb, castd, used, sq, dmaq, castq in (
                            ("g", gstg, gbf, gcast, guse, "dg", "sy", "sc"),
                            ("u", ustg, ubf, ucast, uuse, "du", "sy", "gp")):
                        eap = (gwt if which == "g" else uwt).ap()[tid]
                        csem = "cS" if castq == "sc" else "cG"
                        wv = []
                        if n == 0 and hf == 0:
                            wv = list(entry4)
                        if e2 == 0 and it == 0:
                            wv.append(P.w("cV"))
                        if n >= 2:
                            wv.append(P.w(csem, castd[n - 2]))
                        P.dma(dmaq,
                              stg[par][:, :].rearrange("p (s i) -> p s i", i=128),
                              eap, f"{sq}{par}", waits=wv)
                        dv = P.v(f"{sq}{par}")
                        wv2 = [P.w(f"{sq}{par}", dv)]
                        if n >= 2:
                            wv2.append(P.w("cT", used[n - 2]))
                        if castq == "sc":
                            P.act(bfb[par][:, :], stg[par][:, :], AF.Copy,
                                  waits=wv2, inc="cS")
                        else:
                            P.gp_("tensor_copy", bfb[par][:, :], stg[par][:, :],
                                  waits=wv2, inc="cG")
                        castd[n] = P.v(csem)
                    nmm = (TH + 511) // 512
                    first = True
                    wv = [P.w("cS", gcast[n]), P.w("dxl", xnl)]
                    if it >= 1:
                        wv.append(P.w("cS", silu_ms[it - 1]))
                    elif silu_last[0] is not None:
                        wv.append(P.w("cS", silu_last[0]))
                    for ht in range(HT):
                        for m in range(nmm):
                            cw = min(512, TH - m * 512)
                            P.mm(pg[:, m * 512:m * 512 + cw],
                                 gbf[par][:, ht * 128:(ht + 1) * 128],
                                 xnTo[:, ht * TH + m * 512:ht * TH + m * 512 + cw],
                                 start=(ht == 0), stop=(ht == HT - 1),
                                 waits=(wv if first else ()),
                                 inc=("cT" if (ht == HT - 1 and m == nmm - 1) else None))
                            first = False
                    gmm = P.v("cT")
                    guse[n] = gmm
                    first = True
                    wv = [P.w("cG", ucast[n])]
                    if it >= 1:
                        wv.append(P.w("cV", mupd_ms[it - 1]))
                    elif mupd_last[0] is not None:
                        wv.append(P.w("cV", mupd_last[0]))
                    for ht in range(HT):
                        for m in range(nmm):
                            cw = min(512, TH - m * 512)
                            P.mm(pu[:, m * 512:m * 512 + cw],
                                 ubf[par][:, ht * 128:(ht + 1) * 128],
                                 xnTo[:, ht * TH + m * 512:ht * TH + m * 512 + cw],
                                 start=(ht == 0), stop=(ht == HT - 1),
                                 waits=(wv if first else ()),
                                 inc=("cT" if (ht == HT - 1 and m == nmm - 1) else None))
                            first = False
                    umm = P.v("cT")
                    uuse[n] = umm
                    gb = SGB[:, (e2 - c.E) * c.IT + it:(e2 - c.E) * c.IT + it + 1] if shared else 0.0
                    wv = [P.w("cT", gmm)]
                    if it >= 1:
                        wv.append(P.w("cV", mupd_ms[it - 1]))
                    elif mupd_last[0] is not None:
                        wv.append(P.w("cV", mupd_last[0]))
                    if shared:
                        wv.append(P.w("dsb"))
                    P.act(MGS[:, :], pg[:, :], AF.Silu, bias=gb, waits=wv, inc="cS")
                    silu_ms[it] = P.v("cS")
                    ub = SUB[:, (e2 - c.E) * c.IT + it:(e2 - c.E) * c.IT + it + 1] if shared else 0.0
                    P.ve_("scalar_tensor_tensor", mT[:, it * TH:(it + 1) * TH],
                          pu[:, :], ub, MGS[:, :], ALU.add, ALU.mult,
                          waits=[P.w("cT", umm), P.w("cS", silu_ms[it]),
                                 P.w("cT", P.v("cT"))], inc="cV")
                    mupd_ms[it] = P.v("cV")
                    ex = []
                    if e2 == 0 and it == 0:
                        ex.append(P.w("cV"))
                        if hf == 0:
                            ex += list(entry4)
                    dcast_hh[(0, it)] = emit_down_stream(0, it, extra=ex)
                    gun[0] += 1
                silu_last[0] = silu_ms[c.IT - 1]
                mupd_last[0] = mupd_ms[c.IT - 1]
                m_ready = P.v("cV")

                for hh in range(NHH):
                    if hh + 1 < NHH:
                        for it in range(c.IT):
                            dcast_hh[(hh + 1, it)] = emit_down_stream(hh + 1, it)
                    last_e = (e2 == NE - 1)
                    for tt in range(NTT):
                        pk = tt * NHH + hh
                        gidx = dgc[0]
                        ppar = gidx % 2
                        wv = [P.w("cV", m_ready),
                              P.w("cG", dcast_hh[(hh, c.IT - 1)]),
                              P.w("dyi", yinit)]
                        if stt_ms.get(gidx - 2) is not None:
                            wv.append(P.w("cV", stt_ms[gidx - 2]))
                        first = True
                        nm = DHW // 512
                        reg = hh % 2
                        for it in range(c.IT):
                            for m in range(nm):
                                lastmm = (not last_e) and it == c.IT - 1 and m == nm - 1
                                P.mm(pd[ppar][:, m * 512:(m + 1) * 512],
                                     mT[:, it * TH + tt * 128:it * TH + (tt + 1) * 128],
                                     dbfh[:, (reg * c.IT + it) * DHW + m * 512:
                                          (reg * c.IT + it) * DHW + (m + 1) * 512],
                                     start=(it == 0),
                                     stop=(it == c.IT - 1 and not last_e),
                                     waits=(wv if first else ()),
                                     inc=("cT" if lastmm else None))
                                first = False
                        if last_e:
                            for m in range(nm):
                                P.mm(pd[ppar][:, m * 512:(m + 1) * 512], row0_bf,
                                     SDB[:, hh * DHW + m * 512:hh * DHW + (m + 1) * 512],
                                     False, True,
                                     waits=([P.w("cV", sdb_ready[0])] if (tt == 0 and hh == 0 and hf == 0) else ()),
                                     inc=("cT" if m == nm - 1 else None))
                        dmm = P.v("cT")
                        combs = 1.0 if shared else COMB[:, tt * c.E + e2:tt * c.E + e2 + 1]
                        ysl = yacc[:, tt * H + hh * DHW:tt * H + (hh + 1) * DHW]
                        P.ve_("scalar_tensor_tensor", ysl, pd[ppar][:, :], combs,
                              ysl, ALU.mult, ALU.add,
                              waits=[P.w("cT", dmm)], inc="cV")
                        y_ms[(hf, e2, pk)] = P.v("cV")
                        stt_ms[gidx] = P.v("cV")
                        dgc[0] += 1
                    dbfh_freeR[hh % 2] = P.v("cT")
                down_done[0] = P.v("cT")
            y_prev = {pk: y_ms.get((hf, NE - 1, pk)) for pk in range(NTT * NHH)}
            for tt in range(NTT):
                wv = [P.w("cV", y_ms[(hf, NE - 1, tt * NHH + hhx)])
                      for hhx in range(NHH)]
                P.dma("sy", y_scr.ap()[(hf * NTT + tt) * 128:(hf * NTT + tt + 1) * 128, :],
                      yacc[:, tt * H:(tt + 1) * H], "dys", waits=wv)
            if hf == 0:
                ys_half0 = P.v("dys")
            prev_half_done[0] = P.v("cV")
        ph4_end = dict(cT=P.v("cT"), cV=P.v("cV"), cS=P.v("cS"), cG=P.v("cG"),
                       ys=P.v("dys"))

    if upto < 5:
        _finish(P, nc, stk)
        return nc

    midstk.close()

    with ExitStack() as ph5:
        def S5(name, shape, dt):
            return ph5.enter_context(nc.sbuf_tensor(name, shape, dt))
        G = S5("G", [128, HT * c.TOWN], BF16)
        wo_bf = S5("wo_bf", [128, HT * H], BF16)
        wostg = [S5(f"wostg{i}", [128, H], F32) for i in range(2)]
        yld = [S5(f"yld{i}", [128, H], F32) for i in range(2)]
        ostg = [S5(f"ostg{i}", [128, H], F32) for i in range(2)]

        po = [ph5.enter_context(nc.psum_tensor(f"po{i}", [128, min(H, 2048)], F32)) for i in range(2)]

        entry5 = [P.w("cV", ph4_end["cV"]), P.w("cT", ph4_end["cT"]),
                  P.w("cG", ph4_end["cG"]), P.w("dys", ph4_end["ys"])]

        P.emit("gp", lambda h: h.collective_compute(
            "AllReduce", ALU.add, replica_groups=[list(range(8))],
            ins=[tok_in.ap()], outs=[tok_out.ap()]),
            waits=[P.w("cc")], inc="cc")
        rg = nc.gpsimd.alloc_register("rg_gath")
        for jg in range(8):
            P.emit("gp", lambda h, jg=jg: h.reg_load(rg, CPAR[0:1, 2 * jg:2 * jg + 1]),
                   waits=(([P.w("dmisc", const_ready), P.w("cc", 2)] + entry5)
                          if jg == 0 else ()))
            P.dma("gp", G[:, :].rearrange("p (t x) -> p t x", t=HT)[:, 2 * jg:2 * jg + 2, :],
                  bass.AP(gath, rg, [[S, 128], [128 * S, 2], [1, c.TOWN]]), "dgr")
        g_done = P.v("dgr")

        wo_cms = []
        for j in range(HT):
            par = j % 2
            wv = list(entry5) if j < 2 else [P.w("cV", wo_cms[j - 2])]
            P.dma("sy", wostg[par][:, :], woT.ap()[j * 128:(j + 1) * 128, :],
                  f"dwo{par}", waits=wv)
            P.ve_("tensor_copy", wo_bf[:, j * H:(j + 1) * H], wostg[par][:, :],
                  waits=[P.w(f"dwo{par}")], inc="cV")
            wo_cms.append(P.v("cV"))
        wo_done = P.v("cV")

        fin_ms, st_ms = {}, {}
        for tt in range(c.TOWN // 128):
            ppar = tt % 2
            wv = [P.w("dys", ph4_end["ys"])]
            if tt >= 2:
                wv.append(P.w("cV", fin_ms[tt - 2]))
            P.dma("sc", yld[ppar][:, :], y_scr.ap()[tt * 128:(tt + 1) * 128, :],
                  f"dyl{ppar}", waits=wv)
            ylv = P.v(f"dyl{ppar}")
            first = True
            wv = [P.w("dgr", g_done), P.w("cV", wo_done)]
            if tt >= 2:
                wv.append(P.w("cV", fin_ms[tt - 2]))
            for j in range(HT):
                for m in range(H // 512):
                    P.mm(po[ppar][:, m * 512:(m + 1) * 512],
                         G[:, j * c.TOWN + tt * 128:j * c.TOWN + (tt + 1) * 128],
                         wo_bf[:, j * H + m * 512:j * H + (m + 1) * 512],
                         start=(j == 0), stop=(j == HT - 1),
                         waits=(wv if first else ()),
                         inc=("cT" if (j == HT - 1 and m == H // 512 - 1) else None))
                    first = False
            omm = P.v("cT")
            wv = [P.w("cT", omm), P.w(f"dyl{ppar}", ylv)]
            if tt >= 2:
                wv.append(P.w(f"dout{ppar}", st_ms[tt - 2]))
            P.ve_("tensor_add", ostg[ppar][:, :], po[ppar][:, :], yld[ppar][:, :],
                  waits=wv, inc="cV")
            fin_ms[tt] = P.v("cV")
            P.dma("sy", out_t.ap()[tt * 128:(tt + 1) * 128, :], ostg[ppar][:, :],
                  f"dout{ppar}", waits=[P.w("cV", fin_ms[tt])])
            st_ms[tt] = P.v(f"dout{ppar}")
        P.emit("sy", lambda h: h.nop(),
               waits=[P.w("dout0", P.v("dout0")), P.w("dout1", P.v("dout1"))])

    _finish(P, nc, stk)
    return nc


def _finish(P, nc, stk):
    with nc.Block() as block:
        @block.sync
        def _(h):
            for fn in P.plan["sy"]:
                fn(h)

        @block.tensor
        def _(h):
            for fn in P.plan["te"]:
                fn(h)

        @block.vector
        def _(h):
            for fn in P.plan["ve"]:
                fn(h)

        @block.scalar
        def _(h):
            for fn in P.plan["sc"]:
                fn(h)

        @block.gpsimd
        def _(h):
            for fn in P.plan["gp"]:
                fn(h)
    stk.close()



def shard_inputs(cfg, A):
    c = cfg
    x = np.ascontiguousarray(np.asarray(A["L0_attn_hidden_states"], np.float32))
    w_qkv = np.asarray(A["L0_attn_w_qkv"], dtype=np.float32)
    w_o = np.asarray(A["L0_attn_w_o"], dtype=np.float32)
    woT = np.ascontiguousarray(w_o.T)
    rwT = np.ascontiguousarray(np.asarray(A["L0_ffn_router_w"], np.float32).T)
    def tile_gu(w_eHI, w_sh):
        E_, H_, I_ = w_eHI.shape
        HT_, IT_ = H_ // 128, I_ // 128
        a = w_eHI.reshape(E_, HT_, 128, IT_, 128).transpose(0, 3, 2, 1, 4)
        a = a.reshape(E_ * IT_, 128, HT_, 128)
        b = w_sh.reshape(HT_, 128, 2 * IT_, 128).transpose(2, 1, 0, 3)
        return np.ascontiguousarray(np.concatenate([a, b], axis=0), dtype=np.float32)

    gwt = tile_gu(np.asarray(A["L0_ffn_gate_w"], np.float32),
                  np.asarray(A["L0_ffn_s_gate_w"], np.float32))
    uwt = tile_gu(np.asarray(A["L0_ffn_up_w"], np.float32),
                  np.asarray(A["L0_ffn_s_up_w"], np.float32))
    dwa = np.asarray(A["L0_ffn_down_w"], np.float32)
    dwt = np.ascontiguousarray(np.concatenate(
        [dwa.reshape(-1, c.H), np.asarray(A["L0_ffn_s_down_w"], np.float32)], axis=0))
    sgb = np.ascontiguousarray(
        np.asarray(A["L0_ffn_s_gate_b"], np.float32).reshape(2 * c.IT, 128).T)
    sub = np.ascontiguousarray(
        np.asarray(A["L0_ffn_s_up_b"], np.float32).reshape(2 * c.IT, 128).T)
    sdbp = np.zeros((128, c.H), np.float32)
    sdbp[0, :] = np.asarray(A["L0_ffn_s_down_b"], np.float32)

    HPC, HD = c.HPC, c.HD
    maps = []
    for core in range(8):
        b = core // 4
        qpos = core % 4
        heads = list(range(HPC * qpos, HPC * (qpos + 1)))
        rows = []
        for kind in range(3):
            for h in heads:
                r0 = kind * c.NH * HD + h * HD
                rows.append(w_qkv[r0:r0 + HD, :])
        wqkvT = np.ascontiguousarray(np.concatenate(rows, axis=0).T)
        xbat = np.ascontiguousarray(x[b])
        xown = np.ascontiguousarray(x[b, qpos * c.TOWN:(qpos + 1) * c.TOWN, :])
        cpar = np.zeros((1, 48), np.int32)
        q0 = qpos * c.TOWN
        for j in range(c.HT):
            rank = 4 * b + j // HPC
            row = rank * HPC * 128 + (j % HPC) * 128
            cpar[0, j] = row * c.S + q0
            cpar[0, 16 + j] = (128 * j) * c.S + q0
            cpar[0, 32 + j] = (128 * j) * c.S + q0 + c.TH
        maps.append(dict(xb=xbat, xo=xown, wqkvT=wqkvT, woT=woT, rwT=rwT,
                         gwt=gwt, uwt=uwt, dwt=dwt,
                         sgb=sgb, sub=sub, sdbp=sdbp, cpar=cpar))
    return maps


def kernel(**inputs):
    cfg = Cfg()
    nc = build_program(cfg)
    maps = shard_inputs(cfg, inputs)
    res = run_bass_kernel_spmd(nc, maps, core_ids=list(range(8)))
    parts = [res.results[i]["out"] for i in range(8)]
    out = np.empty((cfg.B, cfg.S, cfg.H), np.float32)
    for core in range(8):
        b, qpos = core // 4, core % 4
        out[b, qpos * cfg.TOWN:(qpos + 1) * cfg.TOWN, :] = parts[core]
    return out


# version 15
# speedup vs baseline: 1.0233x; 1.0119x over previous
import numpy as np
import ml_dtypes
import concourse.bass as bass
import concourse.mybir as mybir
from concourse.bass_utils import run_bass_kernel_spmd
from contextlib import ExitStack

F32 = mybir.dt.float32
BF16 = mybir.dt.bfloat16
I32 = mybir.dt.int32
AF = mybir.ActivationFunctionType
ALU = mybir.AluOpType
AX = mybir.AxisListType

ENGS = ("sy", "te", "ve", "sc", "gp")
SHIFT = 8.0


class Cfg:
    def __init__(self, B=2, S=8192, H=2048, NH=16, HD=128, E=8, I=1408, SI=2816):
        assert SI == 2 * I
        self.B, self.S, self.H, self.NH, self.HD = B, S, H, NH, HD
        self.E, self.I, self.SI = E, I, SI
        self.HT = H // 128
        self.IT = I // 128
        self.HPC = NH * B // 8
        self.TOWN = S // 4
        self.TH = self.TOWN // 2
        self.SEG = S // 4
        self.W = min(1024, self.SEG)
        self.NW = self.SEG // self.W
        self.QCH = min(1024, S)
        self.NCH = S // self.QCH
        self.KSH = self.QCH // 128
        self.QKO = 2 * self.HPC
        self.VD = self.HPC * 128
        self.NHH = max(1, H // 512)
        self.DHW = H // self.NHH
        self.NTT = self.TH // 128
        assert self.QCH % 128 == 0 and self.W % 128 == 0 and self.TH % 128 == 0
        assert self.DHW % 512 == 0


class Plan:
    def __init__(self, nc):
        self.nc = nc
        self.plan = {e: [] for e in ENGS}
        self.cnt = {}
        self.sems = {}

    def sem(self, name):
        self.sems[name] = self.nc.alloc_semaphore(name)
        self.cnt[name] = 0

    def v(self, name):
        return self.cnt[name]

    def w(self, name, val=None):
        return (self.sems[name], self.cnt[name] if val is None else val)

    def emit(self, eng, fn, waits=(), inc=None, inck=1):
        if inc is not None:
            self.cnt[inc] += inck
        ih = None if inc is None else (self.sems[inc], inck)
        ws = tuple((a, b) for (a, b) in waits)

        def run(h, fn=fn, ws=ws, ih=ih):
            for sm, vv in ws:
                h.wait_ge(sm, vv)
            ins = fn(h)
            if ih is not None:
                ins.then_inc(ih[0], ih[1])
        self.plan[eng].append(run)
        return None if inc is None else self.cnt[inc]

    def mm(self, o, l, r, start, stop, waits=(), inc=None):
        return self.emit("te", lambda h, o=o, l=l, r=r, a=start, b=stop:
                         h.matmul(o, l, r, start=a, stop=b), waits, inc)

    def tr(self, o, i, ident, waits=(), inc=None):
        return self.emit("te", lambda h, o=o, i=i, d=ident: h.transpose(o, i, d),
                         waits, inc)

    def act(self, o, i, f, bias=0.0, scale=1.0, waits=(), inc=None):
        sw = [self.w("cS")]
        return self.emit("sc", lambda h, o=o, i=i, f=f, b=bias, s=scale:
                         h.activation(o, i, f, bias=b, scale=s),
                         sw + list(waits), inc)

    def ve_(self, fname, *args, waits=(), inc=None):
        sw = [self.w("cV")]
        return self.emit("ve", lambda h, fn=fname, aa=args: getattr(h, fn)(*aa),
                         sw + list(waits), inc)

    def gp_(self, fname, *args, waits=(), inc=None):
        sw = [self.w("cG")]
        return self.emit("gp", lambda h, fn=fname, aa=args: getattr(h, fn)(*aa),
                         sw + list(waits), inc)

    def dma(self, eng, o, i, sem, waits=()):
        return self.emit(eng, lambda h, o=o, i=i: h.dma_start(out=o, in_=i),
                         waits, sem, 16)


def build_program(cfg, upto=5):
    c = cfg
    nc = bass.Bass("TRN2", target_bir_lowering=False)
    S, H, HT, HPC, W, TH = c.S, c.H, c.HT, c.HPC, c.W, c.TH
    NE = c.E + 2

    din = {}

    def inp(name, shape, dt=F32):
        din[name] = nc.dram_tensor(name, list(shape), dt, kind="ExternalInput")
        return din[name]

    xb = inp("xb", [S, H])
    xo = inp("xo", [c.TOWN, H])
    wqkvT = inp("wqkvT", [H, 3 * HPC * 128])
    woT = inp("woT", [H, H])
    rwT = inp("rwT", [H, c.E])
    gwt = inp("gwt", [NE * c.IT, 128, HT, 128])
    uwt = inp("uwt", [NE * c.IT, 128, HT, 128])
    dwt = inp("dwt", [NE * c.IT * 128, H])
    sgb = inp("sgb", [128, 2 * c.IT])
    sub = inp("sub", [128, 2 * c.IT])
    sdbp = inp("sdbp", [128, H])
    cpar = inp("cpar", [1, 48], I32)

    out_t = nc.dram_tensor("out", [c.TOWN, H], F32, kind="ExternalOutput")

    qT_scr = nc.dram_tensor("qT_scr", [HPC * 128, S], BF16, kind="Internal")
    kT_scr = nc.dram_tensor("kT_scr", [HPC * 128, S], BF16, kind="Internal")
    v_scr = nc.dram_tensor("v_scr", [S, c.VD], BF16, kind="Internal")
    contrib = nc.dram_tensor("contrib", [HPC * 128, S], BF16, kind="Internal")
    gath = nc.dram_tensor("gath", [8 * HPC * 128, S], BF16, kind="Internal")
    tok_in = nc.dram_tensor("tok_in", [1, 4], F32, kind="Internal")
    tok_out = nc.dram_tensor("tok_out", [1, 4], F32, kind="Internal")
    y_scr = nc.dram_tensor("y_scr", [c.TOWN, H], F32, kind="Internal")
    xnT_scr = nc.dram_tensor("xnT_scr", [HT * 128, S], BF16, kind="Internal")

    ident_bf_d = nc.inline_tensor(np.eye(128, dtype=ml_dtypes.bfloat16), "ident_bf")
    ident_f_d = nc.inline_tensor(np.eye(128, dtype=np.float32), "ident_f")
    tri_np = np.triu(np.ones((128, 128), dtype=np.float32)).astype(ml_dtypes.bfloat16)
    tri_bf_d = nc.inline_tensor(tri_np, "tri_bf")
    selm = np.zeros((128, c.KSH * 128), dtype=ml_dtypes.bfloat16)
    for b in range(c.KSH):
        selm[b, b * 128:(b + 1) * 128] = 1
    sel_bf_d = nc.inline_tensor(selm, "sel_bf")
    row0m = np.zeros((128, 128), dtype=ml_dtypes.bfloat16)
    row0m[0, :] = 1
    row0_bf_d = nc.inline_tensor(row0m, "row0_bf")

    P = Plan(nc)
    for name in ["dx0", "dx1", "dqk0", "dqk1", "dv0", "dv1", "dwq0", "dwq1", "dmisc",
                 "dq0", "dq1", "dk0", "dk1", "dk2", "dk3", "dvv0", "dvv1", "dvv2", "dvv3", "dct0", "dct1",
                 "dtok", "dg0", "dg1", "du0", "du1", "dd0", "dd1", "dyi",
                 "dys", "dsb", "dgr", "dwo0", "dwo1", "dyl0", "dyl1", "dout0", "dout1",
                 "dxn", "dxl",
                 "cT", "cV", "cS", "cG", "cc"]:
        P.sem(name)

    stk = ExitStack()

    def SBp(name, shape, dt):
        return stk.enter_context(nc.sbuf_tensor(name, shape, dt))

    rw_bf = SBp("rw_bf", [128, HT * c.E], BF16)
    RDP = SBp("RDP", [128, 128], F32)
    CPAR = SBp("CPAR", [1, 48], I32)
    LNW = SBp("LNW", [128, 80], F32)
    LNW2 = SBp("LNW2", [128, 64], F32)
    TOKS = SBp("TOKS", [1, 4], F32)
    IDB = SBp("IDB", [128, 128], BF16)
    IDF = SBp("IDF", [128, 128], F32)
    TRI = SBp("TRI", [128, 128], BF16)
    SEL = SBp("SEL", [128, c.KSH * 128], BF16)
    ROW0 = SBp("ROW0", [128, 128], BF16)
    ident_bf, ident_f = IDB[:, :], IDF[:, :]
    tri_bf, sel_bf, row0_bf = TRI[:, :], SEL[:, :], ROW0[:, :]

    CONSTS = SBp("CONSTS", [128, 4], F32)

    P.dma("sy", CPAR[:, :], cpar.ap(), "dmisc")
    for dst, srcd in ((IDB, ident_bf_d), (IDF, ident_f_d), (TRI, tri_bf_d),
                      (SEL, sel_bf_d), (ROW0, row0_bf_d)):
        P.dma("sy", dst[:, :], srcd.ap(), "dmisc")
    const_ready = P.v("dmisc")
    P.ve_("memset", RDP[:, :], 0.0, inc="cV")
    P.ve_("memset", CONSTS[:, 0:1], 1e-5, inc="cV")
    P.ve_("memset", CONSTS[:, 1:2], -SHIFT, inc="cV")
    cmem = P.v("cV")
    EPS_AP = CONSTS[:, 0:1]
    NSHIFT_AP = CONSTS[:, 1:2]
    for eng in ENGS:
        P.emit(eng, lambda h: h.nop(),
               waits=[P.w("dmisc", const_ready), P.w("cV", cmem)])

    def emit_ln_tile(xsrc_ap, xstg_t, xnstg_t, xw_sem, par, dma_waits, act_waits=()):
        L = LNW[:, 40 * par:40 * par + 40]
        P.dma("sy", xstg_t[:, :], xsrc_ap, xw_sem, waits=dma_waits)
        dv = P.v(xw_sem)
        nbk = (H + 511) // 512
        for b in range(nbk):
            cw = min(512, H - b * 512)
            P.ve_("bn_stats", L[:, 6 * b:6 * b + 6], xstg_t[:, b * 512:b * 512 + cw],
                  waits=([P.w(xw_sem, dv)] if b == 0 else ()), inc="cV")
        P.ve_("bn_aggr", L[:, 6 * nbk:6 * nbk + 2], L[:, 0:6 * nbk], inc="cV")
        va = P.v("cV")
        mcol = 6 * nbk
        P.act(L[:, mcol + 2:mcol + 3], L[:, mcol + 1:mcol + 2], AF.Sqrt,
              bias=EPS_AP, scale=1.0, waits=[P.w("cV", va)], inc="cS")
        sq = P.v("cS")
        P.ve_("reciprocal", L[:, mcol + 3:mcol + 4], L[:, mcol + 2:mcol + 3],
              waits=[P.w("cS", sq)], inc="cV")
        P.ve_("tensor_scalar", xnstg_t[:, :], xstg_t[:, :], L[:, mcol:mcol + 1],
              L[:, mcol + 3:mcol + 4], ALU.subtract, ALU.mult,
              waits=list(act_waits), inc="cV")
        return ("cV", P.v("cV"))

    midstk = ExitStack()
    xnTo = midstk.enter_context(nc.sbuf_tensor("xnTo", [128, HT * c.TH], BF16))

    with ExitStack() as ph1:
        def S1(name, shape, dt):
            return ph1.enter_context(nc.sbuf_tensor(name, shape, dt))
        xstg = [S1(f"xstg{i}", [128, H], F32) for i in range(2)]
        xnstg = [S1(f"xnstg{i}", [128, H], BF16) for i in range(2)]
        xnT = S1("xnT", [128, HT * W], BF16)
        qkstg = [S1(f"qkstg{i}", [128, W], BF16) for i in range(2)]
        vstg = [S1(f"vstg{i}", [128, c.VD], BF16) for i in range(2)]
        OW = 3 * HPC * 128
        wqstg = [S1(f"wqstg{i}", [128, OW], F32) for i in range(2)]
        wq_bf = S1("wq_bf", [128, HT * OW], BF16)

        pqk = [ph1.enter_context(nc.psum_tensor(f"pqk{i}", [128, W], F32)) for i in range(2)]
        pv = [ph1.enter_context(nc.psum_tensor(f"pv{i}", [128, c.VD], F32)) for i in range(2)]
        ptr = [ph1.enter_context(nc.psum_tensor(f"ptr{i}", [128, 8 * 128], BF16)) for i in range(2)]

        wq_cast = []
        for ht in range(HT):
            par = ht % 2
            wv = [P.w("cV", wq_cast[ht - 2])] if ht >= 2 else []
            P.dma("sc", wqstg[par][:, :], wqkvT.ap()[ht * 128:(ht + 1) * 128, :],
                  f"dwq{par}", waits=wv)
            P.ve_("tensor_copy", wq_bf[:, ht * OW:(ht + 1) * OW], wqstg[par][:, :],
                  waits=[P.w(f"dwq{par}")], inc="cV")
            wq_cast.append(P.v("cV"))
        P.dma("sc", wqstg[0][:, 0:HT * c.E].rearrange("p (t e) -> p t e", t=HT),
              rwT.ap().rearrange("(t p) e -> p t e", p=128), "dwq0",
              waits=[P.w("cV", wq_cast[-2])])
        P.ve_("tensor_copy", rw_bf[:, :], wqstg[0][:, 0:HT * c.E],
              waits=[P.w("dwq0")], inc="cV")
        wq_ready = P.v("cV")

        ln_hist = {}
        tpose_hist = {}
        evac_hist = {}
        trn = [0]
        qk_evac, qk_st = {}, {}
        v_evac, v_st = {}, {}
        nttw = W // 128
        vn = [0]
        xnst = {}

        for g in range(4):
            for wi in range(c.NW):
                wfirst = trn[0]
                for i in range(nttw):
                    ti = (g * c.NW + wi) * nttw + i
                    par = ti % 2
                    row0 = g * c.SEG + wi * W + i * 128
                    wv = [P.w(*ln_hist[ti - 2])] if ti >= 2 else []
                    aw = [P.w("cT", tpose_hist[ti - 2])] if ti >= 2 else []
                    ln_ms = emit_ln_tile(xb.ap()[row0:row0 + 128, :],
                                         xstg[par], xnstg[par], f"dx{par}", par,
                                         wv, aw)
                    ln_hist[ti] = ln_ms
                    ngrp = (HT + 7) // 8
                    for grp in range(ngrp):
                        nj = min(8, HT - grp * 8)
                        u = trn[0]
                        tp = u % 2
                        wv2 = [P.w(*ln_ms)]
                        if u >= 2:
                            wv2.append(P.w("cV", evac_hist[u - 2]))
                        for j in range(nj):
                            jj = grp * 8 + j
                            P.tr(ptr[tp][:, j * 128:(j + 1) * 128],
                                 xnstg[par][:, jj * 128:(jj + 1) * 128], ident_bf,
                                 waits=(wv2 if j == 0 else ()),
                                 inc=("cT" if j == nj - 1 else None))
                        tpose_hist[ti] = P.v("cT")
                        oap = xnT[:, :].rearrange("p (t x) -> p t x", t=HT)[
                            :, grp * 8:grp * 8 + nj, i * 128:i * 128 + 128]
                        iap = ptr[tp][:, 0:nj * 128].rearrange("p (t x) -> p t x", t=nj)
                        ew = [P.w("cT", P.v("cT"))]
                        if u == wfirst:
                            wiq = g * c.NW + wi
                            if wiq >= 1:
                                ew.append(P.w("dxn", xnst[wiq - 1]))
                        P.ve_("tensor_copy", oap, iap, waits=ew, inc="cV")
                        evac_hist[u] = P.v("cV")
                        trn[0] += 1
                xn_ready = P.v("cV")
                col = g * c.SEG + wi * W
                for ht in range(HT):
                    P.dma("sy", xnT_scr.ap()[ht * 128:(ht + 1) * 128, col:col + W],
                          xnT[:, ht * W:(ht + 1) * W], "dxn",
                          waits=([P.w("cV", xn_ready)] if ht == 0 else ()))
                xnst[g * c.NW + wi] = P.v("dxn")

                nmm = (W + 511) // 512
                for ot in range(c.QKO):
                    pb = ot % 2
                    gk = (g * c.NW + wi) * c.QKO + ot
                    wv = [P.w("cV", xn_ready), P.w("cV", wq_ready)]
                    if gk >= 2:
                        wv.append(P.w("cV", qk_evac[gk - 2]))
                    first = True
                    for ht in range(HT):
                        for m in range(nmm):
                            cw = min(512, W - m * 512)
                            P.mm(pqk[pb][:, m * 512:m * 512 + cw],
                                 wq_bf[:, ht * OW + ot * 128:ht * OW + (ot + 1) * 128],
                                 xnT[:, ht * W + m * 512:ht * W + m * 512 + cw],
                                 start=(ht == 0), stop=(ht == HT - 1),
                                 waits=(wv if first else ()),
                                 inc=("cT" if (ht == HT - 1 and m == nmm - 1) else None))
                            first = False
                    tm = P.v("cT")
                    stw = [P.w("cT", tm)]
                    if gk >= 2:
                        stw.append(P.w(f"dqk{pb}", qk_st[gk - 2]))
                    P.ve_("tensor_copy", qkstg[pb][:, :], pqk[pb][:, :],
                          waits=stw, inc="cV")
                    qk_evac[gk] = P.v("cV")
                    scr = qT_scr if ot < HPC else kT_scr
                    rb = (ot % HPC) * 128
                    col = g * c.SEG + wi * W
                    P.dma("sy", scr.ap()[rb:rb + 128, col:col + W], qkstg[pb][:, :],
                          f"dqk{pb}", waits=[P.w("cV", qk_evac[gk])])
                    qk_st[gk] = P.v(f"dqk{pb}")

                for i in range(nttw):
                    vi = vn[0]
                    par = vi % 2
                    row0 = g * c.SEG + wi * W + i * 128
                    wv = [P.w("cV", xn_ready)]
                    if vi >= 2:
                        wv.append(P.w("cV", v_evac[vi - 2]))
                    first = True
                    for ht in range(HT):
                        P.mm(pv[par][:, :],
                             xnT[:, ht * W + i * 128:ht * W + (i + 1) * 128],
                             wq_bf[:, ht * OW + 2 * HPC * 128:(ht + 1) * OW],
                             start=(ht == 0), stop=(ht == HT - 1),
                             waits=(wv if first else ()),
                             inc=("cT" if ht == HT - 1 else None))
                        first = False
                    tm = P.v("cT")
                    stw = [P.w("cT", tm)]
                    if vi >= 2:
                        stw.append(P.w(f"dv{par}", v_st[vi - 2]))
                    P.ve_("tensor_copy", vstg[par][:, :], pv[par][:, :], waits=stw,
                          inc="cV")
                    v_evac[vi] = P.v("cV")
                    P.dma("sy", v_scr.ap()[row0:row0 + 128, :], vstg[par][:, :],
                          f"dv{par}", waits=[P.w("cV", v_evac[vi])])
                    v_st[vi] = P.v(f"dv{par}")
                    vn[0] += 1

        ph1_end = dict(cT=P.v("cT"), cV=P.v("cV"), cS=P.v("cS"),
                       q0=P.v("dqk0"), q1=P.v("dqk1"),
                       v0=P.v("dv0"), v1=P.v("dv1"))

    if upto < 2:
        _finish(P, nc, stk)
        return nc

    scl = 1.0 / float(np.sqrt(c.HD))
    KSH = c.KSH
    with ExitStack() as ph2:
        def S2(name, shape, dt):
            return ph2.enter_context(nc.sbuf_tensor(name, shape, dt))
        QT = [S2(f"QT{i}", [128, c.QCH], BF16) for i in range(2)]
        KT = [S2(f"KT{i}", [128, c.QCH], BF16) for i in range(2)]
        VT = [S2(f"VT{i}", [128, c.QCH], BF16) for i in range(2)]
        PRBN = 7
        PRB = [S2(f"PRB{i}", [128, c.QCH], BF16) for i in range(PRBN)]
        ACL = S2("ACL", [128, c.QCH], BF16)
        ACC = S2("ACC", [128, c.QCH], F32)
        RDB = S2("RDB", [128, c.QCH], BF16)
        AST = [S2(f"AST{i}", [128, c.QCH], BF16) for i in range(2)]
        RDT = S2("RDT", [128, 128], BF16)
        DN = S2("DN", [128, KSH], F32)

        pav = ph2.enter_context(nc.psum_tensor("pav", [128, c.QCH], F32))
        psc = [ph2.enter_context(nc.psum_tensor(f"psc{i}", [128, c.QCH], F32)) for i in range(3)]
        pdn = psc[2]

        entry2 = [P.w("cV", ph1_end["cV"]), P.w("cT", ph1_end["cT"]),
                  P.w("dqk0", ph1_end["q0"]), P.w("dqk1", ph1_end["q1"]),
                  P.w("dv0", ph1_end["v0"]), P.w("dv1", ph1_end["v1"])]

        kld, vld, kuse, vuse = {}, {}, {}, {}
        qld, quse = {}, {}
        ast_st = {}
        prev_norm = [None]
        prev_trm = [None]
        psc_free = [None, None, None]
        prev_accv = [None]
        kst = [0]
        qn = [0]

        def load_ksup(hl, st_idx):
            n = kst[0]
            par = n % 2
            wv = list(entry2) if n == 0 else []
            if n >= 2:
                wv.append(P.w("cT", kuse[n - 2]))
            P.dma("gp", KT[par][:, :],
                  kT_scr.ap()[hl * 128:(hl + 1) * 128,
                              st_idx * c.QCH:(st_idx + 1) * c.QCH],
                  f"dk{par}", waits=wv)
            kld[n] = P.v(f"dk{par}")
            wv2 = list(entry2) if n == 0 else []
            if n >= 2:
                wv2.append(P.w("cT", vuse[n - 2]))
            vsl = v_scr.ap()[st_idx * c.QCH:(st_idx + 1) * c.QCH,
                             hl * 128:(hl + 1) * 128]
            P.dma("sy", VT[par][:, :].rearrange("p (s d) -> p s d", s=KSH),
                  vsl.rearrange("(s p) d -> p s d", p=128), f"dvv{par}", waits=wv2)
            vld[n] = P.v(f"dvv{par}")
            kst[0] += 1
            return n

        for hl in range(HPC):
            for ch in range(c.NCH):
                cn = qn[0]
                qpar = cn % 2
                wv = list(entry2) if cn == 0 else []
                if cn >= 2:
                    wv.append(P.w("cT", quse[cn - 2]))
                P.dma("gp", QT[qpar][:, :],
                      qT_scr.ap()[hl * 128:(hl + 1) * 128,
                                  ch * c.QCH:(ch + 1) * c.QCH],
                      f"dq{qpar}", waits=wv)
                qld[cn] = P.v(f"dq{qpar}")
                nsup = ch + 1
                nkt = nsup * KSH
                ksup_ids = [load_ksup(hl, 0)]
                exp_ms, av_ms, acc_ms = {}, {}, {}
                prb_rdy = {}

                def meta(kt):
                    st_i, j = divmod(kt, KSH)
                    diag = (st_i == nsup - 1)
                    return st_i, j, diag, (j * 128 if diag else 0)

                def emit_scores(kt):
                    st_i, j, diag, qlo = meta(kt)
                    ku = ksup_ids[st_i]
                    kpar = ku % 2
                    spar = kt % 3
                    ppar = kt % PRBN
                    wv = [P.w(f"dk{kpar}", kld[ku]), P.w(f"dq{qpar}", qld[cn])]
                    if kt >= 3:
                        wv.append(P.w("cS", exp_ms[kt - 3]))
                    elif psc_free[spar] is not None:
                        wv.extend(P.w(*x) for x in psc_free[spar])
                    first = True
                    off = qlo
                    while off < c.QCH:
                        cw = min(512 - off % 512, c.QCH - off)
                        P.mm(psc[spar][:, off:off + cw],
                             KT[kpar][:, j * 128:(j + 1) * 128],
                             QT[qpar][:, off:off + cw], True, True,
                             waits=(wv if first else ()),
                             inc=("cT" if off + cw >= c.QCH else None))
                        first = False
                        off += cw
                    sc_ms = P.v("cT")
                    wv = [P.w("cT", sc_ms)]
                    if kt >= PRBN:
                        wv += [P.w("cT", av_ms[kt - PRBN]),
                               P.w("cV", acc_ms[kt - PRBN])]
                    elif prev_accv[0] is not None:
                        wv.append(P.w("cV", prev_accv[0]))
                    P.act(PRB[ppar][:, qlo:c.QCH], psc[spar][:, qlo:c.QCH], AF.Exp,
                          bias=NSHIFT_AP, scale=scl, waits=wv, inc="cS")
                    exp_ms[kt] = P.v("cS")
                    rdy = ("cS", exp_ms[kt])
                    if diag:
                        P.ve_("tensor_mul", PRB[ppar][:, qlo:qlo + 128],
                              PRB[ppar][:, qlo:qlo + 128], tri_bf,
                              waits=[P.w("cS", exp_ms[kt])], inc="cV")
                        rdy = ("cV", P.v("cV"))
                    prb_rdy[kt] = rdy
                    wacc = [P.w(rdy[0], rdy[1])]
                    if kt == 0 and prev_trm[0] is not None:
                        wacc.append(P.w("cT", prev_trm[0]))
                    if j == 0:
                        P.ve_("tensor_copy", ACL[:, :], PRB[ppar][:, :],
                              waits=wacc, inc="cV")
                    else:
                        P.ve_("tensor_add", ACL[:, qlo:c.QCH], ACL[:, qlo:c.QCH],
                              PRB[ppar][:, qlo:c.QCH], waits=wacc, inc="cV")
                    if j == KSH - 1:
                        if st_i == 0:
                            P.ve_("tensor_copy", ACC[:, :], ACL[:, :], inc="cV")
                        else:
                            P.ve_("tensor_add", ACC[:, :], ACC[:, :], ACL[:, :],
                                  inc="cV")
                    acc_ms[kt] = P.v("cV")

                def emit_av(kt):
                    st_i, j, diag, qlo = meta(kt)
                    if j == 0 and st_i + 1 < nsup:
                        ksup_ids.append(load_ksup(hl, st_i + 1))
                    ku = ksup_ids[st_i]
                    kpar = ku % 2
                    ppar = kt % PRBN
                    rdy = prb_rdy[kt]
                    wv = [P.w(rdy[0], rdy[1]), P.w(f"dvv{kpar}", vld[ku])]
                    if kt == 0 and prev_norm[0] is not None:
                        wv.append(P.w("cV", prev_norm[0]))
                    first = True
                    off = qlo
                    while off < c.QCH:
                        cw = min(512 - off % 512, c.QCH - off)
                        bank = off // 512
                        jmax = min(KSH - 1, 4 * bank + 3)
                        stp = diag and (j == jmax)
                        lastmm = (off + cw >= c.QCH)
                        P.mm(pav[:, off:off + cw],
                             VT[kpar][:, j * 128:(j + 1) * 128],
                             PRB[ppar][:, off:off + cw],
                             (kt == 0), stp,
                             waits=(wv if first else ()),
                             inc=("cT" if lastmm else None))
                        first = False
                        off += cw
                    av_ms[kt] = P.v("cT")
                    if j == KSH - 1:
                        kuse[ku] = P.v("cT")
                        vuse[ku] = P.v("cT")

                BAT = 3
                done_av = 0
                for g0 in range(0, nkt, BAT):
                    for kt in range(g0, min(g0 + BAT, nkt)):
                        emit_scores(kt)
                    if g0 > 0:
                        for kt in range(g0 - BAT, g0):
                            emit_av(kt)
                            done_av = kt + 1
                for kt in range(done_av, nkt):
                    emit_av(kt)
                quse[cn] = P.v("cT")
                accv = P.v("cV")
                for qb in range(KSH):
                    P.tr(pdn[:, qb * 128:(qb + 1) * 128],
                         ACC[:, qb * 128:(qb + 1) * 128], ident_f,
                         waits=([P.w("cV", accv)] if qb == 0 else ()),
                         inc=("cT" if qb == KSH - 1 else None))
                trm = P.v("cT")
                prev_trm[0] = trm
                P.ve_("tensor_reduce",
                      DN[:, 0:KSH].rearrange("p (a b) -> p a b", a=KSH),
                      pdn[:, :].rearrange("p (a b) -> p a b", a=KSH),
                      AX.X, ALU.add, waits=[P.w("cT", trm)], inc="cV")
                P.ve_("reciprocal", RDP[:, 0:KSH], DN[:, 0:KSH], inc="cV")
                rdv = P.v("cV")
                P.tr(pdn[:, 0:128], RDP[:, :], ident_f,
                     waits=[P.w("cV", rdv)], inc="cT")
                t2 = P.v("cT")
                P.ve_("tensor_copy", RDT[:, :], pdn[:, 0:128],
                      waits=[P.w("cT", t2)], inc="cV")
                rdtv = P.v("cV")
                for qb in range(KSH):
                    P.mm(pdn[:, qb * 128:(qb + 1) * 128],
                         sel_bf[:, qb * 128:(qb + 1) * 128], RDT[:, :],
                         True, True,
                         waits=([P.w("cV", rdtv)] if qb == 0 else ()),
                         inc=("cT" if qb == KSH - 1 else None))
                bctm = P.v("cT")
                apar = cn % 2
                wv = [P.w("cT", bctm)]
                if cn >= 2:
                    wv.append(P.w(f"dct{apar}", ast_st[cn - 2]))
                P.ve_("tensor_copy", RDB[:, :], pdn[:, :], waits=wv, inc="cV")
                rdbv = P.v("cV")
                P.ve_("tensor_mul", AST[apar][:, :], pav[:, :], RDB[:, :], inc="cV")
                prev_norm[0] = P.v("cV")
                for s_ in range(3):
                    kl = nkt - 1 - ((nkt - 1 - s_) % 3)
                    psc_free[s_] = [("cS", exp_ms[kl])]
                psc_free[2].append(("cV", rdbv))
                prev_accv[0] = accv
                P.dma("gp", contrib.ap()[hl * 128:(hl + 1) * 128,
                                         ch * c.QCH:(ch + 1) * c.QCH],
                      AST[apar][:, :], f"dct{apar}",
                      waits=[P.w("cV", prev_norm[0])])
                ast_st[cn] = P.v(f"dct{apar}")
                qn[0] += 1
        ph2_end = dict(cT=P.v("cT"), cV=P.v("cV"), cS=P.v("cS"),
                       ct0=P.v("dct0"), ct1=P.v("dct1"), xn=P.v("dxn"))

    rg_xn = nc.gpsimd.alloc_register("rg_xn")
    for jg in range(4):
        P.emit("gp", lambda h, jg=jg: h.reg_load(rg_xn, CPAR[0:1, 16 + 4 * jg:17 + 4 * jg]),
               waits=([P.w("dxn", ph2_end["xn"]), P.w("dmisc", const_ready)]
                      if jg == 0 else ()))
        P.dma("gp", xnTo[:, :].rearrange("p (t x) -> p t x", t=HT)[:, 4 * jg:4 * jg + 4, :],
              bass.AP(xnT_scr, rg_xn, [[S, 128], [128 * S, 4], [1, c.TH]]),
              "dxl")
    xnl_h0 = P.v("dxl")

    if upto >= 3:
        P.gp_("memset", TOKS[:, :], 1.0, inc="cG")
        P.dma("gp", tok_in.ap(), TOKS[:, :], "dtok",
              waits=[P.w("cG"), P.w("dct0", ph2_end["ct0"]),
                     P.w("dct1", ph2_end["ct1"])])
        P.emit("gp", lambda h: h.collective_compute(
            "AllGather", ALU.bypass, replica_groups=[list(range(8))],
            ins=[contrib.ap()], outs=[gath.ap()]),
            waits=[P.w("dtok")], inc="cc")

    if upto < 4:
        _finish(P, nc, stk)
        return nc

    NTT, NHH, DHW = c.NTT, c.NHH, c.DHW
    with ExitStack() as ph4:
        def S4(name, shape, dt):
            return ph4.enter_context(nc.sbuf_tensor(name, shape, dt))
        yacc = S4("yacc", [128, NTT * H], F32)
        mT = S4("mT", [128, c.IT * TH], BF16)
        dbfh = S4("dbfh", [128, 2 * c.IT * DHW], BF16)
        LNU = S4("LNU", [128, 4 * H + 2 * DHW], F32)
        gstg = [LNU[:, 0:H], LNU[:, H:2 * H]]
        ustg = [LNU[:, 2 * H:3 * H], LNU[:, 3 * H:4 * H]]
        dstg = [LNU[:, 4 * H:4 * H + DHW], LNU[:, 4 * H + DHW:4 * H + 2 * DHW]]
        gbf = [S4(f"gbf{i}", [128, H], BF16) for i in range(2)]
        ubf = [S4(f"ubf{i}", [128, H], BF16) for i in range(2)]
        MGS = S4("MGS", [128, TH], BF16)
        COMB = S4("COMB", [128, NTT * c.E], F32)
        SGB = S4("SGB", [128, 2 * c.IT], F32)
        SUB = S4("SUB", [128, 2 * c.IT], F32)
        SDB = S4("SDB", [128, H], BF16)

        pg = ph4.enter_context(nc.psum_tensor("pg", [128, TH], F32))
        pu = ph4.enter_context(nc.psum_tensor("pu", [128, TH], F32))
        pd = [ph4.enter_context(nc.psum_tensor(f"pd{i}", [128, DHW], F32)) for i in range(2)]
        pr = ph4.enter_context(nc.psum_tensor("pr", [128, c.E], F32))

        entry4 = [P.w("cV", ph2_end["cV"]), P.w("cT", ph2_end["cT"]),
                  P.w("cS", ph2_end["cS"]),
                  P.w("dct0", ph2_end["ct0"]), P.w("dct1", ph2_end["ct1"]),
                  P.w("dxn", ph2_end["xn"])]

        P.dma("sy", SGB[:, :], sgb.ap(), "dsb", waits=entry4)
        P.dma("sy", SUB[:, :], sub.ap(), "dsb")
        sgb_v = P.v("dsb")

        gcast, ucast, dcast_all = {}, {}, {}
        guse, uuse = {}, {}
        gun = [0]
        g4n = [0]
        g4evac = {}
        ddn = [0]
        y_ms = {}
        y_prev = {}
        silu_last = [None]
        mupd_last = [None]
        dbfh_freeR = [None, None]
        dgc = [0]
        stt_ms = {}
        down_done = [None]
        ys_half0 = None
        prev_half_done = [None]
        sdb_ready = [None]

        P.dma("sy", LNU[:, 0:H], sdbp.ap(), "dsb", waits=entry4)
        sdl = P.v("dsb")
        P.ve_("tensor_copy", SDB[:, :], LNU[:, 0:H],
              waits=[P.w("dsb", sdl)], inc="cV")
        sdb_ready = [P.v("cV")]

        for hf in range(2):
            if hf == 0:
                xnl = xnl_h0
            else:
                for jg in range(4):
                    P.emit("gp", lambda h, jg=jg: h.reg_load(
                        rg_xn, CPAR[0:1, 32 + 4 * jg:33 + 4 * jg]),
                        waits=([P.w("cT"), P.w("cV")] if jg == 0 else ()))
                    P.dma("gp", xnTo[:, :].rearrange("p (t x) -> p t x", t=HT)[:, 4 * jg:4 * jg + 4, :],
                          bass.AP(xnT_scr, rg_xn, [[S, 128], [128 * S, 4], [1, c.TH]]),
                          "dxl")
                xnl = P.v("dxl")

            yi_w = [P.w("cV", P.v("cV"))]
            if hf == 1:
                yi_w.append(P.w("dys", ys_half0))
            for tt in range(NTT):
                P.dma("sy", yacc[:, tt * H:(tt + 1) * H],
                      xo.ap()[(hf * NTT + tt) * 128:(hf * NTT + tt + 1) * 128, :],
                      "dyi", waits=(yi_w if tt == 0 else ()))
            yinit = P.v("dyi")

            for tt in range(NTT):
                first = True
                wv = [P.w("dxl", xnl), P.w("cV", P.v("cV"))]
                for ht in range(HT):
                    P.mm(pr[:, 0:c.E],
                         xnTo[:, ht * TH + tt * 128:ht * TH + (tt + 1) * 128],
                         rw_bf[:, ht * c.E:(ht + 1) * c.E],
                         start=(ht == 0), stop=(ht == HT - 1),
                         waits=(wv if first else ()),
                         inc=("cT" if ht == HT - 1 else None))
                    first = False
                tm = P.v("cT")
                E = c.E
                Z = LNW2[:, 0:E]
                P.ve_("reduce_max", LNW2[:, 32:33], pr[:, 0:E], AX.X,
                      None, None, True, waits=[P.w("cT", tm)], inc="cV")
                P.ve_("tensor_scalar_add", Z, pr[:, 0:E], LNW2[:, 32:33], inc="cV")
                P.ve_("tensor_single_scalar", LNW2[:, 8:8 + E], Z, 0.0,
                      ALU.is_equal, inc="cV")
                P.ve_("scalar_tensor_tensor", LNW2[:, 16:16 + E],
                      LNW2[:, 8:8 + E], -1e30, Z, ALU.mult, ALU.add, inc="cV")
                P.ve_("reduce_max", LNW2[:, 33:34], LNW2[:, 16:16 + E], AX.X,
                      None, None, True, inc="cV")
                P.ve_("tensor_scalar_mul", LNW2[:, 34:35], LNW2[:, 33:34],
                      -1.0, inc="cV")
                vv = P.v("cV")
                P.act(LNW2[:, 35:36], LNW2[:, 34:35], AF.Exp,
                      waits=[P.w("cV", vv)], inc="cS")
                P.act(LNW2[:, 24:24 + E], Z, AF.Exp, inc="cS")
                sv = P.v("cS")
                P.ve_("tensor_scalar_add", LNW2[:, 36:37], LNW2[:, 35:36], 1.0,
                      waits=[P.w("cS", sv)], inc="cV")
                P.ve_("reciprocal", LNW2[:, 37:38], LNW2[:, 36:37], inc="cV")
                P.ve_("tensor_single_scalar", LNW2[:, 40:40 + E], Z,
                      LNW2[:, 34:35], ALU.is_ge, inc="cV")
                P.ve_("tensor_mul", LNW2[:, 48:48 + E], LNW2[:, 24:24 + E],
                      LNW2[:, 40:40 + E], inc="cV")
                P.ve_("tensor_scalar_mul", COMB[:, tt * E:(tt + 1) * E],
                      LNW2[:, 48:48 + E], LNW2[:, 37:38], inc="cV")
            comb_ready = P.v("cV")

            for e2 in range(NE):
                shared = (e2 >= c.E)
                silu_ms = {}
                mupd_ms = {}
                def emit_down_stream(hh, it, extra=()):
                    dn_ = ddn[0]
                    dpar = dn_ % 2
                    wv = list(extra)
                    if dn_ >= 2:
                        wv.append(P.w("cG", dcast_all[dn_ - 2]))
                    else:
                        wv.append(P.w("cG"))
                    reg = hh % 2
                    r0 = (e2 * c.IT + it) * 128
                    P.dma("sc", dstg[dpar][:, :],
                          dwt.ap()[r0:r0 + 128, hh * DHW:(hh + 1) * DHW],
                          f"dd{dpar}", waits=wv)
                    dv = P.v(f"dd{dpar}")
                    wv2 = [P.w(f"dd{dpar}", dv)]
                    if dbfh_freeR[reg] is not None and it == 0:
                        wv2.append(P.w("cT", dbfh_freeR[reg]))
                    P.gp_("tensor_copy",
                          dbfh[:, (reg * c.IT + it) * DHW:(reg * c.IT + it + 1) * DHW],
                          dstg[dpar][:, :], waits=wv2, inc="cG")
                    dcast_all[dn_] = P.v("cG")
                    ddn[0] += 1
                    return P.v("cG")

                dcast_hh = {}
                for it in range(c.IT):
                    n = gun[0]
                    par = n % 2
                    tid = e2 * c.IT + it
                    for which, stg, bfb, castd, used, sq, dmaq, castq in (
                            ("g", gstg, gbf, gcast, guse, "dg", "sy", "sc"),
                            ("u", ustg, ubf, ucast, uuse, "du", "sy", "gp")):
                        eap = (gwt if which == "g" else uwt).ap()[tid]
                        csem = "cS" if castq == "sc" else "cG"
                        wv = []
                        if n == 0 and hf == 0:
                            wv = list(entry4)
                        if e2 == 0 and it == 0:
                            wv.append(P.w("cV"))
                        if n >= 2:
                            wv.append(P.w(csem, castd[n - 2]))
                        P.dma(dmaq,
                              stg[par][:, :].rearrange("p (s i) -> p s i", i=128),
                              eap, f"{sq}{par}", waits=wv)
                        dv = P.v(f"{sq}{par}")
                        wv2 = [P.w(f"{sq}{par}", dv)]
                        if n >= 2:
                            wv2.append(P.w("cT", used[n - 2]))
                        if castq == "sc":
                            P.act(bfb[par][:, :], stg[par][:, :], AF.Copy,
                                  waits=wv2, inc="cS")
                        else:
                            P.gp_("tensor_copy", bfb[par][:, :], stg[par][:, :],
                                  waits=wv2, inc="cG")
                        castd[n] = P.v(csem)
                    nmm = (TH + 511) // 512
                    first = True
                    wv = [P.w("cS", gcast[n]), P.w("dxl", xnl)]
                    if it >= 1:
                        wv.append(P.w("cS", silu_ms[it - 1]))
                    elif silu_last[0] is not None:
                        wv.append(P.w("cS", silu_last[0]))
                    for ht in range(HT):
                        for m in range(nmm):
                            cw = min(512, TH - m * 512)
                            P.mm(pg[:, m * 512:m * 512 + cw],
                                 gbf[par][:, ht * 128:(ht + 1) * 128],
                                 xnTo[:, ht * TH + m * 512:ht * TH + m * 512 + cw],
                                 start=(ht == 0), stop=(ht == HT - 1),
                                 waits=(wv if first else ()),
                                 inc=("cT" if (ht == HT - 1 and m == nmm - 1) else None))
                            first = False
                    gmm = P.v("cT")
                    guse[n] = gmm
                    first = True
                    wv = [P.w("cG", ucast[n])]
                    if it >= 1:
                        wv.append(P.w("cV", mupd_ms[it - 1]))
                    elif mupd_last[0] is not None:
                        wv.append(P.w("cV", mupd_last[0]))
                    for ht in range(HT):
                        for m in range(nmm):
                            cw = min(512, TH - m * 512)
                            P.mm(pu[:, m * 512:m * 512 + cw],
                                 ubf[par][:, ht * 128:(ht + 1) * 128],
                                 xnTo[:, ht * TH + m * 512:ht * TH + m * 512 + cw],
                                 start=(ht == 0), stop=(ht == HT - 1),
                                 waits=(wv if first else ()),
                                 inc=("cT" if (ht == HT - 1 and m == nmm - 1) else None))
                            first = False
                    umm = P.v("cT")
                    uuse[n] = umm
                    gb = SGB[:, (e2 - c.E) * c.IT + it:(e2 - c.E) * c.IT + it + 1] if shared else 0.0
                    wv = [P.w("cT", gmm)]
                    if it >= 1:
                        wv.append(P.w("cV", mupd_ms[it - 1]))
                    elif mupd_last[0] is not None:
                        wv.append(P.w("cV", mupd_last[0]))
                    if shared:
                        wv.append(P.w("dsb"))
                    P.act(MGS[:, :], pg[:, :], AF.Silu, bias=gb, waits=wv, inc="cS")
                    silu_ms[it] = P.v("cS")
                    ub = SUB[:, (e2 - c.E) * c.IT + it:(e2 - c.E) * c.IT + it + 1] if shared else 0.0
                    P.ve_("scalar_tensor_tensor", mT[:, it * TH:(it + 1) * TH],
                          pu[:, :], ub, MGS[:, :], ALU.add, ALU.mult,
                          waits=[P.w("cT", umm), P.w("cS", silu_ms[it]),
                                 P.w("cT", P.v("cT"))], inc="cV")
                    mupd_ms[it] = P.v("cV")
                    ex = []
                    if e2 == 0 and it == 0:
                        ex.append(P.w("cV"))
                        if hf == 0:
                            ex += list(entry4)
                    dcast_hh[(0, it)] = emit_down_stream(0, it, extra=ex)
                    gun[0] += 1
                silu_last[0] = silu_ms[c.IT - 1]
                mupd_last[0] = mupd_ms[c.IT - 1]
                m_ready = P.v("cV")

                for hh in range(NHH):
                    if hh + 1 < NHH:
                        for it in range(c.IT):
                            dcast_hh[(hh + 1, it)] = emit_down_stream(hh + 1, it)
                    last_e = (e2 == NE - 1)
                    for tt in range(NTT):
                        pk = tt * NHH + hh
                        gidx = dgc[0]
                        ppar = gidx % 2
                        wv = [P.w("cV", m_ready),
                              P.w("cG", dcast_hh[(hh, c.IT - 1)]),
                              P.w("dyi", yinit)]
                        if stt_ms.get(gidx - 2) is not None:
                            wv.append(P.w("cV", stt_ms[gidx - 2]))
                        first = True
                        nm = DHW // 512
                        reg = hh % 2
                        for it in range(c.IT):
                            for m in range(nm):
                                lastmm = (not last_e) and it == c.IT - 1 and m == nm - 1
                                P.mm(pd[ppar][:, m * 512:(m + 1) * 512],
                                     mT[:, it * TH + tt * 128:it * TH + (tt + 1) * 128],
                                     dbfh[:, (reg * c.IT + it) * DHW + m * 512:
                                          (reg * c.IT + it) * DHW + (m + 1) * 512],
                                     start=(it == 0),
                                     stop=(it == c.IT - 1 and not last_e),
                                     waits=(wv if first else ()),
                                     inc=("cT" if lastmm else None))
                                first = False
                        if last_e:
                            for m in range(nm):
                                P.mm(pd[ppar][:, m * 512:(m + 1) * 512], row0_bf,
                                     SDB[:, hh * DHW + m * 512:hh * DHW + (m + 1) * 512],
                                     False, True,
                                     waits=([P.w("cV", sdb_ready[0])] if (tt == 0 and hh == 0 and hf == 0) else ()),
                                     inc=("cT" if m == nm - 1 else None))
                        dmm = P.v("cT")
                        combs = 1.0 if shared else COMB[:, tt * c.E + e2:tt * c.E + e2 + 1]
                        ysl = yacc[:, tt * H + hh * DHW:tt * H + (hh + 1) * DHW]
                        P.ve_("scalar_tensor_tensor", ysl, pd[ppar][:, :], combs,
                              ysl, ALU.mult, ALU.add,
                              waits=[P.w("cT", dmm)], inc="cV")
                        y_ms[(hf, e2, pk)] = P.v("cV")
                        stt_ms[gidx] = P.v("cV")
                        dgc[0] += 1
                    dbfh_freeR[hh % 2] = P.v("cT")
                down_done[0] = P.v("cT")
            y_prev = {pk: y_ms.get((hf, NE - 1, pk)) for pk in range(NTT * NHH)}
            for tt in range(NTT):
                wv = [P.w("cV", y_ms[(hf, NE - 1, tt * NHH + hhx)])
                      for hhx in range(NHH)]
                P.dma("sy", y_scr.ap()[(hf * NTT + tt) * 128:(hf * NTT + tt + 1) * 128, :],
                      yacc[:, tt * H:(tt + 1) * H], "dys", waits=wv)
            if hf == 0:
                ys_half0 = P.v("dys")
            prev_half_done[0] = P.v("cV")
        ph4_end = dict(cT=P.v("cT"), cV=P.v("cV"), cS=P.v("cS"), cG=P.v("cG"),
                       ys=P.v("dys"))

    if upto < 5:
        _finish(P, nc, stk)
        return nc

    midstk.close()

    with ExitStack() as ph5:
        def S5(name, shape, dt):
            return ph5.enter_context(nc.sbuf_tensor(name, shape, dt))
        G = S5("G", [128, HT * c.TOWN], BF16)
        wo_bf = S5("wo_bf", [128, HT * H], BF16)
        wostg = [S5(f"wostg{i}", [128, H], F32) for i in range(2)]
        yld = [S5(f"yld{i}", [128, H], F32) for i in range(2)]
        ostg = [S5(f"ostg{i}", [128, H], F32) for i in range(2)]

        po = [ph5.enter_context(nc.psum_tensor(f"po{i}", [128, min(H, 2048)], F32)) for i in range(2)]

        entry5 = [P.w("cV", ph4_end["cV"]), P.w("cT", ph4_end["cT"]),
                  P.w("cG", ph4_end["cG"]), P.w("dys", ph4_end["ys"])]

        P.emit("gp", lambda h: h.collective_compute(
            "AllReduce", ALU.add, replica_groups=[list(range(8))],
            ins=[tok_in.ap()], outs=[tok_out.ap()]),
            waits=[P.w("cc")], inc="cc")
        rg = nc.gpsimd.alloc_register("rg_gath")
        for jg in range(8):
            P.emit("gp", lambda h, jg=jg: h.reg_load(rg, CPAR[0:1, 2 * jg:2 * jg + 1]),
                   waits=(([P.w("dmisc", const_ready), P.w("cc", 2)] + entry5)
                          if jg == 0 else ()))
            P.dma("gp", G[:, :].rearrange("p (t x) -> p t x", t=HT)[:, 2 * jg:2 * jg + 2, :],
                  bass.AP(gath, rg, [[S, 128], [128 * S, 2], [1, c.TOWN]]), "dgr")
        g_done = P.v("dgr")

        wo_cms = []
        for j in range(HT):
            par = j % 2
            wv = list(entry5) if j < 2 else [P.w("cV", wo_cms[j - 2])]
            P.dma("sy", wostg[par][:, :], woT.ap()[j * 128:(j + 1) * 128, :],
                  f"dwo{par}", waits=wv)
            P.ve_("tensor_copy", wo_bf[:, j * H:(j + 1) * H], wostg[par][:, :],
                  waits=[P.w(f"dwo{par}")], inc="cV")
            wo_cms.append(P.v("cV"))
        wo_done = P.v("cV")

        fin_ms, st_ms = {}, {}
        for tt in range(c.TOWN // 128):
            ppar = tt % 2
            wv = [P.w("dys", ph4_end["ys"])]
            if tt >= 2:
                wv.append(P.w("cV", fin_ms[tt - 2]))
            P.dma("sc", yld[ppar][:, :], y_scr.ap()[tt * 128:(tt + 1) * 128, :],
                  f"dyl{ppar}", waits=wv)
            ylv = P.v(f"dyl{ppar}")
            first = True
            wv = [P.w("dgr", g_done), P.w("cV", wo_done)]
            if tt >= 2:
                wv.append(P.w("cV", fin_ms[tt - 2]))
            for j in range(HT):
                for m in range(H // 512):
                    P.mm(po[ppar][:, m * 512:(m + 1) * 512],
                         G[:, j * c.TOWN + tt * 128:j * c.TOWN + (tt + 1) * 128],
                         wo_bf[:, j * H + m * 512:j * H + (m + 1) * 512],
                         start=(j == 0), stop=(j == HT - 1),
                         waits=(wv if first else ()),
                         inc=("cT" if (j == HT - 1 and m == H // 512 - 1) else None))
                    first = False
            omm = P.v("cT")
            wv = [P.w("cT", omm), P.w(f"dyl{ppar}", ylv)]
            if tt >= 2:
                wv.append(P.w(f"dout{ppar}", st_ms[tt - 2]))
            P.ve_("tensor_add", ostg[ppar][:, :], po[ppar][:, :], yld[ppar][:, :],
                  waits=wv, inc="cV")
            fin_ms[tt] = P.v("cV")
            P.dma("sy", out_t.ap()[tt * 128:(tt + 1) * 128, :], ostg[ppar][:, :],
                  f"dout{ppar}", waits=[P.w("cV", fin_ms[tt])])
            st_ms[tt] = P.v(f"dout{ppar}")
        P.emit("sy", lambda h: h.nop(),
               waits=[P.w("dout0", P.v("dout0")), P.w("dout1", P.v("dout1"))])

    _finish(P, nc, stk)
    return nc


def _finish(P, nc, stk):
    with nc.Block() as block:
        @block.sync
        def _(h):
            for fn in P.plan["sy"]:
                fn(h)

        @block.tensor
        def _(h):
            for fn in P.plan["te"]:
                fn(h)

        @block.vector
        def _(h):
            for fn in P.plan["ve"]:
                fn(h)

        @block.scalar
        def _(h):
            for fn in P.plan["sc"]:
                fn(h)

        @block.gpsimd
        def _(h):
            for fn in P.plan["gp"]:
                fn(h)
    stk.close()



def shard_inputs(cfg, A):
    c = cfg
    x = np.ascontiguousarray(np.asarray(A["L0_attn_hidden_states"], np.float32))
    w_qkv = np.asarray(A["L0_attn_w_qkv"], dtype=np.float32)
    w_o = np.asarray(A["L0_attn_w_o"], dtype=np.float32)
    woT = np.ascontiguousarray(w_o.T)
    rwT = np.ascontiguousarray(np.asarray(A["L0_ffn_router_w"], np.float32).T)
    def tile_gu(w_eHI, w_sh):
        E_, H_, I_ = w_eHI.shape
        HT_, IT_ = H_ // 128, I_ // 128
        a = w_eHI.reshape(E_, HT_, 128, IT_, 128).transpose(0, 3, 2, 1, 4)
        a = a.reshape(E_ * IT_, 128, HT_, 128)
        b = w_sh.reshape(HT_, 128, 2 * IT_, 128).transpose(2, 1, 0, 3)
        return np.ascontiguousarray(np.concatenate([a, b], axis=0), dtype=np.float32)

    gwt = tile_gu(np.asarray(A["L0_ffn_gate_w"], np.float32),
                  np.asarray(A["L0_ffn_s_gate_w"], np.float32))
    uwt = tile_gu(np.asarray(A["L0_ffn_up_w"], np.float32),
                  np.asarray(A["L0_ffn_s_up_w"], np.float32))
    dwa = np.asarray(A["L0_ffn_down_w"], np.float32)
    dwt = np.ascontiguousarray(np.concatenate(
        [dwa.reshape(-1, c.H), np.asarray(A["L0_ffn_s_down_w"], np.float32)], axis=0))
    sgb = np.ascontiguousarray(
        np.asarray(A["L0_ffn_s_gate_b"], np.float32).reshape(2 * c.IT, 128).T)
    sub = np.ascontiguousarray(
        np.asarray(A["L0_ffn_s_up_b"], np.float32).reshape(2 * c.IT, 128).T)
    sdbp = np.zeros((128, c.H), np.float32)
    sdbp[0, :] = np.asarray(A["L0_ffn_s_down_b"], np.float32)

    HPC, HD = c.HPC, c.HD
    maps = []
    for core in range(8):
        b = core // 4
        qpos = core % 4
        heads = list(range(HPC * qpos, HPC * (qpos + 1)))
        rows = []
        for kind in range(3):
            for h in heads:
                r0 = kind * c.NH * HD + h * HD
                rows.append(w_qkv[r0:r0 + HD, :])
        wqkvT = np.ascontiguousarray(np.concatenate(rows, axis=0).T)
        xbat = np.ascontiguousarray(x[b])
        xown = np.ascontiguousarray(x[b, qpos * c.TOWN:(qpos + 1) * c.TOWN, :])
        cpar = np.zeros((1, 48), np.int32)
        q0 = qpos * c.TOWN
        for j in range(c.HT):
            rank = 4 * b + j // HPC
            row = rank * HPC * 128 + (j % HPC) * 128
            cpar[0, j] = row * c.S + q0
            cpar[0, 16 + j] = (128 * j) * c.S + q0
            cpar[0, 32 + j] = (128 * j) * c.S + q0 + c.TH
        maps.append(dict(xb=xbat, xo=xown, wqkvT=wqkvT, woT=woT, rwT=rwT,
                         gwt=gwt, uwt=uwt, dwt=dwt,
                         sgb=sgb, sub=sub, sdbp=sdbp, cpar=cpar))
    return maps


def kernel(**inputs):
    cfg = Cfg()
    nc = build_program(cfg)
    maps = shard_inputs(cfg, inputs)
    res = run_bass_kernel_spmd(nc, maps, core_ids=list(range(8)))
    parts = [res.results[i]["out"] for i in range(8)]
    out = np.empty((cfg.B, cfg.S, cfg.H), np.float32)
    for core in range(8):
        b, qpos = core // 4, core % 4
        out[b, qpos * cfg.TOWN:(qpos + 1) * cfg.TOWN, :] = parts[core]
    return out


# version 16 (final)
# speedup vs baseline: 1.0440x; 1.0202x over previous
import numpy as np
import ml_dtypes
import concourse.bass as bass
import concourse.mybir as mybir
from concourse.bass_utils import run_bass_kernel_spmd
from contextlib import ExitStack

F32 = mybir.dt.float32
BF16 = mybir.dt.bfloat16
I32 = mybir.dt.int32
AF = mybir.ActivationFunctionType
ALU = mybir.AluOpType
AX = mybir.AxisListType

ENGS = ("sy", "te", "ve", "sc", "gp")
SHIFT = 8.0


class Cfg:
    def __init__(self, B=2, S=8192, H=2048, NH=16, HD=128, E=8, I=1408, SI=2816):
        assert SI == 2 * I
        self.B, self.S, self.H, self.NH, self.HD = B, S, H, NH, HD
        self.E, self.I, self.SI = E, I, SI
        self.HT = H // 128
        self.IT = I // 128
        self.HPC = NH * B // 8
        self.TOWN = S // 4
        self.TH = self.TOWN // 2
        self.SEG = S // 4
        self.W = min(1024, self.SEG)
        self.NW = self.SEG // self.W
        self.QCH = min(1024, S)
        self.NCH = S // self.QCH
        self.KSH = self.QCH // 128
        self.QKO = 2 * self.HPC
        self.VD = self.HPC * 128
        self.NHH = max(1, H // 512)
        self.DHW = H // self.NHH
        self.NTT = self.TH // 128
        assert self.QCH % 128 == 0 and self.W % 128 == 0 and self.TH % 128 == 0
        assert self.DHW % 512 == 0


class Plan:
    def __init__(self, nc):
        self.nc = nc
        self.plan = {e: [] for e in ENGS}
        self.cnt = {}
        self.sems = {}

    def sem(self, name):
        self.sems[name] = self.nc.alloc_semaphore(name)
        self.cnt[name] = 0

    def v(self, name):
        return self.cnt[name]

    def w(self, name, val=None):
        return (self.sems[name], self.cnt[name] if val is None else val)

    def emit(self, eng, fn, waits=(), inc=None, inck=1):
        if inc is not None:
            self.cnt[inc] += inck
        ih = None if inc is None else (self.sems[inc], inck)
        ws = tuple((a, b) for (a, b) in waits)

        def run(h, fn=fn, ws=ws, ih=ih):
            for sm, vv in ws:
                h.wait_ge(sm, vv)
            ins = fn(h)
            if ih is not None:
                ins.then_inc(ih[0], ih[1])
        self.plan[eng].append(run)
        return None if inc is None else self.cnt[inc]

    def mm(self, o, l, r, start, stop, waits=(), inc=None):
        return self.emit("te", lambda h, o=o, l=l, r=r, a=start, b=stop:
                         h.matmul(o, l, r, start=a, stop=b), waits, inc)

    def tr(self, o, i, ident, waits=(), inc=None):
        return self.emit("te", lambda h, o=o, i=i, d=ident: h.transpose(o, i, d),
                         waits, inc)

    def act(self, o, i, f, bias=0.0, scale=1.0, waits=(), inc=None):
        sw = [self.w("cS")]
        return self.emit("sc", lambda h, o=o, i=i, f=f, b=bias, s=scale:
                         h.activation(o, i, f, bias=b, scale=s),
                         sw + list(waits), inc)

    def ve_(self, fname, *args, waits=(), inc=None):
        sw = [self.w("cV")]
        return self.emit("ve", lambda h, fn=fname, aa=args: getattr(h, fn)(*aa),
                         sw + list(waits), inc)

    def gp_(self, fname, *args, waits=(), inc=None):
        sw = [self.w("cG")]
        return self.emit("gp", lambda h, fn=fname, aa=args: getattr(h, fn)(*aa),
                         sw + list(waits), inc)

    def dma(self, eng, o, i, sem, waits=()):
        return self.emit(eng, lambda h, o=o, i=i: h.dma_start(out=o, in_=i),
                         waits, sem, 16)


def build_program(cfg, upto=5):
    c = cfg
    nc = bass.Bass("TRN2", target_bir_lowering=False)
    S, H, HT, HPC, W, TH = c.S, c.H, c.HT, c.HPC, c.W, c.TH
    NE = c.E + 2

    din = {}

    def inp(name, shape, dt=F32):
        din[name] = nc.dram_tensor(name, list(shape), dt, kind="ExternalInput")
        return din[name]

    xb = inp("xb", [S, H])
    xo = inp("xo", [c.TOWN, H])
    wqkvT = inp("wqkvT", [H, 3 * HPC * 128])
    woT = inp("woT", [H, H])
    rwT = inp("rwT", [H, c.E])
    gwt = inp("gwt", [NE * c.IT, 128, HT, 128])
    uwt = inp("uwt", [NE * c.IT, 128, HT, 128])
    dwt = inp("dwt", [NE * c.IT * 128, H])
    sgb = inp("sgb", [128, 2 * c.IT])
    sub = inp("sub", [128, 2 * c.IT])
    sdbp = inp("sdbp", [128, H])
    cpar = inp("cpar", [1, 48], I32)

    out_t = nc.dram_tensor("out", [c.TOWN, H], F32, kind="ExternalOutput")

    qT_scr = nc.dram_tensor("qT_scr", [HPC * 128, S], BF16, kind="Internal")
    kT_scr = nc.dram_tensor("kT_scr", [HPC * 128, S], BF16, kind="Internal")
    v_scr = nc.dram_tensor("v_scr", [S, c.VD], BF16, kind="Internal")
    contrib = nc.dram_tensor("contrib", [HPC * 128, S], BF16, kind="Internal")
    gath = nc.dram_tensor("gath", [8 * HPC * 128, S], BF16, kind="Internal")
    tok_in = nc.dram_tensor("tok_in", [1, 4], F32, kind="Internal")
    tok_out = nc.dram_tensor("tok_out", [1, 4], F32, kind="Internal")
    y_scr = nc.dram_tensor("y_scr", [c.TOWN, H], F32, kind="Internal")
    xnT_scr = nc.dram_tensor("xnT_scr", [HT * 128, S], BF16, kind="Internal")

    ident_bf_d = nc.inline_tensor(np.eye(128, dtype=ml_dtypes.bfloat16), "ident_bf")
    ident_f_d = nc.inline_tensor(np.eye(128, dtype=np.float32), "ident_f")
    tri_np = np.triu(np.ones((128, 128), dtype=np.float32)).astype(ml_dtypes.bfloat16)
    tri_bf_d = nc.inline_tensor(tri_np, "tri_bf")
    selm = np.zeros((128, c.KSH * 128), dtype=ml_dtypes.bfloat16)
    for b in range(c.KSH):
        selm[b, b * 128:(b + 1) * 128] = 1
    sel_bf_d = nc.inline_tensor(selm, "sel_bf")
    row0m = np.zeros((128, 128), dtype=ml_dtypes.bfloat16)
    row0m[0, :] = 1
    row0_bf_d = nc.inline_tensor(row0m, "row0_bf")

    P = Plan(nc)
    for name in ["dx0", "dx1", "dqk0", "dqk1", "dv0", "dv1", "dwq0", "dwq1", "dmisc",
                 "dq0", "dq1", "dk0", "dk1", "dk2", "dk3", "dvv0", "dvv1", "dvv2", "dvv3", "dct0", "dct1",
                 "dtok", "dg0", "dg1", "du0", "du1", "dd0", "dd1", "dyi",
                 "dys", "dsb", "dgr", "dwo0", "dwo1", "dyl0", "dyl1", "dout0", "dout1",
                 "dxn", "dxl",
                 "cT", "cV", "cS", "cG", "cc"]:
        P.sem(name)

    stk = ExitStack()

    def SBp(name, shape, dt):
        return stk.enter_context(nc.sbuf_tensor(name, shape, dt))

    rw_bf = SBp("rw_bf", [128, HT * c.E], BF16)
    RDP = SBp("RDP", [128, 128], F32)
    CPAR = SBp("CPAR", [1, 48], I32)
    LNW = SBp("LNW", [128, 80], F32)
    LNW2 = SBp("LNW2", [128, 64], F32)
    TOKS = SBp("TOKS", [1, 4], F32)
    IDB = SBp("IDB", [128, 128], BF16)
    IDF = SBp("IDF", [128, 128], F32)
    TRI = SBp("TRI", [128, 128], BF16)
    SEL = SBp("SEL", [128, c.KSH * 128], BF16)
    ROW0 = SBp("ROW0", [128, 128], BF16)
    ident_bf, ident_f = IDB[:, :], IDF[:, :]
    tri_bf, sel_bf, row0_bf = TRI[:, :], SEL[:, :], ROW0[:, :]

    CONSTS = SBp("CONSTS", [128, 4], F32)

    P.dma("sy", CPAR[:, :], cpar.ap(), "dmisc")
    for dst, srcd in ((IDB, ident_bf_d), (IDF, ident_f_d), (TRI, tri_bf_d),
                      (SEL, sel_bf_d), (ROW0, row0_bf_d)):
        P.dma("sy", dst[:, :], srcd.ap(), "dmisc")
    const_ready = P.v("dmisc")
    P.ve_("memset", RDP[:, :], 0.0, inc="cV")
    P.ve_("memset", CONSTS[:, 0:1], 1e-5, inc="cV")
    P.ve_("memset", CONSTS[:, 1:2], -SHIFT, inc="cV")
    cmem = P.v("cV")
    EPS_AP = CONSTS[:, 0:1]
    NSHIFT_AP = CONSTS[:, 1:2]
    for eng in ENGS:
        P.emit(eng, lambda h: h.nop(),
               waits=[P.w("dmisc", const_ready), P.w("cV", cmem)])

    def emit_ln_tile(xsrc_ap, xstg_t, xnstg_t, xw_sem, par, dma_waits, act_waits=()):
        L = LNW[:, 40 * par:40 * par + 40]
        P.dma("sy", xstg_t[:, :], xsrc_ap, xw_sem, waits=dma_waits)
        dv = P.v(xw_sem)
        nbk = (H + 511) // 512
        for b in range(nbk):
            cw = min(512, H - b * 512)
            P.ve_("bn_stats", L[:, 6 * b:6 * b + 6], xstg_t[:, b * 512:b * 512 + cw],
                  waits=([P.w(xw_sem, dv)] if b == 0 else ()), inc="cV")
        P.ve_("bn_aggr", L[:, 6 * nbk:6 * nbk + 2], L[:, 0:6 * nbk], inc="cV")
        va = P.v("cV")
        mcol = 6 * nbk
        P.act(L[:, mcol + 2:mcol + 3], L[:, mcol + 1:mcol + 2], AF.Sqrt,
              bias=EPS_AP, scale=1.0, waits=[P.w("cV", va)], inc="cS")
        sq = P.v("cS")
        P.ve_("reciprocal", L[:, mcol + 3:mcol + 4], L[:, mcol + 2:mcol + 3],
              waits=[P.w("cS", sq)], inc="cV")
        P.ve_("tensor_scalar", xnstg_t[:, :], xstg_t[:, :], L[:, mcol:mcol + 1],
              L[:, mcol + 3:mcol + 4], ALU.subtract, ALU.mult,
              waits=list(act_waits), inc="cV")
        return ("cV", P.v("cV"))

    midstk = ExitStack()
    xnTo = midstk.enter_context(nc.sbuf_tensor("xnTo", [128, HT * c.TH], BF16))

    with ExitStack() as ph1:
        def S1(name, shape, dt):
            return ph1.enter_context(nc.sbuf_tensor(name, shape, dt))
        xstg = [S1(f"xstg{i}", [128, H], F32) for i in range(2)]
        xnstg = [S1(f"xnstg{i}", [128, H], BF16) for i in range(2)]
        xnT = S1("xnT", [128, HT * W], BF16)
        qkstg = [S1(f"qkstg{i}", [128, W], BF16) for i in range(2)]
        vstg = [S1(f"vstg{i}", [128, c.VD], BF16) for i in range(2)]
        OW = 3 * HPC * 128
        wqstg = [S1(f"wqstg{i}", [128, OW], F32) for i in range(2)]
        wq_bf = S1("wq_bf", [128, HT * OW], BF16)

        pqk = [ph1.enter_context(nc.psum_tensor(f"pqk{i}", [128, W], F32)) for i in range(2)]
        pv = [ph1.enter_context(nc.psum_tensor(f"pv{i}", [128, c.VD], F32)) for i in range(2)]
        ptr = [ph1.enter_context(nc.psum_tensor(f"ptr{i}", [128, 8 * 128], BF16)) for i in range(2)]

        wq_cast = []
        for ht in range(HT):
            par = ht % 2
            wv = [P.w("cV", wq_cast[ht - 2])] if ht >= 2 else []
            P.dma("sc", wqstg[par][:, :], wqkvT.ap()[ht * 128:(ht + 1) * 128, :],
                  f"dwq{par}", waits=wv)
            P.ve_("tensor_copy", wq_bf[:, ht * OW:(ht + 1) * OW], wqstg[par][:, :],
                  waits=[P.w(f"dwq{par}")], inc="cV")
            wq_cast.append(P.v("cV"))
        P.dma("sc", wqstg[0][:, 0:HT * c.E].rearrange("p (t e) -> p t e", t=HT),
              rwT.ap().rearrange("(t p) e -> p t e", p=128), "dwq0",
              waits=[P.w("cV", wq_cast[-2])])
        P.ve_("tensor_copy", rw_bf[:, :], wqstg[0][:, 0:HT * c.E],
              waits=[P.w("dwq0")], inc="cV")
        wq_ready = P.v("cV")

        ln_hist = {}
        tpose_hist = {}
        evac_hist = {}
        trn = [0]
        qk_evac, qk_st = {}, {}
        v_evac, v_st = {}, {}
        nttw = W // 128
        vn = [0]
        xnst = {}

        for g in range(4):
            for wi in range(c.NW):
                wfirst = trn[0]
                for i in range(nttw):
                    ti = (g * c.NW + wi) * nttw + i
                    par = ti % 2
                    row0 = g * c.SEG + wi * W + i * 128
                    wv = [P.w(*ln_hist[ti - 2])] if ti >= 2 else []
                    aw = [P.w("cT", tpose_hist[ti - 2])] if ti >= 2 else []
                    ln_ms = emit_ln_tile(xb.ap()[row0:row0 + 128, :],
                                         xstg[par], xnstg[par], f"dx{par}", par,
                                         wv, aw)
                    ln_hist[ti] = ln_ms
                    ngrp = (HT + 7) // 8
                    for grp in range(ngrp):
                        nj = min(8, HT - grp * 8)
                        u = trn[0]
                        tp = u % 2
                        wv2 = [P.w(*ln_ms)]
                        if u >= 2:
                            wv2.append(P.w("cV", evac_hist[u - 2]))
                        for j in range(nj):
                            jj = grp * 8 + j
                            P.tr(ptr[tp][:, j * 128:(j + 1) * 128],
                                 xnstg[par][:, jj * 128:(jj + 1) * 128], ident_bf,
                                 waits=(wv2 if j == 0 else ()),
                                 inc=("cT" if j == nj - 1 else None))
                        tpose_hist[ti] = P.v("cT")
                        oap = xnT[:, :].rearrange("p (t x) -> p t x", t=HT)[
                            :, grp * 8:grp * 8 + nj, i * 128:i * 128 + 128]
                        iap = ptr[tp][:, 0:nj * 128].rearrange("p (t x) -> p t x", t=nj)
                        ew = [P.w("cT", P.v("cT"))]
                        if u == wfirst:
                            wiq = g * c.NW + wi
                            if wiq >= 1:
                                ew.append(P.w("dxn", xnst[wiq - 1]))
                        P.ve_("tensor_copy", oap, iap, waits=ew, inc="cV")
                        evac_hist[u] = P.v("cV")
                        trn[0] += 1
                xn_ready = P.v("cV")
                col = g * c.SEG + wi * W
                for ht in range(HT):
                    P.dma("sy", xnT_scr.ap()[ht * 128:(ht + 1) * 128, col:col + W],
                          xnT[:, ht * W:(ht + 1) * W], "dxn",
                          waits=([P.w("cV", xn_ready)] if ht == 0 else ()))
                xnst[g * c.NW + wi] = P.v("dxn")

                nmm = (W + 511) // 512
                for ot in range(c.QKO):
                    pb = ot % 2
                    gk = (g * c.NW + wi) * c.QKO + ot
                    wv = [P.w("cV", xn_ready), P.w("cV", wq_ready)]
                    if gk >= 2:
                        wv.append(P.w("cV", qk_evac[gk - 2]))
                    first = True
                    for ht in range(HT):
                        for m in range(nmm):
                            cw = min(512, W - m * 512)
                            P.mm(pqk[pb][:, m * 512:m * 512 + cw],
                                 wq_bf[:, ht * OW + ot * 128:ht * OW + (ot + 1) * 128],
                                 xnT[:, ht * W + m * 512:ht * W + m * 512 + cw],
                                 start=(ht == 0), stop=(ht == HT - 1),
                                 waits=(wv if first else ()),
                                 inc=("cT" if (ht == HT - 1 and m == nmm - 1) else None))
                            first = False
                    tm = P.v("cT")
                    stw = [P.w("cT", tm)]
                    if gk >= 2:
                        stw.append(P.w(f"dqk{pb}", qk_st[gk - 2]))
                    P.ve_("tensor_copy", qkstg[pb][:, :], pqk[pb][:, :],
                          waits=stw, inc="cV")
                    qk_evac[gk] = P.v("cV")
                    scr = qT_scr if ot < HPC else kT_scr
                    rb = (ot % HPC) * 128
                    col = g * c.SEG + wi * W
                    P.dma("sy", scr.ap()[rb:rb + 128, col:col + W], qkstg[pb][:, :],
                          f"dqk{pb}", waits=[P.w("cV", qk_evac[gk])])
                    qk_st[gk] = P.v(f"dqk{pb}")

                for i in range(nttw):
                    vi = vn[0]
                    par = vi % 2
                    row0 = g * c.SEG + wi * W + i * 128
                    wv = [P.w("cV", xn_ready)]
                    if vi >= 2:
                        wv.append(P.w("cV", v_evac[vi - 2]))
                    first = True
                    for ht in range(HT):
                        P.mm(pv[par][:, :],
                             xnT[:, ht * W + i * 128:ht * W + (i + 1) * 128],
                             wq_bf[:, ht * OW + 2 * HPC * 128:(ht + 1) * OW],
                             start=(ht == 0), stop=(ht == HT - 1),
                             waits=(wv if first else ()),
                             inc=("cT" if ht == HT - 1 else None))
                        first = False
                    tm = P.v("cT")
                    stw = [P.w("cT", tm)]
                    if vi >= 2:
                        stw.append(P.w(f"dv{par}", v_st[vi - 2]))
                    P.ve_("tensor_copy", vstg[par][:, :], pv[par][:, :], waits=stw,
                          inc="cV")
                    v_evac[vi] = P.v("cV")
                    P.dma("sy", v_scr.ap()[row0:row0 + 128, :], vstg[par][:, :],
                          f"dv{par}", waits=[P.w("cV", v_evac[vi])])
                    v_st[vi] = P.v(f"dv{par}")
                    vn[0] += 1

        ph1_end = dict(cT=P.v("cT"), cV=P.v("cV"), cS=P.v("cS"),
                       q0=P.v("dqk0"), q1=P.v("dqk1"),
                       v0=P.v("dv0"), v1=P.v("dv1"))

    if upto < 2:
        _finish(P, nc, stk)
        return nc

    scl = 1.0 / float(np.sqrt(c.HD))
    KSH = c.KSH
    with ExitStack() as ph2:
        def S2(name, shape, dt):
            return ph2.enter_context(nc.sbuf_tensor(name, shape, dt))
        QT = [S2(f"QT{i}", [128, c.QCH], BF16) for i in range(2)]
        KT = [S2(f"KT{i}", [128, c.QCH], BF16) for i in range(2)]
        VT = [S2(f"VT{i}", [128, c.QCH], BF16) for i in range(2)]
        PRBN = 7
        PRB = [S2(f"PRB{i}", [128, c.QCH], BF16) for i in range(PRBN)]
        ACL = S2("ACL", [128, c.QCH], BF16)
        ACC = S2("ACC", [128, c.QCH], F32)
        RDB = S2("RDB", [128, c.QCH], BF16)
        AST = [S2(f"AST{i}", [128, c.QCH], BF16) for i in range(2)]
        RDT = S2("RDT", [128, 128], BF16)
        DN = S2("DN", [128, KSH], F32)

        pav = ph2.enter_context(nc.psum_tensor("pav", [128, c.QCH], F32))
        psc = [ph2.enter_context(nc.psum_tensor(f"psc{i}", [128, c.QCH], F32)) for i in range(3)]
        pdn = psc[2]

        entry2 = [P.w("cV", ph1_end["cV"]), P.w("cT", ph1_end["cT"]),
                  P.w("dqk0", ph1_end["q0"]), P.w("dqk1", ph1_end["q1"]),
                  P.w("dv0", ph1_end["v0"]), P.w("dv1", ph1_end["v1"])]

        kld, vld, kuse, vuse = {}, {}, {}, {}
        qld, quse = {}, {}
        ast_st = {}
        prev_norm = [None]
        prev_trm = [None]
        psc_free = [None, None, None]
        prev_accv = [None]
        kst = [0]
        qn = [0]

        def load_ksup(hl, st_idx):
            n = kst[0]
            par = n % 2
            wv = list(entry2) if n == 0 else []
            if n >= 2:
                wv.append(P.w("cT", kuse[n - 2]))
            P.dma("gp", KT[par][:, :],
                  kT_scr.ap()[hl * 128:(hl + 1) * 128,
                              st_idx * c.QCH:(st_idx + 1) * c.QCH],
                  f"dk{par}", waits=wv)
            kld[n] = P.v(f"dk{par}")
            wv2 = list(entry2) if n == 0 else []
            if n >= 2:
                wv2.append(P.w("cT", vuse[n - 2]))
            vsl = v_scr.ap()[st_idx * c.QCH:(st_idx + 1) * c.QCH,
                             hl * 128:(hl + 1) * 128]
            P.dma("sy", VT[par][:, :].rearrange("p (s d) -> p s d", s=KSH),
                  vsl.rearrange("(s p) d -> p s d", p=128), f"dvv{par}", waits=wv2)
            vld[n] = P.v(f"dvv{par}")
            kst[0] += 1
            return n

        for hl in range(HPC):
            for ch in range(c.NCH):
                cn = qn[0]
                qpar = cn % 2
                wv = list(entry2) if cn == 0 else []
                if cn >= 2:
                    wv.append(P.w("cT", quse[cn - 2]))
                P.dma("gp", QT[qpar][:, :],
                      qT_scr.ap()[hl * 128:(hl + 1) * 128,
                                  ch * c.QCH:(ch + 1) * c.QCH],
                      f"dq{qpar}", waits=wv)
                qld[cn] = P.v(f"dq{qpar}")
                nsup = ch + 1
                nkt = nsup * KSH
                ksup_ids = [load_ksup(hl, 0)]
                exp_ms, av_ms, acc_ms = {}, {}, {}
                prb_rdy = {}

                def meta(kt):
                    st_i, j = divmod(kt, KSH)
                    diag = (st_i == nsup - 1)
                    return st_i, j, diag, (j * 128 if diag else 0)

                def emit_scores(kt):
                    st_i, j, diag, qlo = meta(kt)
                    ku = ksup_ids[st_i]
                    kpar = ku % 2
                    spar = kt % 3
                    ppar = kt % PRBN
                    wv = [P.w(f"dk{kpar}", kld[ku]), P.w(f"dq{qpar}", qld[cn])]
                    if kt >= 3:
                        wv.append(P.w("cS", exp_ms[kt - 3]))
                    elif psc_free[spar] is not None:
                        wv.extend(P.w(*x) for x in psc_free[spar])
                    first = True
                    off = qlo
                    while off < c.QCH:
                        cw = min(512 - off % 512, c.QCH - off)
                        P.mm(psc[spar][:, off:off + cw],
                             KT[kpar][:, j * 128:(j + 1) * 128],
                             QT[qpar][:, off:off + cw], True, True,
                             waits=(wv if first else ()),
                             inc=("cT" if off + cw >= c.QCH else None))
                        first = False
                        off += cw
                    sc_ms = P.v("cT")
                    wv = [P.w("cT", sc_ms)]
                    if kt >= PRBN:
                        wv += [P.w("cT", av_ms[kt - PRBN]),
                               P.w("cV", acc_ms[kt - PRBN])]
                    elif prev_accv[0] is not None:
                        wv.append(P.w("cV", prev_accv[0]))
                    P.act(PRB[ppar][:, qlo:c.QCH], psc[spar][:, qlo:c.QCH], AF.Exp,
                          bias=NSHIFT_AP, scale=scl, waits=wv, inc="cS")
                    exp_ms[kt] = P.v("cS")
                    rdy = ("cS", exp_ms[kt])
                    if diag:
                        P.ve_("tensor_mul", PRB[ppar][:, qlo:qlo + 128],
                              PRB[ppar][:, qlo:qlo + 128], tri_bf,
                              waits=[P.w("cS", exp_ms[kt])], inc="cV")
                        rdy = ("cV", P.v("cV"))
                    prb_rdy[kt] = rdy
                    wacc = [P.w(rdy[0], rdy[1])]
                    if kt == 0 and prev_trm[0] is not None:
                        wacc.append(P.w("cT", prev_trm[0]))
                    if j == 0:
                        P.ve_("tensor_copy", ACL[:, :], PRB[ppar][:, :],
                              waits=wacc, inc="cV")
                    else:
                        P.ve_("tensor_add", ACL[:, qlo:c.QCH], ACL[:, qlo:c.QCH],
                              PRB[ppar][:, qlo:c.QCH], waits=wacc, inc="cV")
                    if j == KSH - 1:
                        if st_i == 0:
                            P.ve_("tensor_copy", ACC[:, :], ACL[:, :], inc="cV")
                        else:
                            P.ve_("tensor_add", ACC[:, :], ACC[:, :], ACL[:, :],
                                  inc="cV")
                    acc_ms[kt] = P.v("cV")

                def emit_av(kt):
                    st_i, j, diag, qlo = meta(kt)
                    if j == 0 and st_i + 1 < nsup:
                        ksup_ids.append(load_ksup(hl, st_i + 1))
                    ku = ksup_ids[st_i]
                    kpar = ku % 2
                    ppar = kt % PRBN
                    rdy = prb_rdy[kt]
                    wv = [P.w(rdy[0], rdy[1]), P.w(f"dvv{kpar}", vld[ku])]
                    if kt == 0 and prev_norm[0] is not None:
                        wv.append(P.w("cV", prev_norm[0]))
                    first = True
                    off = qlo
                    while off < c.QCH:
                        cw = min(512 - off % 512, c.QCH - off)
                        bank = off // 512
                        jmax = min(KSH - 1, 4 * bank + 3)
                        stp = diag and (j == jmax)
                        lastmm = (off + cw >= c.QCH)
                        P.mm(pav[:, off:off + cw],
                             VT[kpar][:, j * 128:(j + 1) * 128],
                             PRB[ppar][:, off:off + cw],
                             (kt == 0), stp,
                             waits=(wv if first else ()),
                             inc=("cT" if lastmm else None))
                        first = False
                        off += cw
                    av_ms[kt] = P.v("cT")
                    if j == KSH - 1:
                        kuse[ku] = P.v("cT")
                        vuse[ku] = P.v("cT")

                BAT = 3
                done_av = 0
                for g0 in range(0, nkt, BAT):
                    for kt in range(g0, min(g0 + BAT, nkt)):
                        emit_scores(kt)
                    if g0 > 0:
                        for kt in range(g0 - BAT, g0):
                            emit_av(kt)
                            done_av = kt + 1
                for kt in range(done_av, nkt):
                    emit_av(kt)
                quse[cn] = P.v("cT")
                accv = P.v("cV")
                for qb in range(KSH):
                    P.tr(pdn[:, qb * 128:(qb + 1) * 128],
                         ACC[:, qb * 128:(qb + 1) * 128], ident_f,
                         waits=([P.w("cV", accv)] if qb == 0 else ()),
                         inc=("cT" if qb == KSH - 1 else None))
                trm = P.v("cT")
                prev_trm[0] = trm
                P.ve_("tensor_reduce",
                      DN[:, 0:KSH].rearrange("p (a b) -> p a b", a=KSH),
                      pdn[:, :].rearrange("p (a b) -> p a b", a=KSH),
                      AX.X, ALU.add, waits=[P.w("cT", trm)], inc="cV")
                P.ve_("reciprocal", RDP[:, 0:KSH], DN[:, 0:KSH], inc="cV")
                rdv = P.v("cV")
                P.tr(pdn[:, 0:128], RDP[:, :], ident_f,
                     waits=[P.w("cV", rdv)], inc="cT")
                t2 = P.v("cT")
                P.ve_("tensor_copy", RDT[:, :], pdn[:, 0:128],
                      waits=[P.w("cT", t2)], inc="cV")
                rdtv = P.v("cV")
                for qb in range(KSH):
                    P.mm(pdn[:, qb * 128:(qb + 1) * 128],
                         sel_bf[:, qb * 128:(qb + 1) * 128], RDT[:, :],
                         True, True,
                         waits=([P.w("cV", rdtv)] if qb == 0 else ()),
                         inc=("cT" if qb == KSH - 1 else None))
                bctm = P.v("cT")
                apar = cn % 2
                wv = [P.w("cT", bctm)]
                if cn >= 2:
                    wv.append(P.w(f"dct{apar}", ast_st[cn - 2]))
                P.ve_("tensor_copy", RDB[:, :], pdn[:, :], waits=wv, inc="cV")
                rdbv = P.v("cV")
                P.ve_("tensor_mul", AST[apar][:, :], pav[:, :], RDB[:, :], inc="cV")
                prev_norm[0] = P.v("cV")
                for s_ in range(3):
                    kl = nkt - 1 - ((nkt - 1 - s_) % 3)
                    psc_free[s_] = [("cS", exp_ms[kl])]
                psc_free[2].append(("cV", rdbv))
                prev_accv[0] = accv
                P.dma("gp", contrib.ap()[hl * 128:(hl + 1) * 128,
                                         ch * c.QCH:(ch + 1) * c.QCH],
                      AST[apar][:, :], f"dct{apar}",
                      waits=[P.w("cV", prev_norm[0])])
                ast_st[cn] = P.v(f"dct{apar}")
                qn[0] += 1
        ph2_end = dict(cT=P.v("cT"), cV=P.v("cV"), cS=P.v("cS"),
                       ct0=P.v("dct0"), ct1=P.v("dct1"), xn=P.v("dxn"))

    rg_xn = nc.gpsimd.alloc_register("rg_xn")
    for jg in range(4):
        P.emit("gp", lambda h, jg=jg: h.reg_load(rg_xn, CPAR[0:1, 16 + 4 * jg:17 + 4 * jg]),
               waits=([P.w("dxn", ph2_end["xn"]), P.w("dmisc", const_ready)]
                      if jg == 0 else ()))
        P.dma("gp", xnTo[:, :].rearrange("p (t x) -> p t x", t=HT)[:, 4 * jg:4 * jg + 4, :],
              bass.AP(xnT_scr, rg_xn, [[S, 128], [128 * S, 4], [1, c.TH]]),
              "dxl")
    xnl_h0 = P.v("dxl")

    if upto >= 3:
        P.gp_("memset", TOKS[:, :], 1.0, inc="cG")
        P.dma("gp", tok_in.ap(), TOKS[:, :], "dtok",
              waits=[P.w("cG"), P.w("dct0", ph2_end["ct0"]),
                     P.w("dct1", ph2_end["ct1"])])
        P.emit("gp", lambda h: h.collective_compute(
            "AllGather", ALU.bypass, replica_groups=[list(range(8))],
            ins=[contrib.ap()], outs=[gath.ap()]),
            waits=[P.w("dtok")], inc="cc")

    if upto < 4:
        _finish(P, nc, stk)
        return nc

    NTT, NHH, DHW = c.NTT, c.NHH, c.DHW
    with ExitStack() as ph4:
        def S4(name, shape, dt):
            return ph4.enter_context(nc.sbuf_tensor(name, shape, dt))
        yacc = S4("yacc", [128, NTT * H], F32)
        mT = S4("mT", [128, c.IT * TH], BF16)
        dbfh = S4("dbfh", [128, 2 * c.IT * DHW], BF16)
        LNU = S4("LNU", [128, 4 * H + 2 * DHW], F32)
        gstg = [LNU[:, 0:H], LNU[:, H:2 * H]]
        ustg = [LNU[:, 2 * H:3 * H], LNU[:, 3 * H:4 * H]]
        dstg = [LNU[:, 4 * H:4 * H + DHW], LNU[:, 4 * H + DHW:4 * H + 2 * DHW]]
        gbf = [S4(f"gbf{i}", [128, H], BF16) for i in range(2)]
        ubf = [S4(f"ubf{i}", [128, H], BF16) for i in range(2)]
        MGS = S4("MGS", [128, TH], BF16)
        COMB = S4("COMB", [128, NTT * c.E], F32)
        SGB = S4("SGB", [128, 2 * c.IT], F32)
        SUB = S4("SUB", [128, 2 * c.IT], F32)
        SDB = S4("SDB", [128, H], BF16)

        pg = ph4.enter_context(nc.psum_tensor("pg", [128, TH], F32))
        pu = ph4.enter_context(nc.psum_tensor("pu", [128, TH], F32))
        pd = [ph4.enter_context(nc.psum_tensor(f"pd{i}", [128, DHW], F32)) for i in range(2)]
        pr = ph4.enter_context(nc.psum_tensor("pr", [128, c.E], F32))

        entry4 = [P.w("cV", ph2_end["cV"]), P.w("cT", ph2_end["cT"]),
                  P.w("cS", ph2_end["cS"]),
                  P.w("dct0", ph2_end["ct0"]), P.w("dct1", ph2_end["ct1"]),
                  P.w("dxn", ph2_end["xn"])]

        P.dma("sy", SGB[:, :], sgb.ap(), "dsb", waits=entry4)
        P.dma("sy", SUB[:, :], sub.ap(), "dsb")
        sgb_v = P.v("dsb")

        gcast, ucast, dcast_all = {}, {}, {}
        guse, uuse = {}, {}
        gun = [0]
        g4n = [0]
        g4evac = {}
        ddn = [0]
        y_ms = {}
        y_prev = {}
        silu_last = [None]
        mupd_last = [None]
        dbfh_freeR = [None, None]
        dgc = [0]
        stt_ms = {}
        down_done = [None]
        ys_half0 = None
        prev_half_done = [None]
        sdb_ready = [None]
        xn_done = [None]

        P.dma("sy", LNU[:, 0:H], sdbp.ap(), "dsb", waits=entry4)
        sdl = P.v("dsb")
        P.ve_("tensor_copy", SDB[:, :], LNU[:, 0:H],
              waits=[P.w("dsb", sdl)], inc="cV")
        sdb_ready = [P.v("cV")]

        for hf in range(2):
            if hf == 0:
                xnl = xnl_h0
            else:
                for jg in range(4):
                    P.emit("gp", lambda h, jg=jg: h.reg_load(
                        rg_xn, CPAR[0:1, 32 + 4 * jg:33 + 4 * jg]),
                        waits=([P.w("cT", xn_done[0])] if jg == 0 else ()))
                    P.dma("gp", xnTo[:, :].rearrange("p (t x) -> p t x", t=HT)[:, 4 * jg:4 * jg + 4, :],
                          bass.AP(xnT_scr, rg_xn, [[S, 128], [128 * S, 4], [1, c.TH]]),
                          "dxl")
                xnl = P.v("dxl")

            yi_w = [P.w("cV", P.v("cV"))]
            if hf == 1:
                yi_w.append(P.w("dys", ys_half0))
            for tt in range(NTT):
                P.dma("sy", yacc[:, tt * H:(tt + 1) * H],
                      xo.ap()[(hf * NTT + tt) * 128:(hf * NTT + tt + 1) * 128, :],
                      "dyi", waits=(yi_w if tt == 0 else ()))
            yinit = P.v("dyi")

            for tt in range(NTT):
                first = True
                wv = [P.w("dxl", xnl), P.w("cV", P.v("cV"))]
                for ht in range(HT):
                    P.mm(pr[:, 0:c.E],
                         xnTo[:, ht * TH + tt * 128:ht * TH + (tt + 1) * 128],
                         rw_bf[:, ht * c.E:(ht + 1) * c.E],
                         start=(ht == 0), stop=(ht == HT - 1),
                         waits=(wv if first else ()),
                         inc=("cT" if ht == HT - 1 else None))
                    first = False
                tm = P.v("cT")
                E = c.E
                Z = LNW2[:, 0:E]
                P.ve_("reduce_max", LNW2[:, 32:33], pr[:, 0:E], AX.X,
                      None, None, True, waits=[P.w("cT", tm)], inc="cV")
                P.ve_("tensor_scalar_add", Z, pr[:, 0:E], LNW2[:, 32:33], inc="cV")
                P.ve_("tensor_single_scalar", LNW2[:, 8:8 + E], Z, 0.0,
                      ALU.is_equal, inc="cV")
                P.ve_("scalar_tensor_tensor", LNW2[:, 16:16 + E],
                      LNW2[:, 8:8 + E], -1e30, Z, ALU.mult, ALU.add, inc="cV")
                P.ve_("reduce_max", LNW2[:, 33:34], LNW2[:, 16:16 + E], AX.X,
                      None, None, True, inc="cV")
                P.ve_("tensor_scalar_mul", LNW2[:, 34:35], LNW2[:, 33:34],
                      -1.0, inc="cV")
                vv = P.v("cV")
                P.act(LNW2[:, 35:36], LNW2[:, 34:35], AF.Exp,
                      waits=[P.w("cV", vv)], inc="cS")
                P.act(LNW2[:, 24:24 + E], Z, AF.Exp, inc="cS")
                sv = P.v("cS")
                P.ve_("tensor_scalar_add", LNW2[:, 36:37], LNW2[:, 35:36], 1.0,
                      waits=[P.w("cS", sv)], inc="cV")
                P.ve_("reciprocal", LNW2[:, 37:38], LNW2[:, 36:37], inc="cV")
                P.ve_("tensor_single_scalar", LNW2[:, 40:40 + E], Z,
                      LNW2[:, 34:35], ALU.is_ge, inc="cV")
                P.ve_("tensor_mul", LNW2[:, 48:48 + E], LNW2[:, 24:24 + E],
                      LNW2[:, 40:40 + E], inc="cV")
                P.ve_("tensor_scalar_mul", COMB[:, tt * E:(tt + 1) * E],
                      LNW2[:, 48:48 + E], LNW2[:, 37:38], inc="cV")
            comb_ready = P.v("cV")

            for e2 in range(NE):
                shared = (e2 >= c.E)
                silu_ms = {}
                mupd_ms = {}
                def emit_down_stream(hh, it, extra=()):
                    dn_ = ddn[0]
                    dpar = dn_ % 2
                    wv = list(extra)
                    if dn_ >= 2:
                        wv.append(P.w("cG", dcast_all[dn_ - 2]))
                    else:
                        wv.append(P.w("cG"))
                    reg = hh % 2
                    r0 = (e2 * c.IT + it) * 128
                    P.dma("sc", dstg[dpar][:, :],
                          dwt.ap()[r0:r0 + 128, hh * DHW:(hh + 1) * DHW],
                          f"dd{dpar}", waits=wv)
                    dv = P.v(f"dd{dpar}")
                    wv2 = [P.w(f"dd{dpar}", dv)]
                    if dbfh_freeR[reg] is not None and it == 0:
                        wv2.append(P.w("cT", dbfh_freeR[reg]))
                    P.gp_("tensor_copy",
                          dbfh[:, (reg * c.IT + it) * DHW:(reg * c.IT + it + 1) * DHW],
                          dstg[dpar][:, :], waits=wv2, inc="cG")
                    dcast_all[dn_] = P.v("cG")
                    ddn[0] += 1
                    return P.v("cG")

                dcast_hh = {}
                for it in range(c.IT):
                    n = gun[0]
                    par = n % 2
                    tid = e2 * c.IT + it
                    for which, stg, bfb, castd, used, sq, dmaq, castq in (
                            ("g", gstg, gbf, gcast, guse, "dg", "sy", "sc"),
                            ("u", ustg, ubf, ucast, uuse, "du", "sy", "gp")):
                        eap = (gwt if which == "g" else uwt).ap()[tid]
                        csem = "cS" if castq == "sc" else "cG"
                        wv = []
                        if n == 0 and hf == 0:
                            wv = list(entry4)
                        if e2 == 0 and it == 0:
                            wv.append(P.w("cV"))
                        if n >= 2:
                            wv.append(P.w(csem, castd[n - 2]))
                        P.dma(dmaq,
                              stg[par][:, :].rearrange("p (s i) -> p s i", i=128),
                              eap, f"{sq}{par}", waits=wv)
                        dv = P.v(f"{sq}{par}")
                        wv2 = [P.w(f"{sq}{par}", dv)]
                        if n >= 2:
                            wv2.append(P.w("cT", used[n - 2]))
                        if castq == "sc":
                            P.act(bfb[par][:, :], stg[par][:, :], AF.Copy,
                                  waits=wv2, inc="cS")
                        else:
                            P.gp_("tensor_copy", bfb[par][:, :], stg[par][:, :],
                                  waits=wv2, inc="cG")
                        castd[n] = P.v(csem)
                    nmm = (TH + 511) // 512
                    first = True
                    wv = [P.w("cS", gcast[n]), P.w("dxl", xnl)]
                    if it >= 1:
                        wv.append(P.w("cS", silu_ms[it - 1]))
                    elif silu_last[0] is not None:
                        wv.append(P.w("cS", silu_last[0]))
                    for ht in range(HT):
                        for m in range(nmm):
                            cw = min(512, TH - m * 512)
                            P.mm(pg[:, m * 512:m * 512 + cw],
                                 gbf[par][:, ht * 128:(ht + 1) * 128],
                                 xnTo[:, ht * TH + m * 512:ht * TH + m * 512 + cw],
                                 start=(ht == 0), stop=(ht == HT - 1),
                                 waits=(wv if first else ()),
                                 inc=("cT" if (ht == HT - 1 and m == nmm - 1) else None))
                            first = False
                    gmm = P.v("cT")
                    guse[n] = gmm
                    first = True
                    wv = [P.w("cG", ucast[n])]
                    if it >= 1:
                        wv.append(P.w("cV", mupd_ms[it - 1]))
                    elif mupd_last[0] is not None:
                        wv.append(P.w("cV", mupd_last[0]))
                    for ht in range(HT):
                        for m in range(nmm):
                            cw = min(512, TH - m * 512)
                            P.mm(pu[:, m * 512:m * 512 + cw],
                                 ubf[par][:, ht * 128:(ht + 1) * 128],
                                 xnTo[:, ht * TH + m * 512:ht * TH + m * 512 + cw],
                                 start=(ht == 0), stop=(ht == HT - 1),
                                 waits=(wv if first else ()),
                                 inc=("cT" if (ht == HT - 1 and m == nmm - 1) else None))
                            first = False
                    umm = P.v("cT")
                    uuse[n] = umm
                    gb = SGB[:, (e2 - c.E) * c.IT + it:(e2 - c.E) * c.IT + it + 1] if shared else 0.0
                    wv = [P.w("cT", gmm)]
                    if it >= 1:
                        wv.append(P.w("cV", mupd_ms[it - 1]))
                    elif mupd_last[0] is not None:
                        wv.append(P.w("cV", mupd_last[0]))
                    if shared:
                        wv.append(P.w("dsb"))
                    P.act(MGS[:, :], pg[:, :], AF.Silu, bias=gb, waits=wv, inc="cS")
                    silu_ms[it] = P.v("cS")
                    ub = SUB[:, (e2 - c.E) * c.IT + it:(e2 - c.E) * c.IT + it + 1] if shared else 0.0
                    P.ve_("scalar_tensor_tensor", mT[:, it * TH:(it + 1) * TH],
                          pu[:, :], ub, MGS[:, :], ALU.add, ALU.mult,
                          waits=[P.w("cT", umm), P.w("cS", silu_ms[it]),
                                 P.w("cT", P.v("cT"))], inc="cV")
                    mupd_ms[it] = P.v("cV")
                    ex = []
                    if e2 == 0 and it == 0:
                        ex.append(P.w("cV"))
                        if hf == 0:
                            ex += list(entry4)
                    dcast_hh[(0, it)] = emit_down_stream(0, it, extra=ex)
                    gun[0] += 1
                silu_last[0] = silu_ms[c.IT - 1]
                mupd_last[0] = mupd_ms[c.IT - 1]
                m_ready = P.v("cV")
                if e2 == NE - 1:
                    xn_done[0] = P.v("cT")

                for hh in range(NHH):
                    if hh + 1 < NHH:
                        for it in range(c.IT):
                            dcast_hh[(hh + 1, it)] = emit_down_stream(hh + 1, it)
                    last_e = (e2 == NE - 1)
                    for tt in range(NTT):
                        pk = tt * NHH + hh
                        gidx = dgc[0]
                        ppar = gidx % 2
                        wv = [P.w("cV", m_ready),
                              P.w("cG", dcast_hh[(hh, c.IT - 1)]),
                              P.w("dyi", yinit)]
                        if stt_ms.get(gidx - 2) is not None:
                            wv.append(P.w("cV", stt_ms[gidx - 2]))
                        first = True
                        nm = DHW // 512
                        reg = hh % 2
                        for it in range(c.IT):
                            for m in range(nm):
                                lastmm = (not last_e) and it == c.IT - 1 and m == nm - 1
                                P.mm(pd[ppar][:, m * 512:(m + 1) * 512],
                                     mT[:, it * TH + tt * 128:it * TH + (tt + 1) * 128],
                                     dbfh[:, (reg * c.IT + it) * DHW + m * 512:
                                          (reg * c.IT + it) * DHW + (m + 1) * 512],
                                     start=(it == 0),
                                     stop=(it == c.IT - 1 and not last_e),
                                     waits=(wv if first else ()),
                                     inc=("cT" if lastmm else None))
                                first = False
                        if last_e:
                            for m in range(nm):
                                P.mm(pd[ppar][:, m * 512:(m + 1) * 512], row0_bf,
                                     SDB[:, hh * DHW + m * 512:hh * DHW + (m + 1) * 512],
                                     False, True,
                                     waits=([P.w("cV", sdb_ready[0])] if (tt == 0 and hh == 0 and hf == 0) else ()),
                                     inc=("cT" if m == nm - 1 else None))
                        dmm = P.v("cT")
                        combs = 1.0 if shared else COMB[:, tt * c.E + e2:tt * c.E + e2 + 1]
                        ysl = yacc[:, tt * H + hh * DHW:tt * H + (hh + 1) * DHW]
                        P.ve_("scalar_tensor_tensor", ysl, pd[ppar][:, :], combs,
                              ysl, ALU.mult, ALU.add,
                              waits=[P.w("cT", dmm)], inc="cV")
                        y_ms[(hf, e2, pk)] = P.v("cV")
                        stt_ms[gidx] = P.v("cV")
                        dgc[0] += 1
                    dbfh_freeR[hh % 2] = P.v("cT")
                down_done[0] = P.v("cT")
            y_prev = {pk: y_ms.get((hf, NE - 1, pk)) for pk in range(NTT * NHH)}
            for tt in range(NTT):
                wv = [P.w("cV", y_ms[(hf, NE - 1, tt * NHH + hhx)])
                      for hhx in range(NHH)]
                P.dma("sy", y_scr.ap()[(hf * NTT + tt) * 128:(hf * NTT + tt + 1) * 128, :],
                      yacc[:, tt * H:(tt + 1) * H], "dys", waits=wv)
            if hf == 0:
                ys_half0 = P.v("dys")
            prev_half_done[0] = P.v("cV")
        ph4_end = dict(cT=P.v("cT"), cV=P.v("cV"), cS=P.v("cS"), cG=P.v("cG"),
                       ys=P.v("dys"))

    if upto < 5:
        _finish(P, nc, stk)
        return nc

    midstk.close()

    with ExitStack() as ph5:
        def S5(name, shape, dt):
            return ph5.enter_context(nc.sbuf_tensor(name, shape, dt))
        G = S5("G", [128, HT * c.TOWN], BF16)
        wo_bf = S5("wo_bf", [128, HT * H], BF16)
        wostg = [S5(f"wostg{i}", [128, H], F32) for i in range(2)]
        yld = [S5(f"yld{i}", [128, H], F32) for i in range(2)]
        ostg = [S5(f"ostg{i}", [128, H], F32) for i in range(2)]

        po = [ph5.enter_context(nc.psum_tensor(f"po{i}", [128, min(H, 2048)], F32)) for i in range(2)]

        entry5 = [P.w("cV", ph4_end["cV"]), P.w("cT", ph4_end["cT"]),
                  P.w("cG", ph4_end["cG"]), P.w("dys", ph4_end["ys"])]

        P.emit("gp", lambda h: h.collective_compute(
            "AllReduce", ALU.add, replica_groups=[list(range(8))],
            ins=[tok_in.ap()], outs=[tok_out.ap()]),
            waits=[P.w("cc")], inc="cc")
        rg = nc.gpsimd.alloc_register("rg_gath")
        for jg in range(8):
            P.emit("gp", lambda h, jg=jg: h.reg_load(rg, CPAR[0:1, 2 * jg:2 * jg + 1]),
                   waits=(([P.w("dmisc", const_ready), P.w("cc", 2)] + entry5)
                          if jg == 0 else ()))
            P.dma("gp", G[:, :].rearrange("p (t x) -> p t x", t=HT)[:, 2 * jg:2 * jg + 2, :],
                  bass.AP(gath, rg, [[S, 128], [128 * S, 2], [1, c.TOWN]]), "dgr")
        g_done = P.v("dgr")

        wo_cms = []
        for j in range(HT):
            par = j % 2
            wv = list(entry5) if j < 2 else [P.w("cV", wo_cms[j - 2])]
            P.dma("sy", wostg[par][:, :], woT.ap()[j * 128:(j + 1) * 128, :],
                  f"dwo{par}", waits=wv)
            P.ve_("tensor_copy", wo_bf[:, j * H:(j + 1) * H], wostg[par][:, :],
                  waits=[P.w(f"dwo{par}")], inc="cV")
            wo_cms.append(P.v("cV"))
        wo_done = P.v("cV")

        fin_ms, st_ms = {}, {}
        for tt in range(c.TOWN // 128):
            ppar = tt % 2
            wv = [P.w("dys", ph4_end["ys"])]
            if tt >= 2:
                wv.append(P.w("cV", fin_ms[tt - 2]))
            P.dma("sc", yld[ppar][:, :], y_scr.ap()[tt * 128:(tt + 1) * 128, :],
                  f"dyl{ppar}", waits=wv)
            ylv = P.v(f"dyl{ppar}")
            first = True
            wv = [P.w("dgr", g_done), P.w("cV", wo_done)]
            if tt >= 2:
                wv.append(P.w("cV", fin_ms[tt - 2]))
            for j in range(HT):
                for m in range(H // 512):
                    P.mm(po[ppar][:, m * 512:(m + 1) * 512],
                         G[:, j * c.TOWN + tt * 128:j * c.TOWN + (tt + 1) * 128],
                         wo_bf[:, j * H + m * 512:j * H + (m + 1) * 512],
                         start=(j == 0), stop=(j == HT - 1),
                         waits=(wv if first else ()),
                         inc=("cT" if (j == HT - 1 and m == H // 512 - 1) else None))
                    first = False
            omm = P.v("cT")
            wv = [P.w("cT", omm), P.w(f"dyl{ppar}", ylv)]
            if tt >= 2:
                wv.append(P.w(f"dout{ppar}", st_ms[tt - 2]))
            P.ve_("tensor_add", ostg[ppar][:, :], po[ppar][:, :], yld[ppar][:, :],
                  waits=wv, inc="cV")
            fin_ms[tt] = P.v("cV")
            P.dma("sy", out_t.ap()[tt * 128:(tt + 1) * 128, :], ostg[ppar][:, :],
                  f"dout{ppar}", waits=[P.w("cV", fin_ms[tt])])
            st_ms[tt] = P.v(f"dout{ppar}")
        P.emit("sy", lambda h: h.nop(),
               waits=[P.w("dout0", P.v("dout0")), P.w("dout1", P.v("dout1"))])

    _finish(P, nc, stk)
    return nc


def _finish(P, nc, stk):
    with nc.Block() as block:
        @block.sync
        def _(h):
            for fn in P.plan["sy"]:
                fn(h)

        @block.tensor
        def _(h):
            for fn in P.plan["te"]:
                fn(h)

        @block.vector
        def _(h):
            for fn in P.plan["ve"]:
                fn(h)

        @block.scalar
        def _(h):
            for fn in P.plan["sc"]:
                fn(h)

        @block.gpsimd
        def _(h):
            for fn in P.plan["gp"]:
                fn(h)
    stk.close()



def shard_inputs(cfg, A):
    c = cfg
    x = np.ascontiguousarray(np.asarray(A["L0_attn_hidden_states"], np.float32))
    w_qkv = np.asarray(A["L0_attn_w_qkv"], dtype=np.float32)
    w_o = np.asarray(A["L0_attn_w_o"], dtype=np.float32)
    woT = np.ascontiguousarray(w_o.T)
    rwT = np.ascontiguousarray(np.asarray(A["L0_ffn_router_w"], np.float32).T)
    def tile_gu(w_eHI, w_sh):
        E_, H_, I_ = w_eHI.shape
        HT_, IT_ = H_ // 128, I_ // 128
        a = w_eHI.reshape(E_, HT_, 128, IT_, 128).transpose(0, 3, 2, 1, 4)
        a = a.reshape(E_ * IT_, 128, HT_, 128)
        b = w_sh.reshape(HT_, 128, 2 * IT_, 128).transpose(2, 1, 0, 3)
        return np.ascontiguousarray(np.concatenate([a, b], axis=0), dtype=np.float32)

    gwt = tile_gu(np.asarray(A["L0_ffn_gate_w"], np.float32),
                  np.asarray(A["L0_ffn_s_gate_w"], np.float32))
    uwt = tile_gu(np.asarray(A["L0_ffn_up_w"], np.float32),
                  np.asarray(A["L0_ffn_s_up_w"], np.float32))
    dwa = np.asarray(A["L0_ffn_down_w"], np.float32)
    dwt = np.ascontiguousarray(np.concatenate(
        [dwa.reshape(-1, c.H), np.asarray(A["L0_ffn_s_down_w"], np.float32)], axis=0))
    sgb = np.ascontiguousarray(
        np.asarray(A["L0_ffn_s_gate_b"], np.float32).reshape(2 * c.IT, 128).T)
    sub = np.ascontiguousarray(
        np.asarray(A["L0_ffn_s_up_b"], np.float32).reshape(2 * c.IT, 128).T)
    sdbp = np.zeros((128, c.H), np.float32)
    sdbp[0, :] = np.asarray(A["L0_ffn_s_down_b"], np.float32)

    HPC, HD = c.HPC, c.HD
    maps = []
    for core in range(8):
        b = core // 4
        qpos = core % 4
        heads = list(range(HPC * qpos, HPC * (qpos + 1)))
        rows = []
        for kind in range(3):
            for h in heads:
                r0 = kind * c.NH * HD + h * HD
                rows.append(w_qkv[r0:r0 + HD, :])
        wqkvT = np.ascontiguousarray(np.concatenate(rows, axis=0).T)
        xbat = np.ascontiguousarray(x[b])
        xown = np.ascontiguousarray(x[b, qpos * c.TOWN:(qpos + 1) * c.TOWN, :])
        cpar = np.zeros((1, 48), np.int32)
        q0 = qpos * c.TOWN
        for j in range(c.HT):
            rank = 4 * b + j // HPC
            row = rank * HPC * 128 + (j % HPC) * 128
            cpar[0, j] = row * c.S + q0
            cpar[0, 16 + j] = (128 * j) * c.S + q0
            cpar[0, 32 + j] = (128 * j) * c.S + q0 + c.TH
        maps.append(dict(xb=xbat, xo=xown, wqkvT=wqkvT, woT=woT, rwT=rwT,
                         gwt=gwt, uwt=uwt, dwt=dwt,
                         sgb=sgb, sub=sub, sdbp=sdbp, cpar=cpar))
    return maps


def kernel(**inputs):
    cfg = Cfg()
    nc = build_program(cfg)
    maps = shard_inputs(cfg, inputs)
    res = run_bass_kernel_spmd(nc, maps, core_ids=list(range(8)))
    parts = [res.results[i]["out"] for i in range(8)]
    out = np.empty((cfg.B, cfg.S, cfg.H), np.float32)
    for core in range(8):
        b, qpos = core // 4, core % 4
        out[b, qpos * cfg.TOWN:(qpos + 1) * cfg.TOWN, :] = parts[core]
    return out
